# version 16
# baseline (speedup 1.0000x reference)
import numpy as np
import ml_dtypes
from contextlib import ExitStack

import concourse.bass as bass
import concourse.mybir as mybir
from concourse.alu_op_type import AluOpType as OP
from concourse.bass_utils import run_bass_kernel_spmd

F32 = mybir.dt.float32
BF16 = mybir.dt.bfloat16
I32 = mybir.dt.int32
AX = mybir.AxisListType.X
AF = mybir.ActivationFunctionType

S = 2048
D = 2048
H = 16
RD = 64
KVL = 512
QL = 1536
FI = 4096
NT = 16
INV_SQRT_HD = float(1.0 / np.sqrt(128.0))
EPS = 1e-5

ENGS = ("sync", "scalar", "vector", "tensor", "gpsimd")


class Prog:
    def __init__(self):
        self.ops = {e: [] for e in ENGS}
        self.cnt = {k: 0 for k in ("PE", "AC", "VE", "ST", "SA", "SB", "SC",
                                   "LA", "XF", "VT", "CT", "RW", "WG",
                                   "XS", "VS", "CS", "X1S", "QS", "HS", "TS", "CZ",
                                   "RWA", "RWB", "XQF", "SE", "SF", "SG", "SH",
                                   "SI", "SJ", "SK", "SL",
                                   "VSA", "VSB", "VTA", "VTB", "VTC", "VTD")}
        self.handles = {}

    def emit(self, eng, fn):
        self.ops[eng].append(fn)

    def bump(self, name, k):
        self.cnt[name] = self.cnt.get(name, 0) + k
        return self.cnt[name]

    def wait(self, eng, name, val):
        if not val or val <= 0:
            return
        h = self.handles

        def f(e, name=name, val=val):
            e.wait_ge(h[name], val)
        self.emit(eng, f)

    AUTO = {"vector": "VE", "scalar": "AC"}

    def run(self, eng, fn, inc=None, inck=1):
        h = self.handles
        own = self.AUTO.get(eng)
        if own is not None:
            prev = self.cnt[own]
            if inc is None:
                inc, inck = own, 1
            assert inc == own, f"{eng} op must inc {own}"
            v = self.bump(inc, inck)

            def f(e, fn=fn, prev=prev, own=own, inck=inck):
                if prev > 0:
                    e.wait_ge(h[own], prev)
                fn(e).then_inc(h[own], inck)
            self.emit(eng, f)
            return v
        if inc is None:
            self.emit(eng, fn)
            return None
        v = self.bump(inc, inck)

        def f(e, fn=fn, inc=inc, inck=inck):
            fn(e).then_inc(h[inc], inck)
        self.emit(eng, f)
        return v


def build_nc():
    nc = bass.Bass("TRN2", target_bir_lowering=False)

    x_in = nc.dram_tensor("x_in", [S, D], F32, kind="ExternalInput").ap()
    wkvd = nc.dram_tensor("wkvd", [D, KVL], F32, kind="ExternalInput").ap()
    wqd = nc.dram_tensor("wqd", [D, QL], F32, kind="ExternalInput").ap()
    wkc = nc.dram_tensor("wkc", [KVL, 1024], F32, kind="ExternalInput").ap()
    wv = nc.dram_tensor("wv", [KVL, D], F32, kind="ExternalInput").ap()
    wqc = nc.dram_tensor("wqc", [QL, 1024], F32, kind="ExternalInput").ap()
    wkr = nc.dram_tensor("wkr", [D, RD], F32, kind="ExternalInput").ap()
    wqr = nc.dram_tensor("wqr", [QL, 1024], F32, kind="ExternalInput").ap()
    wo = nc.dram_tensor("wo", [D, D], F32, kind="ExternalInput").ap()
    bo = nc.dram_tensor("bo", [1, D], F32, kind="ExternalInput").ap()
    wup = nc.dram_tensor("wup", [FI, D], F32, kind="ExternalInput").ap()
    wdn = nc.dram_tensor("wdn", [D, FI], F32, kind="ExternalInput").ap()
    cosd = nc.dram_tensor("cosd", [128, S], F32, kind="ExternalInput").ap()
    sind = nc.dram_tensor("sind", [128, S], F32, kind="ExternalInput").ap()
    trimask = nc.dram_tensor("trimask", [128, 128], BF16, kind="ExternalInput").ap()
    onescol_bf = nc.dram_tensor("onescol_bf", [128, 1], BF16, kind="ExternalInput").ap()
    onesrow_f = nc.dram_tensor("onesrow_f", [1, 128], F32, kind="ExternalInput").ap()
    onescol_f = nc.dram_tensor("onescol_f", [128, 1], F32, kind="ExternalInput").ap()
    invN_row = nc.dram_tensor("invN_row", [1, 128], F32, kind="ExternalInput").ap()
    halves = nc.dram_tensor("halves", [128, 512], F32, kind="ExternalInput").ap()

    xln_s = nc.dram_tensor("xln_s", [S, D], BF16, kind="Internal").ap()
    v_s = nc.dram_tensor("v_s", [S, D], BF16, kind="Internal").ap()
    ctx_s = nc.dram_tensor("ctx_s", [D, S], BF16, kind="Internal").ap()
    x1_s = nc.dram_tensor("x1_s", [S, D], F32, kind="Internal").ap()
    xq_s = nc.dram_tensor("xq_s", [S, D], BF16, kind="Internal").ap()
    hq_s = nc.dram_tensor("hq_s", [S, FI], BF16, kind="Internal").ap()
    tup_s = nc.dram_tensor("tup_s", [FI, D], BF16, kind="Internal").ap()
    tdn_s = nc.dram_tensor("tdn_s", [D, FI], BF16, kind="Internal").ap()
    out_t = nc.dram_tensor("out", [S, D], F32, kind="ExternalOutput").ap()

    p = Prog()

    def mm_seq(mms, inc="PE"):
        v = None
        for i, (o, l, r, st, sp) in enumerate(mms):
            last = (i == len(mms) - 1)
            if last:
                v = p.run("tensor", lambda e, o=o, l=l, r=r, st=st, sp=sp:
                          e.matmul(o, l, r, start=st, stop=sp), inc=inc)
            else:
                p.run("tensor", lambda e, o=o, l=l, r=r, st=st, sp=sp:
                      e.matmul(o, l, r, start=st, stop=sp))
        return v

    SLOTS = ("SA", "SB", "SC")

    def slot_load(slot, out_ap, in_ap, queue="sync"):
        sem = SLOTS[slot]
        p.wait(queue, sem, p.cnt[sem])
        return p.run(queue, lambda e, o=out_ap, i=in_ap: e.dma_start(out=o, in_=i),
                     inc=sem, inck=16)

    def nr_recip(rt, mt, tmp, steps=2):
        for _ in range(steps):
            p.run("vector", lambda e: e.tensor_mul(tmp, mt, rt))
            p.run("vector", lambda e: e.tensor_scalar(
                tmp, tmp, -1.0, 2.0, OP.mult, OP.add))
            p.run("vector", lambda e: e.tensor_mul(rt, rt, tmp))

    def barrier(exclude=(), overrides=None):
        snap = dict(p.cnt)
        if overrides:
            snap.update(overrides)
        for eng in ENGS:
            for sn in snap:
                if sn in exclude:
                    continue
                p.wait(eng, sn, snap[sn])

    with ExitStack() as top:
        E = top.enter_context
        s_toff = E(nc.sbuf_tensor("s_toff", [128, 8], F32))
        s_ones_bf = E(nc.sbuf_tensor("s_ones_bf", [128, 1], BF16))
        s_onesr_f = E(nc.sbuf_tensor("s_onesr_f", [1, 128], F32))
        s_onesc_f = E(nc.sbuf_tensor("s_onesc_f", [128, 1], F32))
        s_invN = E(nc.sbuf_tensor("s_invN", [1, 128], F32))
        s_trim = E(nc.sbuf_tensor("s_trim", [128, 128], BF16))
        s_eps = E(nc.sbuf_tensor("s_eps", [128, 1], F32))
        s_sdc = E(nc.sbuf_tensor("s_sdc", [128, 1], F32))
        s_isq = E(nc.sbuf_tensor("s_isq", [128, 1], F32))
        s_isq2 = E(nc.sbuf_tensor("s_isq2", [128, 1], F32))
        s_r127 = E(nc.sbuf_tensor("s_r127", [128, 1], F32))
        s_r127b = E(nc.sbuf_tensor("s_r127b", [128, 1], F32))
        s_nrw = E(nc.sbuf_tensor("s_nrw", [128, 1], F32))
        pan = {k: E(nc.sbuf_tensor("pan_" + k, [128, NT], F32))
               for k in ("sum", "sum2", "sq", "mean", "var", "rstd", "rstd2", "nmr")}
        pan2 = {k: E(nc.sbuf_tensor("pan2_" + k, [128, NT], F32))
                for k in ("sum", "sum2", "sq", "mean", "var", "rstd", "rstd2", "nmr")}
        s_thr = {k: E(nc.sbuf_tensor(f"s_thr_{k}", [128, 1], F32)) for k in ("up", "dn")}
        s_nthr = {k: E(nc.sbuf_tensor(f"s_nthr_{k}", [128, 1], F32)) for k in ("up", "dn")}
        s_sw = {k: E(nc.sbuf_tensor(f"s_sw_{k}", [128, 1], F32)) for k in ("up", "dn")}
        abspan = {"up": E(nc.sbuf_tensor("abspan_up", [128, 64], F32)),
                  "dn": E(nc.sbuf_tensor("abspan_dn", [128, 64], F32))}
        colsum = E(nc.sbuf_tensor("colsum", [128, 2], F32))
        tot_row = E(nc.sbuf_tensor("tot_row", [1, 2], F32))
        maxpan1 = E(nc.sbuf_tensor("maxpan1", [128, 8], F32))
        scalepan1 = E(nc.sbuf_tensor("scalepan1", [128, 8], F32))
        evacpan1 = E(nc.sbuf_tensor("evacpan1", [128, 8], F32))
        hmaxpan = E(nc.sbuf_tensor("hmaxpan", [128, 64], F32))
        max2pan = E(nc.sbuf_tensor("max2pan", [128, 8], F32))
        scalepan2 = E(nc.sbuf_tensor("scalepan2", [128, 8], F32))
        evacpan2 = E(nc.sbuf_tensor("evacpan2", [128, 8], F32))
        panwk = E(nc.sbuf_tensor("panwk", [128, 8], F32))

        p.run("sync", lambda e: e.dma_start(out=s_ones_bf[:], in_=onescol_bf), inc="LA", inck=16)
        p.run("sync", lambda e: e.dma_start(out=s_onesr_f[:], in_=onesrow_f), inc="LA", inck=16)
        p.run("sync", lambda e: e.dma_start(out=s_onesc_f[:], in_=onescol_f), inc="LA", inck=16)
        p.run("sync", lambda e: e.dma_start(out=s_invN[:], in_=invN_row), inc="LA", inck=16)
        p.run("sync", lambda e: e.dma_start(out=s_trim[:], in_=trimask), inc="LA", inck=16)
        st0 = ExitStack()
        s_offw = st0.enter_context(nc.sbuf_tensor("s_offw", [128, 512], F32))
        s_offs_i = st0.enter_context(nc.sbuf_tensor("s_offs_i", [128, 512], I32))
        p.run("sync", lambda e: e.dma_start(out=s_offw[:], in_=halves), inc="LA", inck=16)
        p.wait("vector", "LA", p.cnt["LA"])
        p.run("vector", lambda e: e.memset(s_eps[:], EPS))
        p.run("vector", lambda e: e.memset(s_isq2[:], 128.0))
        p.run("vector", lambda e: e.memset(s_r127b[:], 127.0))
        v_c0 = p.run("vector", lambda e: e.memset(s_nrw[:], 0.0), inc="VE")
        p.wait("scalar", "VE", v_c0)
        v_c1 = p.run("scalar", lambda e: e.activation(
            s_isq2[:], s_isq2[:], AF.Sqrt, bias=0.0, scale=1.0), inc="AC")
        p.wait("vector", "AC", v_c1)
        p.run("vector", lambda e: e.reciprocal(s_isq[:], s_isq2[:]))
        nr_recip(s_isq[:], s_isq2[:], s_nrw[:])
        p.run("vector", lambda e: e.reciprocal(s_r127[:], s_r127b[:]))
        nr_recip(s_r127[:], s_r127b[:], s_nrw[:])
        p.run("vector", lambda e: e.tensor_copy(s_offs_i[:], s_offw[:]))
        p.run("vector", lambda e: e.tensor_copy(s_offw[:], s_offs_i[:]))
        p.run("vector", lambda e: e.tensor_scalar(
            s_offw[:], s_offw[:], -0.5, 257.0, OP.mult, OP.add))
        p.run("vector", lambda e: e.tensor_copy(s_toff[:], s_offw[:, 0:8]))
        v_toff = p.run("vector", lambda e: e.tensor_scalar(
            s_sdc[:], s_toff[:, 0:1], -1.0, 256.5, OP.mult, OP.add), inc="VE")
        st0.close()
        barrier()

        tern = {}
        NSL = 6
        TSLOT = ("SG", "SH", "SI", "SJ", "SK", "SL")
        st_tern = ExitStack()
        Et = st_tern.enter_context
        tern["buf"] = [Et(nc.sbuf_tensor(f"twb{i}", [128, 1024], F32))
                       for i in range(NSL)]
        tern["ta"] = Et(nc.sbuf_tensor("tta", [128, 1024], BF16))
        tern["tb"] = Et(nc.sbuf_tensor("ttb", [128, 1024], BF16))
        tern["tt"] = [Et(nc.sbuf_tensor(f"ttt{i}", [128, 1024], BF16)) for i in range(2)]
        tern["gv"] = [0] * NSL
        tern["tguard"] = [0, 0]

        def tern_src(kind, c):
            if kind == "up":
                row, hf = c // 2, c % 2
                return (wup[row * 128:(row + 1) * 128, hf * 1024:(hf + 1) * 1024],
                        tup_s[row * 128:(row + 1) * 128, hf * 1024:(hf + 1) * 1024])
            row, qq = c // 4, c % 4
            return (wdn[row * 128:(row + 1) * 128, qq * 1024:(qq + 1) * 1024],
                    tdn_s[row * 128:(row + 1) * 128, qq * 1024:(qq + 1) * 1024])

        tern["ld"] = [0] * NSL

        def tern_issue(units, i):
            kind, c = units[i]
            b = i % NSL
            src, _ = tern_src(kind, c)
            p.wait("sync", "VE", tern["gv"][b])
            p.wait("sync", TSLOT[b], p.cnt[TSLOT[b]])
            tern["ld"][b] = p.run(
                "sync", lambda e, b=b, src=src: e.dma_start(
                    out=tern["buf"][b][:], in_=src), inc=TSLOT[b], inck=16)

        def tern_stat(units, i):
            kind, c = units[i]
            b = i % NSL
            p.wait("vector", TSLOT[b], tern["ld"][b])
            tern["gv"][b] = p.run(
                "vector", lambda e, b=b, kind=kind, c=c: e.reduce_sum(
                    abspan[kind][:, c:c + 1], tern["buf"][b][:], AX,
                    apply_absolute_value=True), inc="VE")

        def tern_scalars(kind, psc, pbc):
            col = 0 if kind == "up" else 1
            v_cs = p.run("vector", lambda e, kind=kind, col=col: e.reduce_sum(
                colsum[:, col:col + 1], abspan[kind][:], AX), inc="VE")
            p.wait("tensor", "VE", v_cs)
            p.wait("tensor", "AC", p.cnt["AC"])
            v_m1 = p.run("tensor", lambda e, col=col, psc=psc: e.matmul(
                psc[0:1, col:col + 1], colsum[:, col:col + 1], s_onesc_f[:],
                start=True, stop=True), inc="PE")
            p.wait("vector", "PE", v_m1)
            v_ev = p.run("vector", lambda e, col=col, psc=psc: e.tensor_copy(
                tot_row[:, col:col + 1], psc[0:1, col:col + 1]), inc="VE")
            p.wait("tensor", "VE", v_ev)
            v_m2 = p.run("tensor", lambda e, col=col, pbc=pbc: e.matmul(
                pbc[:, col:col + 1], s_invN[:], tot_row[:, col:col + 1],
                start=True, stop=True), inc="PE")
            p.wait("vector", "PE", v_m2)
            p.run("vector", lambda e, kind=kind, col=col, pbc=pbc: e.tensor_copy(
                s_sw[kind][:], pbc[:, col:col + 1]))
            p.run("vector", lambda e, kind=kind: e.tensor_scalar(
                s_sw[kind][:], s_sw[kind][:], EPS, None, OP.max))
            p.run("vector", lambda e, kind=kind: e.tensor_scalar(
                s_thr[kind][:], s_sw[kind][:], 0.5, None, OP.mult))
            v_t = p.run("vector", lambda e, kind=kind: e.tensor_scalar(
                s_nthr[kind][:], s_sw[kind][:], -0.5, None, OP.mult), inc="VE")
            tern["rdy_" + kind] = v_t

        def tern_quant(units, i):
            kind, c = units[i]
            b = i % NSL
            tb2 = i % 2
            _, dst = tern_src(kind, c)
            p.wait("vector", TSLOT[b], tern["ld"][b])
            p.wait("vector", "VE", tern["rdy_" + kind])
            p.wait("vector", "TS", p.cnt["TS"] if tern["tguard"][tb2] else 0)
            p.run("vector", lambda e, b=b, kind=kind: e.tensor_scalar(
                tern["ta"][:], tern["buf"][b][:], s_thr[kind][:], None, OP.is_gt))
            p.run("vector", lambda e, b=b, kind=kind: e.tensor_scalar(
                tern["tb"][:], tern["buf"][b][:], s_nthr[kind][:], None, OP.is_lt))
            vt_ = p.run("vector", lambda e, tb2=tb2: e.tensor_tensor(
                tern["tt"][tb2][:], tern["ta"][:], tern["tb"][:], OP.subtract),
                inc="VE")
            tern["gv"][b] = vt_
            p.wait("gpsimd", "VE", vt_)
            vg = p.run("gpsimd", lambda e, tb2=tb2, dst=dst: e.dma_start(
                out=dst, in_=tern["tt"][tb2][:]), inc="TS", inck=16)
            tern["tguard"][tb2] = vg

        stat_units = [("up", c) for c in range(64)] + [("dn", c) for c in range(64)]
        quant_units = list(stat_units)
        cur = {"stat": [0, 0], "quant": [0, 0]}

        def _do(n, key, units, consume):
            for _ in range(n):
                if cur[key][1] >= len(units):
                    break
                while cur[key][0] < min(cur[key][1] + NSL, len(units)):
                    tern_issue(units, cur[key][0])
                    cur[key][0] += 1
                consume(units, cur[key][1])
                cur[key][1] += 1

        def do_stats(n):
            _do(n, "stat", stat_units, tern_stat)

        def do_quants(n):
            _do(n, "quant", quant_units, tern_quant)


        st1 = ExitStack()
        E1 = st1.enter_context
        xt = [E1(nc.sbuf_tensor(f"xt{i}", [128, D], F32)) for i in range(3)]
        dump = E1(nc.sbuf_tensor("dump", [128, D], F32))
        xlnt = [E1(nc.sbuf_tensor(f"xlnt{i}", [128, D], BF16)) for i in range(2)]

        gve = [0, 0, 0]
        gac = [0, 0, 0]
        for t in range(NT):
            b = t % 3
            p.wait("sync", "VE", gve[b])
            p.wait("sync", "AC", gac[b])
            vls = slot_load(b, xt[b][:], x_in[t * 128:(t + 1) * 128, :])
            p.wait("scalar", SLOTS[b], vls)
            va = p.run("scalar", lambda e, b=b, t=t: e.activation(
                dump[:], xt[b][:], AF.Square, bias=0.0, scale=1.0,
                accum_out=pan["sq"][:, t:t + 1]), inc="AC")
            p.wait("vector", SLOTS[b], vls)
            vr = p.run("vector", lambda e, b=b, t=t: e.reduce_sum(
                pan["sum"][:, t:t + 1], xt[b][:], AX), inc="VE")
            gve[b], gac[b] = vr, va
        v_stats_ac, v_stats_ve = p.cnt["AC"], p.cnt["VE"]

        def panel_math(P, v_red_ve, v_sq_ac):
            p.wait("vector", "AC", v_sq_ac)
            p.run("vector", lambda e: e.tensor_scalar(
                P["mean"][:], P["sum"][:], 1.0 / D, None, OP.mult))
            p.run("vector", lambda e: e.tensor_scalar(
                P["var"][:], P["sq"][:], 1.0 / D, None, OP.mult))
            p.run("vector", lambda e: e.tensor_tensor(
                P["rstd"][:], P["mean"][:], P["mean"][:], OP.mult))
            v_pv = p.run("vector", lambda e: e.tensor_tensor(
                P["var"][:], P["var"][:], P["rstd"][:], OP.subtract), inc="VE")
            p.wait("scalar", "VE", v_pv)
            v_sq = p.run("scalar", lambda e: e.activation(
                P["rstd"][:], P["var"][:], AF.Sqrt, bias=s_eps[:], scale=1.0), inc="AC")
            p.wait("vector", "AC", v_sq)
            p.run("vector", lambda e: e.reciprocal(P["rstd2"][:], P["rstd"][:]))
            nr_recip(P["rstd2"][:], P["rstd"][:], P["sum2"][:])
            p.run("vector", lambda e: e.tensor_tensor(
                P["nmr"][:], P["mean"][:], P["rstd2"][:], OP.mult))
            return p.run("vector", lambda e: e.tensor_scalar(
                P["nmr"][:], P["nmr"][:], -1.0, None, OP.mult), inc="VE")

        v_nmr = panel_math(pan, v_stats_ve, v_stats_ac)

        p.wait("scalar", "VE", v_nmr)
        xg = [0, 0]
        g2 = [v_stats_ac, v_stats_ac, v_stats_ac]
        p.wait("sync", "VE", v_stats_ve)
        for t in range(NT):
            b, ob = t % 3, t % 2
            p.wait("sync", "AC", g2[b])
            vls = slot_load(b, xt[b][:], x_in[t * 128:(t + 1) * 128, :])
            p.wait("scalar", SLOTS[b], vls)
            p.wait("scalar", "XS", p.cnt["XS"])
            va = p.run("scalar", lambda e, b=b, ob=ob, t=t: e.activation(
                xlnt[ob][:], xt[b][:], AF.Identity,
                bias=pan["nmr"][:, t:t + 1], scale=pan["rstd2"][:, t:t + 1]), inc="AC")
            g2[b] = va
            p.wait("gpsimd", "AC", va)
            vg = p.run("gpsimd", lambda e, ob=ob, t=t: e.dma_start(
                out=xln_s[t * 128:(t + 1) * 128, :], in_=xlnt[ob][:]), inc="XS", inck=16)
        v_xln_gd = p.cnt["XS"]
        st1.close()
        barrier()

        st_ckv = ExitStack()
        ckv = [st_ckv.enter_context(nc.sbuf_tensor(f"ckv{c}", [128, S], BF16))
               for c in range(4)]
        st_krp = ExitStack()
        krp = st_krp.enter_context(nc.sbuf_tensor("krp", [64, S], BF16))
        st_cq = ExitStack()
        cq = [st_cq.enter_context(nc.sbuf_tensor(f"cq{c}", [128, S], BF16))
              for c in range(12)]
        st_rope = ExitStack()
        Er = st_rope.enter_context
        s_cos = Er(nc.sbuf_tensor("s_cos", [128, S], BF16))
        s_sin = Er(nc.sbuf_tensor("s_sin", [128, S], BF16))
        ropeA = Er(nc.sbuf_tensor("ropeA", [128, 512], F32))
        ropeB = Er(nc.sbuf_tensor("ropeB", [128, 512], F32))
        st_p2ps = ExitStack()
        Ep = st_p2ps.enter_context
        pj = [Ep(nc.psum_tensor(f"pj{i}", [128, 512], F32)) for i in range(8)]
        pr = [pj[4], pj[5], pj[6], pj[7]]
        t_psc = pj[2]
        t_pbc = pj[3]

        st_xf = ExitStack()
        xf = [st_xf.enter_context(nc.sbuf_tensor(f"xf{c}", [128, S], BF16, side="right"))
              for c in range(16)]
        st_wqd = ExitStack()
        wqd_t = [st_wqd.enter_context(nc.sbuf_tensor(f"wqdt{k}", [128, QL // 2], BF16))
                 for k in range(16)]

        p.wait("sync", "XS", v_xln_gd)
        xf_ld = []
        for c in range(16):
            vls = p.run("sync", lambda e, c=c: e.dma_start(
                out=xf[c][:], in_=xln_s[:, c * 128:(c + 1) * 128],
                transpose=True), inc="XF", inck=16)
            xf_ld.append(vls)
        v_xf_all = xf_ld[-1]

        p.wait("tensor", "XF", v_xf_all)
        ev_g = [0, 0]
        for half in range(2):
            colbase = half * (QL // 2)
            p.wait("gpsimd", "PE", p.cnt["PE"])
            wld = []
            for k in range(16):
                vg = p.run("gpsimd", lambda e, k=k, colbase=colbase: e.dma_start(
                    out=wqd_t[k][:],
                    in_=wqd[k * 128:(k + 1) * 128, colbase:colbase + QL // 2]),
                    inc="WG", inck=16)
                wld.append(vg)
            p.wait("tensor", "WG", wld[-1])
            for mh in range(6):
                m = half * 6 + mh
                gg = m % 2
                p.wait("tensor", "AC", ev_g[gg])
                mms = []
                for k in range(16):
                    for n4 in range(4):
                        mms.append((pj[4 * gg + n4][:],
                                    wqd_t[k][:, mh * 128:(mh + 1) * 128],
                                    xf[k][:, n4 * 512:(n4 + 1) * 512],
                                    k == 0, k == 15))
                v_mm = mm_seq(mms)
                p.wait("scalar", "PE", v_mm)
                for n4 in range(4):
                    ev_g[gg] = p.run("scalar", lambda e, m=m, n4=n4, gg=gg: e.copy(
                        cq[m][:, n4 * 512:(n4 + 1) * 512], pj[4 * gg + n4][:]),
                        inc="AC")
                do_stats(22)
        do_stats(128)
        st_wqd.close()

        st_wkv = ExitStack()
        wkv_t = [st_wkv.enter_context(nc.sbuf_tensor(f"wkvt{k}", [128, KVL], BF16))
                 for k in range(16)]
        p.wait("gpsimd", "PE", p.cnt["PE"])
        wld2 = []
        for k in range(16):
            vg = p.run("gpsimd", lambda e, k=k: e.dma_start(
                out=wkv_t[k][:], in_=wkvd[k * 128:(k + 1) * 128, :]), inc="WG", inck=16)
            wld2.append(vg)
        p.wait("tensor", "WG", wld2[-1])
        for m in range(4):
            gg = m % 2
            p.wait("tensor", "AC", ev_g[gg])
            mms = []
            for k in range(16):
                for n4 in range(4):
                    mms.append((pj[4 * gg + n4][:],
                                wkv_t[k][:, m * 128:(m + 1) * 128],
                                xf[k][:, n4 * 512:(n4 + 1) * 512],
                                k == 0, k == 15))
            v_mm = mm_seq(mms)
            p.wait("scalar", "PE", v_mm)
            for n4 in range(4):
                ev_g[gg] = p.run("scalar", lambda e, m=m, n4=n4, gg=gg: e.copy(
                    ckv[m][:, n4 * 512:(n4 + 1) * 512], pj[4 * gg + n4][:]),
                    inc="AC")
        st_wkv.close()

        st_kr = ExitStack()
        Ek = st_kr.enter_context
        wkr_t = [Ek(nc.sbuf_tensor(f"wkrt{k}", [128, RD], BF16)) for k in range(16)]
        wkrr_t = [Ek(nc.sbuf_tensor(f"wkrrt{k}", [128, RD], BF16)) for k in range(16)]

        p.wait("gpsimd", "PE", p.cnt["PE"])
        v_cos = p.run("gpsimd", lambda e: e.dma_start(out=s_cos[:], in_=cosd), inc="CZ", inck=16)
        v_sin = p.run("gpsimd", lambda e: e.dma_start(out=s_sin[:], in_=sind), inc="CZ", inck=16)
        wld3 = []
        for k in range(16):
            vg = p.run("gpsimd", lambda e, k=k: e.dma_start(
                out=wkr_t[k][:], in_=wkr[k * 128:(k + 1) * 128, :]), inc="WG", inck=16)
            wld3.append(vg)
        p.wait("vector", "WG", wld3[-1])
        for k in range(16):
            p.run("vector", lambda e, k=k: e.tensor_scalar(
                wkrr_t[k][:, 0:32], wkr_t[k][:, 32:64], -1.0, None, OP.mult))
            if k == 15:
                v_wkrr = p.run("vector", lambda e, k=k: e.tensor_copy(
                    wkrr_t[k][:, 32:64], wkr_t[k][:, 0:32]), inc="VE")
            else:
                p.run("vector", lambda e, k=k: e.tensor_copy(
                    wkrr_t[k][:, 32:64], wkr_t[k][:, 0:32]))
        p.wait("tensor", "VE", v_wkrr)
        p.wait("tensor", "CZ", max(v_cos, v_sin))
        g_prp = [0, 0]
        p.wait("tensor", "AC", p.cnt["AC"])
        for n4 in range(4):
            pp = n4 % 2
            p.wait("tensor", "VE", g_prp[pp])
            mms = []
            for k in range(16):
                mms.append((pr[2 * pp][0:64, :], wkr_t[k][:],
                            xf[k][:, n4 * 512:(n4 + 1) * 512], k == 0, k == 15))
            for k in range(16):
                mms.append((pr[2 * pp + 1][0:64, :], wkrr_t[k][:],
                            xf[k][:, n4 * 512:(n4 + 1) * 512], k == 0, k == 15))
            v_krmm = mm_seq(mms)
            p.wait("vector", "PE", v_krmm)
            p.run("vector", lambda e, n4=n4, pp=pp: e.tensor_tensor(
                ropeA[0:64, :], pr[2 * pp][0:64, :],
                s_cos[0:64, n4 * 512:(n4 + 1) * 512], OP.mult))
            p.run("vector", lambda e, n4=n4, pp=pp: e.tensor_tensor(
                ropeB[0:64, :], pr[2 * pp + 1][0:64, :],
                s_sin[0:64, n4 * 512:(n4 + 1) * 512], OP.mult))
            v_krp = p.run("vector", lambda e, n4=n4: e.tensor_tensor(
                krp[:, n4 * 512:(n4 + 1) * 512], ropeA[0:64, :],
                ropeB[0:64, :], OP.add), inc="VE")
            g_prp[pp] = v_krp
        v_krp_done = v_krp
        st_kr.close()
        v_pe_xf_done = p.cnt["PE"]
        st_xf.close()
        do_stats(999)
        tern_scalars("up", t_psc, t_pbc)
        tern_scalars("dn", t_psc, t_pbc)

        st_qk = ExitStack()
        qh = [st_qk.enter_context(nc.sbuf_tensor(f"qh{h}", [128, S], BF16, side="right"))
              for h in range(H)]
        st_wqc = ExitStack()
        wqc_t = [st_wqc.enter_context(nc.sbuf_tensor(f"wqct{k}", [128, 1024], BF16))
                 for k in range(12)]
        p.wait("gpsimd", "PE", v_pe_xf_done)
        p.wait("gpsimd", "VE", v_krp_done)
        wld4 = []
        for k in range(12):
            vg = p.run("gpsimd", lambda e, k=k: e.dma_start(
                out=wqc_t[k][:], in_=wqc[k * 128:(k + 1) * 128, :]), inc="WG", inck=16)
            wld4.append(vg)
        p.wait("tensor", "WG", wld4[-1])
        p.wait("tensor", "AC", p.cnt["AC"])
        p.wait("tensor", "VE", tern["rdy_dn"])
        for m in range(8):
            gg = m % 2
            p.wait("tensor", "AC", ev_g[gg])
            mms = []
            for k in range(12):
                for n4 in range(4):
                    mms.append((pj[4 * gg + n4][:],
                                wqc_t[k][:, m * 128:(m + 1) * 128],
                                cq[k][:, n4 * 512:(n4 + 1) * 512],
                                k == 0, k == 11))
            v_mm = mm_seq(mms)
            p.wait("scalar", "PE", v_mm)
            for n4 in range(4):
                p.run("scalar", lambda e, m=m, n4=n4, gg=gg: e.copy(
                    qh[2 * m][0:64, n4 * 512:(n4 + 1) * 512],
                    pj[4 * gg + n4][0:64, :]))
                ev_g[gg] = p.run("scalar", lambda e, m=m, n4=n4, gg=gg: e.copy(
                    qh[2 * m + 1][0:64, n4 * 512:(n4 + 1) * 512],
                    pj[4 * gg + n4][64:128, :]), inc="AC")
            do_quants(4)
        v_pe_qc_done = p.cnt["PE"]
        st_wqc.close()

        st_wqr = ExitStack()
        Eq = st_wqr.enter_context
        wqr_t = [Eq(nc.sbuf_tensor(f"wqrt{k}", [128, 512], BF16)) for k in range(12)]
        wqrr_t = [Eq(nc.sbuf_tensor(f"wqrrt{k}", [128, 512], BF16)) for k in range(12)]
        p.wait("tensor", "AC", p.cnt["AC"])
        g_prp = [0, 0]
        step = 0
        v_half_pe = v_pe_qc_done
        for hh in range(2):
            p.wait("gpsimd", "PE", v_half_pe)
            if hh:
                p.wait("gpsimd", "VE", v_wqrr)
            wld5 = []
            for k in range(12):
                vg = p.run("gpsimd", lambda e, k=k, hh=hh: e.dma_start(
                    out=wqr_t[k][:],
                    in_=wqr[k * 128:(k + 1) * 128, hh * 512:(hh + 1) * 512]),
                    inc="WG", inck=16)
                wld5.append(vg)
            p.wait("vector", "WG", wld5[-1])
            for k in range(12):
                wt = wqr_t[k][:].rearrange("p (h t c) -> p h t c", h=8, t=2, c=32)
                rt = wqrr_t[k][:].rearrange("p (h t c) -> p h t c", h=8, t=2, c=32)
                p.run("vector", lambda e, wt=wt, rt=rt: e.tensor_scalar(
                    rt[:, :, 0, :], wt[:, :, 1, :], -1.0, None, OP.mult))
                if k == 11:
                    v_wqrr = p.run("vector", lambda e, wt=wt, rt=rt: e.tensor_copy(
                        rt[:, :, 1, :], wt[:, :, 0, :]), inc="VE")
                else:
                    p.run("vector", lambda e, wt=wt, rt=rt: e.tensor_copy(
                        rt[:, :, 1, :], wt[:, :, 0, :]))
            p.wait("tensor", "VE", v_wqrr)
            for m in range(4 * hh, 4 * hh + 4):
                mh = m - 4 * hh
                for n4 in range(4):
                    pp = step % 2
                    step += 1
                    p.wait("tensor", "VE", g_prp[pp])
                    mms = []
                    for k in range(12):
                        mms.append((pr[2 * pp][:], wqr_t[k][:, mh * 128:(mh + 1) * 128],
                                    cq[k][:, n4 * 512:(n4 + 1) * 512], k == 0, k == 11))
                    for k in range(12):
                        mms.append((pr[2 * pp + 1][:],
                                    wqrr_t[k][:, mh * 128:(mh + 1) * 128],
                                    cq[k][:, n4 * 512:(n4 + 1) * 512], k == 0, k == 11))
                    v_qr = mm_seq(mms)
                    p.wait("vector", "PE", v_qr)
                    p.run("vector", lambda e, n4=n4, pp=pp: e.tensor_tensor(
                        ropeA[:], pr[2 * pp][:],
                        s_cos[:, n4 * 512:(n4 + 1) * 512], OP.mult))
                    p.run("vector", lambda e, n4=n4, pp=pp: e.tensor_tensor(
                        ropeB[:], pr[2 * pp + 1][:],
                        s_sin[:, n4 * 512:(n4 + 1) * 512], OP.mult))
                    p.run("vector", lambda e, m=m, n4=n4: e.tensor_tensor(
                        qh[2 * m][64:128, n4 * 512:(n4 + 1) * 512],
                        ropeA[0:64, :], ropeB[0:64, :], OP.add))
                    v_qrope = p.run("vector", lambda e, m=m, n4=n4: e.tensor_tensor(
                        qh[2 * m + 1][64:128, n4 * 512:(n4 + 1) * 512],
                        ropeA[64:128, :], ropeB[64:128, :], OP.add), inc="VE")
                    g_prp[pp] = v_qrope
                    do_quants(1)
            v_half_pe = v_qr
        st_wqr.close()
        st_rope.close()
        v_pe_cq_done = p.cnt["PE"]
        st_cq.close()

        st_kh = ExitStack()
        kh = [st_kh.enter_context(nc.sbuf_tensor(f"kh{h}", [128, S], BF16, side="right"))
              for h in range(H)]
        st_wkc = ExitStack()
        wkc_t = [st_wkc.enter_context(nc.sbuf_tensor(f"wkct{k}", [128, 1024], BF16))
                 for k in range(4)]
        p.wait("gpsimd", "PE", v_pe_cq_done)
        p.wait("gpsimd", "VE", v_qrope)
        wld6 = []
        for k in range(4):
            vg = p.run("gpsimd", lambda e, k=k: e.dma_start(
                out=wkc_t[k][:], in_=wkc[k * 128:(k + 1) * 128, :]), inc="WG", inck=16)
            wld6.append(vg)
        p.wait("tensor", "WG", wld6[-1])
        p.wait("tensor", "VE", v_qrope)
        for m in range(8):
            gg = m % 2
            p.wait("tensor", "AC", ev_g[gg])
            mms = []
            for k in range(4):
                for n4 in range(4):
                    mms.append((pj[4 * gg + n4][:],
                                wkc_t[k][:, m * 128:(m + 1) * 128],
                                ckv[k][:, n4 * 512:(n4 + 1) * 512], k == 0, k == 3))
            v_mm = mm_seq(mms)
            p.wait("scalar", "PE", v_mm)
            for n4 in range(4):
                p.run("scalar", lambda e, m=m, n4=n4, gg=gg: e.copy(
                    kh[2 * m][0:64, n4 * 512:(n4 + 1) * 512],
                    pj[4 * gg + n4][0:64, :]))
                ev_g[gg] = p.run("scalar", lambda e, m=m, n4=n4, gg=gg: e.copy(
                    kh[2 * m + 1][0:64, n4 * 512:(n4 + 1) * 512],
                    pj[4 * gg + n4][64:128, :]), inc="AC")
            do_quants(2)
        p.wait("vector", "VE", v_krp_done)
        for h in range(H):
            if h == H - 1:
                v_khcp = p.run("vector", lambda e, h=h: e.tensor_copy(
                    kh[h][64:128, :], krp[:]), inc="VE")
            else:
                p.run("vector", lambda e, h=h: e.tensor_copy(
                    kh[h][64:128, :], krp[:]))
        st_wkc.close()

        st_wv = ExitStack()
        Ev = st_wv.enter_context
        wv_t = [Ev(nc.sbuf_tensor(f"wvt{k}", [128, D], BF16)) for k in range(4)]
        vout = [Ev(nc.sbuf_tensor(f"vout{i}", [128, D], BF16)) for i in range(2)]
        p.wait("gpsimd", "PE", p.cnt["PE"])
        wld7 = []
        for k in range(4):
            vg = p.run("gpsimd", lambda e, k=k: e.dma_start(
                out=wv_t[k][:], in_=wv[k * 128:(k + 1) * 128, :]), inc="WG", inck=16)
            wld7.append(vg)
        p.wait("tensor", "WG", wld7[-1])
        VSS = ("VSA", "VSB")
        VTS = ("VTA", "VTB")
        for t in range(NT):
            ob = t % 2
            gg = t % 2
            p.wait("tensor", "AC", ev_g[gg])
            mms = []
            for k in range(4):
                for n4 in range(4):
                    mms.append((pj[4 * gg + n4][:],
                                ckv[k][:, t * 128:(t + 1) * 128],
                                wv_t[k][:, n4 * 512:(n4 + 1) * 512], k == 0, k == 3))
            v_mm = mm_seq(mms)
            p.wait("scalar", "PE", v_mm)
            p.wait("scalar", VSS[ob], 16 * (t // 2))
            for n4 in range(4):
                ev_g[gg] = p.run("scalar", lambda e, ob=ob, n4=n4, gg=gg: e.copy(
                    vout[ob][:, n4 * 512:(n4 + 1) * 512], pj[4 * gg + n4][:]),
                    inc="AC")
            p.wait("gpsimd", "AC", ev_g[gg])
            p.run("gpsimd", lambda e, ob=ob, t=t: e.dma_start(
                out=v_s[t * 128:(t + 1) * 128, :], in_=vout[ob][:]),
                inc=VSS[ob], inck=16)
            do_quants(1)
        st_wv.close()
        st_krp.close()
        st_ckv.close()
        st_p2ps.close()
        barrier(exclude=("VSA", "VSB", "SG", "SH", "SI", "SJ", "SK", "SL", "TS"),
                overrides={"VE": v_khcp})
        v_p2_pe = p.cnt["PE"]
        v_p2_ve = v_khcp
        v_p2_ac = p.cnt["AC"]

        st3 = ExitStack()
        E3 = st3.enter_context
        vth = [E3(nc.sbuf_tensor(f"vth{i}", [128, S], BF16)) for i in range(4)]
        ae = [E3(nc.sbuf_tensor(f"ae{i}", [128, 512], BF16)) for i in range(3)]
        srow = E3(nc.sbuf_tensor("srow", [1, 512], F32))
        srr = [E3(nc.sbuf_tensor("srr0", [1, 512], F32))]
        rbc = [E3(nc.sbuf_tensor("rbc0", [128, 512], F32))]
        cxo = [E3(nc.sbuf_tensor(f"cxo{i}", [128, 512], BF16)) for i in range(2)]
        psc = [E3(nc.psum_tensor(f"psc{i}", [128, 512], F32)) for i in range(3)]
        pctx = [E3(nc.psum_tensor(f"pctx{i}", [128, 512], F32)) for i in range(2)]
        psr = [E3(nc.psum_tensor(f"psr{i}", [1, 512], F32)) for i in range(2)]
        pbcp = E3(nc.psum_tensor("pbcp", [128, 512], F32))

        VTS4 = ("VTA", "VTB", "VTC", "VTD")
        p.wait("sync", "VE", v_p2_ve)
        p.wait("sync", "AC", v_p2_ac)
        p.wait("sync", "VSA", 16 * 8)
        p.wait("sync", "VSB", 16 * 8)

        def vth_issue(h, pe_guard):
            sl = VTS4[h % 4]
            p.wait("sync", "PE", pe_guard)
            for t in range(NT):
                p.wait("sync", sl, p.cnt[sl])
                p.run("sync", lambda e, h=h, t=t: e.dma_start(
                    out=vth[h % 4][:, t * 128:(t + 1) * 128],
                    in_=v_s[t * 128:(t + 1) * 128, h * 128:(h + 1) * 128]),
                    inc=sl, inck=16)

        for hh in range(4):
            vth_issue(hh, 0)

        g_psc = [0, 0, 0]
        g_ae = [0, 0, 0]
        g_pctx = [0, 0]
        g_psr = [0, 0]
        g_pbc = 0
        g_cxo = [0, 0]
        g_srr = 0
        p.wait("tensor", "VE", v_p2_ve)
        p.wait("tensor", "AC", v_p2_ac)
        u = 0
        for h in range(H):
            hb4 = h % 4
            p.wait("tensor", VTS4[hb4], 256 * (h // 4 + 1))
            for s4 in range(4):
                Q0 = 512 * s4
                nblocks = 4 * (s4 + 1)
                hb = u % 2
                u += 1
                score_done = {}

                def emit_score(j, h=h, hb=hb, Q0=Q0, s4=s4):
                    r = 128 * (j - 4 * s4) if j >= 4 * s4 else 0
                    sb = j % 3
                    p.wait("tensor", "AC", g_psc[sb])
                    if j == 0:
                        p.wait("tensor", "VE", g_pctx[hb])
                        p.wait("tensor", "AC", g_psr[hb])
                    score_done[j] = p.run(
                        "tensor", lambda e, h=h, j=j, sb=sb, r=r, Q0=Q0:
                        e.matmul(psc[sb][:, r:512],
                                 kh[h][:, j * 128:(j + 1) * 128],
                                 qh[h][:, Q0 + r:Q0 + 512],
                                 start=True, stop=True), inc="PE")

                emit_score(0)
                if nblocks > 1:
                    emit_score(1)
                for j in range(nblocks):
                    r = 128 * (j - 4 * s4) if j >= 4 * s4 else 0
                    sb = j % 3
                    ab = j % 3
                    p.wait("scalar", "PE", max(score_done[j], g_ae[ab]))
                    v_exp = p.run("scalar", lambda e, sb=sb, r=r, ab=ab:
                                  e.activation(ae[ab][:, r:512], psc[sb][:, r:512],
                                               AF.Exp, bias=0.0,
                                               scale=s_isq[:]), inc="AC")
                    g_psc[sb] = v_exp
                    if j >= 4 * s4:
                        p.wait("vector", "AC", v_exp)
                        v_msk = p.run("vector", lambda e, ab=ab, r=r:
                                      e.tensor_tensor(ae[ab][:, r:r + 128],
                                                      ae[ab][:, r:r + 128],
                                                      s_trim[:], OP.mult), inc="VE")
                    else:
                        v_msk = None
                    if j + 2 < nblocks:
                        emit_score(j + 2)
                    if v_msk is not None:
                        p.wait("tensor", "VE", v_msk)
                    else:
                        p.wait("tensor", "AC", v_exp)
                    st_, sp_ = (j == 0), (j == nblocks - 1)
                    p.run("tensor", lambda e, hb4=hb4, j=j, hb=hb, r=r, ab=ab,
                          st_=st_, sp_=sp_:
                          e.matmul(pctx[hb][:, r:512],
                                   vth[hb4][:, j * 128:(j + 1) * 128],
                                   ae[ab][:, r:512], start=st_, stop=sp_))
                    v_bmm = p.run("tensor", lambda e, hb=hb, r=r, ab=ab, st_=st_,
                                  sp_=sp_: e.matmul(psr[hb][:, r:512], s_ones_bf[:],
                                                    ae[ab][:, r:512],
                                                    start=st_, stop=sp_), inc="PE")
                    g_ae[ab] = v_bmm
                p.wait("scalar", "PE", max(v_bmm, g_srr))
                p.run("scalar", lambda e, hb=hb: e.activation(
                    srow[:], psr[hb][:], AF.Ln, bias=0.0, scale=1.0))
                v_rcp = p.run("scalar", lambda e: e.activation(
                    srr[0][:], srow[:], AF.Exp, bias=0.0, scale=-1.0), inc="AC")
                g_psr[hb] = v_rcp
                p.wait("tensor", "AC", v_rcp)
                p.wait("tensor", "VE", g_pbc)
                v_bc = p.run("tensor", lambda e: e.matmul(
                    pbcp[:], s_onesr_f[:], srr[0][:], start=True, stop=True),
                    inc="PE")
                g_srr = v_bc
                p.wait("vector", "PE", v_bc)
                v_rbc = p.run("vector", lambda e: e.tensor_copy(
                    rbc[0][:], pbcp[:]), inc="VE")
                g_pbc = v_rbc
                p.wait("vector", "CS", p.cnt["CS"] if g_cxo[hb] else 0)
                v_cx = p.run("vector", lambda e, hb=hb: e.tensor_tensor(
                    cxo[hb][:], pctx[hb][:], rbc[0][:], OP.mult), inc="VE")
                g_pctx[hb] = v_cx
                p.wait("gpsimd", "VE", v_cx)
                v_gd = p.run("gpsimd", lambda e, h=h, s4=s4, hb=hb: e.dma_start(
                    out=ctx_s[h * 128:(h + 1) * 128, s4 * 512:(s4 + 1) * 512],
                    in_=cxo[hb][:]), inc="CS", inck=16)
                g_cxo[hb] = v_gd
                do_quants(1)
            if h + 4 < H:
                vth_issue(h + 4, p.cnt["PE"])
        v_ctx_gd = p.cnt["CS"]
        v_p3_pe = p.cnt["PE"]
        v_p3_ve = p.cnt["VE"]
        v_p3_ac = p.cnt["AC"]
        st3.close()
        st_kh.close()
        st_qk.close()
        barrier(exclude=("SG", "SH", "SI", "SJ", "SK", "SL", "TS"))

        st4 = ExitStack()
        E4 = st4.enter_context
        ctxc = [E4(nc.sbuf_tensor(f"ctxc{c}", [128, S], BF16)) for c in range(16)]
        woc = [E4(nc.sbuf_tensor(f"woc{c}", [128, D], BF16)) for c in range(16)]
        s_bo = E4(nc.sbuf_tensor("s_bo", [1, D], F32))
        xt4 = [E4(nc.sbuf_tensor(f"xt4_{i}", [128, D], F32)) for i in range(2)]
        x1t = [E4(nc.sbuf_tensor(f"x1t{i}", [128, D], F32)) for i in range(2)]
        pout = [E4(nc.psum_tensor(f"pout{i}", [128, 512], F32)) for i in range(8)]

        p.wait("sync", "CS", v_ctx_gd)
        p.wait("sync", "VE", v_p3_ve)
        p.wait("sync", "AC", v_p3_ac)
        ctx_ld = []
        for c in range(16):
            vls = p.run("sync", lambda e, c=c: e.dma_start(
                out=ctxc[c][:], in_=ctx_s[c * 128:(c + 1) * 128, :]), inc="CT", inck=16)
            ctx_ld.append(vls)
        v_bo_ld = p.run("sync", lambda e: e.dma_start(out=s_bo[:], in_=bo),
                        inc="LA", inck=16)
        p.wait("gpsimd", "PE", v_p3_pe)
        wo_ld = []
        for c in range(16):
            vg = p.run("gpsimd", lambda e, c=c: e.dma_start(
                out=woc[c][:], in_=wo[c * 128:(c + 1) * 128, :]), inc="WG", inck=16)
            wo_ld.append(vg)
        p.wait("tensor", "LA", v_bo_ld)
        p.wait("tensor", "PE", v_p3_pe)
        p.wait("tensor", "VE", v_p3_ve)
        p.wait("tensor", "AC", v_p3_ac)

        g_pout = [0, 0]
        g_x1 = [0, 0]
        g_xt4 = [0, 0]
        for t in range(NT):
            pg = t % 2
            p.wait("sync", "AC", g_xt4[pg])
            v_xl = slot_load(pg, xt4[pg][:], x_in[t * 128:(t + 1) * 128, :])
            p.wait("tensor", "WG", wo_ld[15])
            p.wait("tensor", "CT", ctx_ld[15])
            p.wait("tensor", "VE", g_pout[pg])
            mms = []
            for dv in range(16):
                for o4 in range(4):
                    mms.append((pout[4 * pg + o4][:],
                                ctxc[dv][:, t * 128:(t + 1) * 128],
                                woc[dv][:, o4 * 512:(o4 + 1) * 512],
                                dv == 0, False))
            for o4 in range(4):
                mms.append((pout[4 * pg + o4][:], s_onesr_f[:],
                            s_bo[:, o4 * 512:(o4 + 1) * 512], False, True))
            v_wmm = mm_seq(mms)
            p.wait("vector", "PE", v_wmm)
            p.wait("vector", SLOTS[pg], v_xl)
            p.wait("vector", "X1S", p.cnt["X1S"] if g_x1[pg] else 0)
            for o4 in range(4):
                p.run("vector", lambda e, pg=pg, o4=o4: e.tensor_tensor(
                    x1t[pg][:, o4 * 512:(o4 + 1) * 512], pout[4 * pg + o4][:],
                    xt4[pg][:, o4 * 512:(o4 + 1) * 512], OP.add))
            v_x1 = p.run("vector", lambda e, pg=pg, t=t: e.reduce_sum(
                pan2["sum"][:, t:t + 1], x1t[pg][:], AX), inc="VE")
            g_pout[pg] = v_x1
            p.wait("scalar", "VE", v_x1)
            v_sq2 = p.run("scalar", lambda e, pg=pg, t=t: e.activation(
                xt4[pg][:], x1t[pg][:], AF.Square, bias=0.0, scale=1.0,
                accum_out=pan2["sq"][:, t:t + 1]), inc="AC")
            g_xt4[pg] = v_sq2
            p.wait("gpsimd", "AC", v_sq2)
            v_gd = p.run("gpsimd", lambda e, pg=pg, t=t: e.dma_start(
                out=x1_s[t * 128:(t + 1) * 128, :], in_=x1t[pg][:]), inc="X1S", inck=16)
            g_x1[pg] = v_gd
            do_quants(3)
        do_quants(999)
        v_x1_gd = p.cnt["X1S"]
        v_nmr2 = panel_math(pan2, p.cnt["VE"], p.cnt["AC"])
        v_p4_pe = p.cnt["PE"]
        v_p4_ve = p.cnt["VE"]
        v_p4_ac = p.cnt["AC"]
        st4.close()
        st_tern.close()
        barrier()

        def ffn_half(hf, v_p4_pe, v_p4_ve, v_p4_ac, v_x1_gd, v_nmr2):
            T0 = hf * 8
            R0 = hf * 1024
            st5a = ExitStack()
            E5 = st5a.enter_context
            x1l = [E5(nc.sbuf_tensor(f"x1l{i}_{hf}", [128, D], F32)) for i in range(2)]
            xln2 = [E5(nc.sbuf_tensor(f"xln2_{i}_{hf}", [128, D], F32)) for i in range(8)]
            tq = [E5(nc.sbuf_tensor(f"tq{i}_{hf}", [128, D], F32)) for i in range(2)]
            tqk = [E5(nc.sbuf_tensor(f"tqk{i}_{hf}", [128, D], F32)) for i in range(2)]
            tqd = E5(nc.sbuf_tensor(f"tqd_{hf}", [128, D], BF16))
            tqc = E5(nc.sbuf_tensor(f"tqc_{hf}", [128, D], BF16))
            tqi = [E5(nc.sbuf_tensor(f"tqi{i}_{hf}", [128, D], I32)) for i in range(2)]
            xqb = [E5(nc.sbuf_tensor(f"xqb{i}_{hf}", [128, D], BF16)) for i in range(2)]

            p.wait("scalar", "VE", v_nmr2)
            p.wait("sync", "X1S", v_x1_gd)
            p.wait("sync", "PE", v_p4_pe)
            p.wait("sync", "VE", v_p4_ve)
            p.wait("sync", "AC", v_p4_ac)
            p.wait("sync", "ST", p.cnt["ST"])
            g_x1l = [0, 0]
            for t in range(8):
                gt = T0 + t
                b = t % 2
                p.wait("sync", "VE", g_x1l[b])
                v_xls = slot_load(b, x1l[b][:], x1_s[gt * 128:(gt + 1) * 128, :])
                p.wait("scalar", SLOTS[b], v_xls)
                v_n = p.run("scalar", lambda e, b=b, t=t, gt=gt: e.activation(
                    xln2[t][:], x1l[b][:], AF.Identity,
                    bias=pan2["nmr"][:, gt:gt + 1],
                    scale=pan2["rstd2"][:, gt:gt + 1]), inc="AC")
                p.wait("vector", "AC", v_n)
                v_mx = p.run("vector", lambda e, t=t: e.reduce_max(
                    maxpan1[:, t:t + 1], xln2[t][:], AX,
                    apply_absolute_value=True), inc="VE")
                g_x1l[b] = v_mx
            p.run("vector", lambda e: e.tensor_scalar(
                maxpan1[:], maxpan1[:], EPS, None, OP.max))
            p.run("vector", lambda e: e.reciprocal(panwk[:], maxpan1[:]))
            nr_recip(panwk[:], maxpan1[:], evacpan1[:])
            p.run("vector", lambda e: e.tensor_scalar(
                scalepan1[:], panwk[:], 127.0, None, OP.mult))
            p.run("vector", lambda e: e.tensor_scalar(
                evacpan1[:], maxpan1[:], s_r127[:], None, OP.mult))
            v_pan1 = p.run("vector", lambda e: e.tensor_scalar(
                evacpan1[:], evacpan1[:], s_sw["up"][:], None, OP.mult), inc="VE")

            g_xqb = [0, 0]
            g_tq = [0, 0]
            p.wait("scalar", "VE", max(v_toff, v_pan1))
            v_q = 0
            for t in range(8):
                gt = T0 + t
                b = t % 2
                p.wait("scalar", "VE", g_tq[b])
                p.run("scalar", lambda e, t=t, b=b: e.activation(
                    tq[b][:], xln2[t][:], AF.Identity,
                    bias=s_toff[:, 0:1], scale=scalepan1[:, t:t + 1]))
                p.run("scalar", lambda e, b=b: e.copy(tqi[b][:], tq[b][:]))
                v_t1 = p.run("scalar", lambda e, b=b: e.copy(tqk[b][:], tqi[b][:]),
                             inc="AC")
                p.wait("vector", "AC", v_t1)
                p.wait("vector", "QS", p.cnt["QS"] if g_xqb[b] else 0)
                p.run("vector", lambda e, b=b: e.tensor_sub(tqd[:], tq[b][:], tqk[b][:]))
                p.run("vector", lambda e: e.tensor_scalar(
                    tqd[:], tqd[:], s_sdc[:], None, OP.add))
                p.run("vector", lambda e: e.tensor_scalar(
                    tqc[:], tqd[:], 1.0, None, OP.is_gt))
                p.run("vector", lambda e: e.scalar_tensor_tensor(
                    tqc[:], tqd[:], 2.0, tqc[:], OP.is_gt, OP.add))
                p.run("vector", lambda e: e.scalar_tensor_tensor(
                    tqc[:], tqd[:], 3.0, tqc[:], OP.is_gt, OP.add))
                p.run("vector", lambda e: e.scalar_tensor_tensor(
                    tqc[:], tqd[:], 0.0, tqc[:], OP.is_ge, OP.add))
                p.run("vector", lambda e: e.scalar_tensor_tensor(
                    tqc[:], tqd[:], -1.0, tqc[:], OP.is_ge, OP.add))
                p.run("vector", lambda e: e.scalar_tensor_tensor(
                    tqc[:], tqd[:], -2.0, tqc[:], OP.is_ge, OP.add))
                v_q = p.run("vector", lambda e, b=b: e.scalar_tensor_tensor(
                    xqb[b][:], tqc[:], -259.0, tqk[b][:], OP.add, OP.add), inc="VE")
                g_tq[b] = v_q
                p.wait("gpsimd", "VE", v_q)
                g_xqb[b] = p.run("gpsimd", lambda e, b=b, gt=gt: e.dma_start(
                    out=xq_s[gt * 128:(gt + 1) * 128, :], in_=xqb[b][:]),
                    inc="QS", inck=16)
            v_xq_gd = p.cnt["QS"]
            v_a_ve = p.cnt["VE"]
            v_a_ac = p.cnt["AC"]
            st5a.close()

            st5b = ExitStack()
            E5 = st5b.enter_context
            ht = [E5(nc.sbuf_tensor(f"ht{t}_{hf}", [128, FI], F32)) for t in range(8)]
            trelu = E5(nc.sbuf_tensor(f"trelu_{hf}", [128, 512], F32))
            pup = [E5(nc.psum_tensor(f"pup{i}_{hf}", [128, 512], F32)) for i in range(4)]
            st5bio = ExitStack()
            E5i = st5bio.enter_context
            xqf = [E5i(nc.sbuf_tensor(f"xqf{c}_{hf}", [128, 1024], BF16)) for c in range(16)]
            rup = [[E5i(nc.sbuf_tensor(f"rup{g}_{k}_{hf}", [128, 512], BF16))
                    for k in range(16)] for g in range(2)]

            p.wait("sync", "QS", v_xq_gd)
            p.wait("sync", "VE", v_a_ve)
            p.wait("sync", "AC", v_a_ac)
            xqf_ld = []
            for c in range(16):
                vls = p.run("sync", lambda e, c=c, R0=R0: e.dma_start(
                    out=xqf[c][:], in_=xq_s[R0:R0 + 1024, c * 128:(c + 1) * 128],
                    transpose=True), inc="XQF", inck=16)
                xqf_ld.append(vls)

            g_pup = [0, 0, 0, 0]
            p.wait("tensor", "PE", v_p4_pe)
            p.wait("tensor", "VE", v_p4_ve)
            p.wait("tensor", "AC", v_p4_ac)

            def rup_load(ip):
                g = ip % 2
                sem = ("RWA", "RWB")[g]
                last = None
                for hc in range(16):
                    last = p.run("sync", lambda e, g=g, hc=hc, ip=ip: e.dma_start(
                        out=rup[g][hc][:],
                        in_=tup_s[ip * 512:(ip + 1) * 512, hc * 128:(hc + 1) * 128],
                        transpose=True), inc=sem, inck=16)
                return last

            rup_rdy = [0, 0]
            rup_pe_hist = [0, 0]
            rup_rdy[0] = rup_load(0)
            for ip in range(8):
                g = ip % 2
                if ip + 1 < 8:
                    p.wait("sync", "PE", rup_pe_hist[(ip + 1) % 2])
                    rup_rdy[(ip + 1) % 2] = rup_load(ip + 1)
                for t in range(8):
                    pb = t % 4
                    p.wait("tensor", ("RWA", "RWB")[g], rup_rdy[g])
                    p.wait("tensor", "XQF", xqf_ld[-1])
                    p.wait("tensor", "AC", g_pup[pb])
                    mms = []
                    for hc in range(16):
                        mms.append((pup[pb][:], xqf[hc][:, t * 128:(t + 1) * 128],
                                    rup[g][hc][:], hc == 0, hc == 15))
                    v_up = mm_seq(mms)
                    p.wait("scalar", "PE", v_up)
                    p.run("scalar", lambda e, pb=pb, t=t: e.activation(
                        trelu[:], pup[pb][:], AF.Relu, bias=0.0,
                        scale=evacpan1[:, t:t + 1]))
                    v_hsq = p.run("scalar", lambda e, t=t, ip=ip: e.activation(
                        ht[t][:, ip * 512:(ip + 1) * 512], trelu[:], AF.Square,
                        bias=0.0, scale=1.0), inc="AC")
                    g_pup[pb] = v_hsq
                    p.wait("vector", "AC", v_hsq)
                    v_hm = p.run("vector", lambda e, t=t, ip=ip: e.reduce_max(
                        hmaxpan[:, t * 8 + ip:t * 8 + ip + 1],
                        ht[t][:, ip * 512:(ip + 1) * 512], AX), inc="VE")
                rup_pe_hist[g] = p.cnt["PE"]

            for t in range(8):
                p.run("vector", lambda e, t=t: e.reduce_max(
                    max2pan[:, t:t + 1], hmaxpan[:, t * 8:(t + 1) * 8], AX))
            p.run("vector", lambda e: e.tensor_scalar(
                max2pan[:], max2pan[:], EPS, None, OP.max))
            p.run("vector", lambda e: e.reciprocal(panwk[:], max2pan[:]))
            nr_recip(panwk[:], max2pan[:], evacpan2[:])
            p.run("vector", lambda e: e.tensor_scalar(
                scalepan2[:], panwk[:], 127.0, None, OP.mult))
            p.run("vector", lambda e: e.tensor_scalar(
                evacpan2[:], max2pan[:], s_r127[:], None, OP.mult))
            v_pan2 = p.run("vector", lambda e: e.tensor_scalar(
                evacpan2[:], evacpan2[:], s_sw["dn"][:], None, OP.mult), inc="VE")

            st5bio.close()
            FH = FI // 2
            tq2 = [E5(nc.sbuf_tensor(f"tq2{i}_{hf}", [128, FH], F32)) for i in range(2)]
            tq2k = [E5(nc.sbuf_tensor(f"tq2k{i}_{hf}", [128, FH], F32)) for i in range(2)]
            tq2d = E5(nc.sbuf_tensor(f"tq2d_{hf}", [128, FH], BF16))
            tq2c = E5(nc.sbuf_tensor(f"tq2c_{hf}", [128, FH], BF16))
            tq2i = [E5(nc.sbuf_tensor(f"tq2i{i}_{hf}", [128, FH], I32)) for i in range(2)]
            hqb = [E5(nc.sbuf_tensor(f"hqb{i}_{hf}", [128, FH], BF16)) for i in range(2)]
            g_hqb = [0, 0]
            g_tq2 = [0, 0]
            p.wait("scalar", "VE", v_pan2)
            v_q2 = 0
            for idx in range(16):
                t, q2 = idx // 2, idx % 2
                gt = T0 + t
                b = idx % 2
                p.wait("scalar", "VE", g_tq2[b])
                p.run("scalar", lambda e, t=t, q2=q2, b=b: e.activation(
                    tq2[b][:], ht[t][:, q2 * FH:(q2 + 1) * FH], AF.Identity,
                    bias=s_toff[:, 0:1], scale=scalepan2[:, t:t + 1]))
                p.run("scalar", lambda e, b=b: e.copy(tq2i[b][:], tq2[b][:]))
                v_t1 = p.run("scalar", lambda e, b=b: e.copy(tq2k[b][:], tq2i[b][:]),
                             inc="AC")
                p.wait("vector", "AC", v_t1)
                p.wait("vector", "HS", p.cnt["HS"] if g_hqb[b] else 0)
                p.run("vector", lambda e, b=b: e.tensor_sub(
                    tq2d[:], tq2[b][:], tq2k[b][:]))
                p.run("vector", lambda e: e.tensor_scalar(
                    tq2d[:], tq2d[:], s_sdc[:], None, OP.add))
                p.run("vector", lambda e: e.tensor_scalar(
                    tq2c[:], tq2d[:], 1.0, None, OP.is_gt))
                p.run("vector", lambda e: e.scalar_tensor_tensor(
                    tq2c[:], tq2d[:], 2.0, tq2c[:], OP.is_gt, OP.add))
                p.run("vector", lambda e: e.scalar_tensor_tensor(
                    tq2c[:], tq2d[:], 3.0, tq2c[:], OP.is_gt, OP.add))
                p.run("vector", lambda e: e.scalar_tensor_tensor(
                    tq2c[:], tq2d[:], 0.0, tq2c[:], OP.is_ge, OP.add))
                p.run("vector", lambda e: e.scalar_tensor_tensor(
                    tq2c[:], tq2d[:], -1.0, tq2c[:], OP.is_ge, OP.add))
                p.run("vector", lambda e: e.scalar_tensor_tensor(
                    tq2c[:], tq2d[:], -2.0, tq2c[:], OP.is_ge, OP.add))
                v_q2 = p.run("vector", lambda e, b=b: e.scalar_tensor_tensor(
                    hqb[b][:], tq2c[:], -259.0, tq2k[b][:], OP.add, OP.add), inc="VE")
                g_tq2[b] = v_q2
                p.wait("gpsimd", "VE", v_q2)
                g_hqb[b] = p.run("gpsimd", lambda e, b=b, gt=gt, q2=q2: e.dma_start(
                    out=hq_s[gt * 128:(gt + 1) * 128, q2 * FH:(q2 + 1) * FH],
                    in_=hqb[b][:]), inc="HS", inck=16)
                if idx == 7:
                    v_hq_mid = p.cnt["HS"]
                if idx == 11:
                    v_hq_t5 = p.cnt["HS"]
            v_hq_gd = p.cnt["HS"]
            v_b_ve = p.cnt["VE"]
            v_b_ac = p.cnt["AC"]
            v_b_pe = p.cnt["PE"]
            st5b.close()

            st5c = ExitStack()
            E5 = st5c.enter_context
            hqf = [E5(nc.sbuf_tensor(f"hqf{c}_{hf}", [128, 1024], BF16)) for c in range(32)]
            rdn = [[E5(nc.sbuf_tensor(f"rdn{g}_{k}_{hf}", [128, 512], BF16))
                    for k in range(32)] for g in range(2)]
            x1c = [E5(nc.sbuf_tensor(f"x1c{i}_{hf}", [128, 512], F32)) for i in range(2)]
            obuf = [E5(nc.sbuf_tensor(f"obuf{i}_{hf}", [128, 512], F32)) for i in range(2)]
            tdo = E5(nc.sbuf_tensor(f"tdo_{hf}", [128, 512], F32))
            pdn = [E5(nc.psum_tensor(f"pdn{i}_{hf}", [128, 512], F32)) for i in range(4)]

            def rdn_load(op4):
                g = op4 % 2
                sem = ("RWA", "RWB")[g]
                last = None
                for ic in range(32):
                    last = p.run("sync", lambda e, g=g, ic=ic, op4=op4: e.dma_start(
                        out=rdn[g][ic][:],
                        in_=tdn_s[op4 * 512:(op4 + 1) * 512, ic * 128:(ic + 1) * 128],
                        transpose=True), inc=sem, inck=16)
                return last

            hqf_ld = []
            p.wait("sync", "HS", v_hq_mid)
            for c in range(32):
                vls = p.run("sync", lambda e, c=c, R0=R0: e.dma_start(
                    out=hqf[c][:, 0:512], in_=hq_s[R0:R0 + 512, c * 128:(c + 1) * 128],
                    transpose=True), inc="XQF", inck=16)
                hqf_ld.append(vls)
            p.wait("sync", "HS", v_hq_t5)
            rdn_rdy = [0, 0]
            rdn_pe_hist = [v_b_pe, v_b_pe]
            rdn_rdy[0] = rdn_load(0)
            p.wait("sync", "HS", v_hq_gd)
            p.wait("sync", "VE", v_b_ve)
            p.wait("sync", "AC", v_b_ac)
            for c in range(32):
                vls = p.run("sync", lambda e, c=c, R0=R0: e.dma_start(
                    out=hqf[c][:, 512:1024],
                    in_=hq_s[R0 + 512:R0 + 1024, c * 128:(c + 1) * 128],
                    transpose=True), inc="XQF", inck=16)
                hqf_ld.append(vls)

            g_pdn = [0, 0, 0, 0]
            g_ob = [0, 0]
            g_x1c = [0, 0]
            g_tdo = 0
            p.wait("tensor", "AC", v_b_ac)
            for op4 in range(4):
                g = op4 % 2
                if op4 + 1 < 4:
                    p.wait("sync", "PE", rdn_pe_hist[(op4 + 1) % 2])
                    rdn_rdy[(op4 + 1) % 2] = rdn_load(op4 + 1)
                for t in range(8):
                    gt = T0 + t
                    b = t % 2
                    pb = t % 4
                    p.wait("sync", "VE", g_x1c[b])
                    v_x1r = slot_load(b, x1c[b][:],
                                      x1_s[gt * 128:(gt + 1) * 128,
                                           op4 * 512:(op4 + 1) * 512])
                    p.wait("tensor", ("RWA", "RWB")[g], rdn_rdy[g])
                    p.wait("tensor", "XQF", hqf_ld[-1])
                    p.wait("tensor", "AC", g_pdn[pb])
                    mms = []
                    for ic in range(32):
                        mms.append((pdn[pb][:], hqf[ic][:, t * 128:(t + 1) * 128],
                                    rdn[g][ic][:], ic == 0, ic == 31))
                    v_dn = mm_seq(mms)
                    p.wait("scalar", "PE", v_dn)
                    p.wait("scalar", "VE", g_tdo)
                    v_ev = p.run("scalar", lambda e, pb=pb, t=t: e.activation(
                        tdo[:], pdn[pb][:], AF.Copy, bias=0.0,
                        scale=evacpan2[:, t:t + 1]), inc="AC")
                    g_pdn[pb] = v_ev
                    p.wait("vector", "AC", v_ev)
                    p.wait("vector", SLOTS[b], v_x1r)
                    p.wait("vector", "ST", p.cnt["ST"] if g_ob[b] else 0)
                    v_o = p.run("vector", lambda e, b=b: e.tensor_tensor(
                        obuf[b][:], tdo[:], x1c[b][:], OP.add), inc="VE")
                    g_x1c[b] = v_o
                    g_tdo = v_o
                    p.wait("gpsimd", "VE", v_o)
                    v_st = p.run("gpsimd", lambda e, b=b, gt=gt, op4=op4: e.dma_start(
                        out=out_t[gt * 128:(gt + 1) * 128,
                                  op4 * 512:(op4 + 1) * 512], in_=obuf[b][:]),
                        inc="ST", inck=16)
                    g_ob[b] = v_st
                rdn_pe_hist[g] = p.cnt["PE"]
            v_dn_ve = p.cnt["VE"]
            v_dn_ac = p.cnt["AC"]
            v_dn_pe = p.cnt["PE"]
            st5c.close()
            return v_dn_pe, v_dn_ve, v_dn_ac

        for hf in range(2):
            v_p4_pe, v_p4_ve, v_p4_ac = ffn_half(
                hf, v_p4_pe, v_p4_ve, v_p4_ac, v_x1_gd, v_nmr2)
            v_x1_gd = 0

        p.wait("sync", "ST", p.cnt["ST"])

        for name in p.cnt:
            p.handles[name] = E(nc.semaphore(name))
        block = E(nc.Block())

        for ename, attr in (("sync", block.sync), ("scalar", block.scalar),
                            ("vector", block.vector), ("tensor", block.tensor),
                            ("gpsimd", block.gpsimd)):
            def section(eng, ename=ename):
                for f in p.ops[ename]:
                    f(eng)
            attr(section)

    return nc


def _host_constants():
    import jax
    import jax.numpy as jnp
    ROPE_BASE = 10000.0
    with jax.default_device(jax.local_devices(backend="cpu")[0]):
        inv_freq = 1.0 / ROPE_BASE ** (jnp.arange(0, RD, 2, dtype=jnp.float32) / RD)
        t = jnp.arange(S, dtype=jnp.float32)
        freqs = jnp.outer(t, inv_freq)
        emb = jnp.concatenate((freqs, freqs), axis=-1)
        cos = np.asarray(jnp.cos(emb).astype(jnp.float32))
        sin = np.asarray(jnp.sin(emb).astype(jnp.float32))
    cosd = np.zeros((128, S), np.float32)
    sind = np.zeros((128, S), np.float32)
    cosd[0:64] = cos.T
    cosd[64:128] = cos.T
    sind[0:64] = sin.T
    sind[64:128] = sin.T
    tm = np.triu(np.ones((128, 128), np.float32)).astype(ml_dtypes.bfloat16)
    return {
        "cosd": cosd,
        "sind": sind,
        "trimask": tm,
        "onescol_bf": np.ones((128, 1), ml_dtypes.bfloat16),
        "onesrow_f": np.ones((1, 128), np.float32),
        "onescol_f": np.ones((128, 1), np.float32),
        "invN_row": np.full((1, 128), 1.0 / (FI * D), np.float32),
        "halves": np.full((128, 512), 1.5, np.float32),
    }


_CACHE = {}


def kernel(L0_attn_x, L0_attn_W_kv_down, L0_attn_W_q_down, L0_attn_W_kc,
           L0_attn_W_v, L0_attn_W_qc, L0_attn_W_kr, L0_attn_W_qr, L0_attn_W_o,
           L0_attn_b_o, L0_ffn_w_up, L0_ffn_w_down):
    if "nc" not in _CACHE:
        _CACHE["nc"] = build_nc()
    nc = _CACHE["nc"]
    x = np.ascontiguousarray(np.asarray(L0_attn_x, np.float32))
    common = dict(
        wkvd=np.ascontiguousarray(np.asarray(L0_attn_W_kv_down, np.float32)),
        wqd=np.ascontiguousarray(np.asarray(L0_attn_W_q_down, np.float32)),
        wkc=np.ascontiguousarray(np.asarray(L0_attn_W_kc, np.float32)),
        wv=np.ascontiguousarray(np.asarray(L0_attn_W_v, np.float32)),
        wqc=np.ascontiguousarray(np.asarray(L0_attn_W_qc, np.float32)),
        wkr=np.ascontiguousarray(np.asarray(L0_attn_W_kr, np.float32)),
        wqr=np.ascontiguousarray(np.asarray(L0_attn_W_qr, np.float32)),
        wo=np.ascontiguousarray(np.asarray(L0_attn_W_o, np.float32)),
        bo=np.ascontiguousarray(np.asarray(L0_attn_b_o, np.float32).reshape(1, D)),
        wup=np.ascontiguousarray(np.asarray(L0_ffn_w_up, np.float32)),
        wdn=np.ascontiguousarray(np.asarray(L0_ffn_w_down, np.float32)),
        **_host_constants(),
    )
    in_maps = []
    for b in range(8):
        m = dict(common)
        m["x_in"] = np.ascontiguousarray(x[b])
        in_maps.append(m)
    res = run_bass_kernel_spmd(nc, in_maps, core_ids=list(range(8)))
    return np.stack([np.asarray(res.results[b]["out"]) for b in range(8)], axis=0)


# version 17
# speedup vs baseline: 1.0192x; 1.0192x over previous
import numpy as np
import ml_dtypes
from contextlib import ExitStack

import concourse.bass as bass
import concourse.mybir as mybir
from concourse.alu_op_type import AluOpType as OP
from concourse.bass_utils import run_bass_kernel_spmd

F32 = mybir.dt.float32
BF16 = mybir.dt.bfloat16
I32 = mybir.dt.int32
AX = mybir.AxisListType.X
AF = mybir.ActivationFunctionType

S = 2048
D = 2048
H = 16
RD = 64
KVL = 512
QL = 1536
FI = 4096
NT = 16
INV_SQRT_HD = float(1.0 / np.sqrt(128.0))
EPS = 1e-5

ENGS = ("sync", "scalar", "vector", "tensor", "gpsimd")


class Prog:
    def __init__(self):
        self.ops = {e: [] for e in ENGS}
        self.cnt = {k: 0 for k in ("PE", "AC", "VE", "ST", "SA", "SB", "SC",
                                   "LA", "XF", "VT", "CT", "RW", "WG",
                                   "XS", "VS", "CS", "X1S", "QS", "HS", "TS", "CZ",
                                   "RWA", "RWB", "XQF", "SE", "SF", "SG", "SH",
                                   "SI", "SJ", "SK", "SL",
                                   "VSA", "VSB", "VTA", "VTB", "VTC", "VTD")}
        self.handles = {}

    def emit(self, eng, fn):
        self.ops[eng].append(fn)

    def bump(self, name, k):
        self.cnt[name] = self.cnt.get(name, 0) + k
        return self.cnt[name]

    def wait(self, eng, name, val):
        if not val or val <= 0:
            return
        h = self.handles

        def f(e, name=name, val=val):
            e.wait_ge(h[name], val)
        self.emit(eng, f)

    AUTO = {"vector": "VE", "scalar": "AC"}

    def run(self, eng, fn, inc=None, inck=1):
        h = self.handles
        own = self.AUTO.get(eng)
        if own is not None:
            prev = self.cnt[own]
            if inc is None:
                inc, inck = own, 1
            assert inc == own, f"{eng} op must inc {own}"
            v = self.bump(inc, inck)

            def f(e, fn=fn, prev=prev, own=own, inck=inck):
                if prev > 0:
                    e.wait_ge(h[own], prev)
                fn(e).then_inc(h[own], inck)
            self.emit(eng, f)
            return v
        if inc is None:
            self.emit(eng, fn)
            return None
        v = self.bump(inc, inck)

        def f(e, fn=fn, inc=inc, inck=inck):
            fn(e).then_inc(h[inc], inck)
        self.emit(eng, f)
        return v


def build_nc():
    nc = bass.Bass("TRN2", target_bir_lowering=False)

    x_in = nc.dram_tensor("x_in", [S, D], F32, kind="ExternalInput").ap()
    wkvd = nc.dram_tensor("wkvd", [D, KVL], F32, kind="ExternalInput").ap()
    wqd = nc.dram_tensor("wqd", [D, QL], F32, kind="ExternalInput").ap()
    wkc = nc.dram_tensor("wkc", [KVL, 1024], F32, kind="ExternalInput").ap()
    wv = nc.dram_tensor("wv", [KVL, D], F32, kind="ExternalInput").ap()
    wqc = nc.dram_tensor("wqc", [QL, 1024], F32, kind="ExternalInput").ap()
    wkr = nc.dram_tensor("wkr", [D, RD], F32, kind="ExternalInput").ap()
    wqr = nc.dram_tensor("wqr", [QL, 1024], F32, kind="ExternalInput").ap()
    wo = nc.dram_tensor("wo", [D, D], F32, kind="ExternalInput").ap()
    bo = nc.dram_tensor("bo", [1, D], F32, kind="ExternalInput").ap()
    wup = nc.dram_tensor("wup", [FI, D], F32, kind="ExternalInput").ap()
    wdn = nc.dram_tensor("wdn", [D, FI], F32, kind="ExternalInput").ap()
    cosd = nc.dram_tensor("cosd", [128, S], F32, kind="ExternalInput").ap()
    sind = nc.dram_tensor("sind", [128, S], F32, kind="ExternalInput").ap()
    trimask = nc.dram_tensor("trimask", [128, 128], BF16, kind="ExternalInput").ap()
    onescol_bf = nc.dram_tensor("onescol_bf", [128, 1], BF16, kind="ExternalInput").ap()
    onesrow_f = nc.dram_tensor("onesrow_f", [1, 128], F32, kind="ExternalInput").ap()
    onescol_f = nc.dram_tensor("onescol_f", [128, 1], F32, kind="ExternalInput").ap()
    invN_row = nc.dram_tensor("invN_row", [1, 128], F32, kind="ExternalInput").ap()
    halves = nc.dram_tensor("halves", [128, 512], F32, kind="ExternalInput").ap()

    xln_s = nc.dram_tensor("xln_s", [S, D], BF16, kind="Internal").ap()
    v_s = nc.dram_tensor("v_s", [S, D], BF16, kind="Internal").ap()
    ctx_s = nc.dram_tensor("ctx_s", [D, S], BF16, kind="Internal").ap()
    x1_s = nc.dram_tensor("x1_s", [S, D], F32, kind="Internal").ap()
    xq_s = nc.dram_tensor("xq_s", [S, D], BF16, kind="Internal").ap()
    hq_s = nc.dram_tensor("hq_s", [S, FI], BF16, kind="Internal").ap()
    tup_s = nc.dram_tensor("tup_s", [FI, D], BF16, kind="Internal").ap()
    tdn_s = nc.dram_tensor("tdn_s", [D, FI], BF16, kind="Internal").ap()
    out_t = nc.dram_tensor("out", [S, D], F32, kind="ExternalOutput").ap()

    p = Prog()

    def mm_seq(mms, inc="PE"):
        v = None
        for i, (o, l, r, st, sp) in enumerate(mms):
            last = (i == len(mms) - 1)
            if last:
                v = p.run("tensor", lambda e, o=o, l=l, r=r, st=st, sp=sp:
                          e.matmul(o, l, r, start=st, stop=sp), inc=inc)
            else:
                p.run("tensor", lambda e, o=o, l=l, r=r, st=st, sp=sp:
                      e.matmul(o, l, r, start=st, stop=sp))
        return v

    SLOTS = ("SA", "SB", "SC")

    def slot_load(slot, out_ap, in_ap, queue="sync"):
        sem = SLOTS[slot]
        p.wait(queue, sem, p.cnt[sem])
        return p.run(queue, lambda e, o=out_ap, i=in_ap: e.dma_start(out=o, in_=i),
                     inc=sem, inck=16)

    def nr_recip(rt, mt, tmp, steps=2):
        for _ in range(steps):
            p.run("vector", lambda e: e.tensor_mul(tmp, mt, rt))
            p.run("vector", lambda e: e.tensor_scalar(
                tmp, tmp, -1.0, 2.0, OP.mult, OP.add))
            p.run("vector", lambda e: e.tensor_mul(rt, rt, tmp))

    def barrier(exclude=(), overrides=None):
        snap = dict(p.cnt)
        if overrides:
            snap.update(overrides)
        for eng in ENGS:
            for sn in snap:
                if sn in exclude:
                    continue
                p.wait(eng, sn, snap[sn])

    with ExitStack() as top:
        E = top.enter_context
        s_toff = E(nc.sbuf_tensor("s_toff", [128, 8], F32))
        s_ones_bf = E(nc.sbuf_tensor("s_ones_bf", [128, 1], BF16))
        s_onesr_f = E(nc.sbuf_tensor("s_onesr_f", [1, 128], F32))
        s_onesc_f = E(nc.sbuf_tensor("s_onesc_f", [128, 1], F32))
        s_invN = E(nc.sbuf_tensor("s_invN", [1, 128], F32))
        s_trim = E(nc.sbuf_tensor("s_trim", [128, 128], BF16))
        s_eps = E(nc.sbuf_tensor("s_eps", [128, 1], F32))
        s_sdc = E(nc.sbuf_tensor("s_sdc", [128, 1], F32))
        s_isq = E(nc.sbuf_tensor("s_isq", [128, 1], F32))
        s_isq2 = E(nc.sbuf_tensor("s_isq2", [128, 1], F32))
        s_r127 = E(nc.sbuf_tensor("s_r127", [128, 1], F32))
        s_r127b = E(nc.sbuf_tensor("s_r127b", [128, 1], F32))
        s_nrw = E(nc.sbuf_tensor("s_nrw", [128, 1], F32))
        pan = {k: E(nc.sbuf_tensor("pan_" + k, [128, NT], F32))
               for k in ("sum", "sum2", "sq", "mean", "var", "rstd", "rstd2", "nmr")}
        pan2 = {k: E(nc.sbuf_tensor("pan2_" + k, [128, NT], F32))
                for k in ("sum", "sum2", "sq", "mean", "var", "rstd", "rstd2", "nmr")}
        s_thr = {k: E(nc.sbuf_tensor(f"s_thr_{k}", [128, 1], F32)) for k in ("up", "dn")}
        s_nthr = {k: E(nc.sbuf_tensor(f"s_nthr_{k}", [128, 1], F32)) for k in ("up", "dn")}
        s_sw = {k: E(nc.sbuf_tensor(f"s_sw_{k}", [128, 1], F32)) for k in ("up", "dn")}
        abspan = {"up": E(nc.sbuf_tensor("abspan_up", [128, 64], F32)),
                  "dn": E(nc.sbuf_tensor("abspan_dn", [128, 64], F32))}
        colsum = E(nc.sbuf_tensor("colsum", [128, 2], F32))
        tot_row = E(nc.sbuf_tensor("tot_row", [1, 2], F32))
        maxpan1 = E(nc.sbuf_tensor("maxpan1", [128, 8], F32))
        scalepan1 = E(nc.sbuf_tensor("scalepan1", [128, 8], F32))
        evacpan1 = E(nc.sbuf_tensor("evacpan1", [128, 8], F32))
        hmaxpan = E(nc.sbuf_tensor("hmaxpan", [128, 64], F32))
        max2pan = E(nc.sbuf_tensor("max2pan", [128, 8], F32))
        scalepan2 = E(nc.sbuf_tensor("scalepan2", [128, 8], F32))
        evacpan2 = E(nc.sbuf_tensor("evacpan2", [128, 8], F32))
        panwk = E(nc.sbuf_tensor("panwk", [128, 8], F32))

        p.run("sync", lambda e: e.dma_start(out=s_ones_bf[:], in_=onescol_bf), inc="LA", inck=16)
        p.run("sync", lambda e: e.dma_start(out=s_onesr_f[:], in_=onesrow_f), inc="LA", inck=16)
        p.run("sync", lambda e: e.dma_start(out=s_onesc_f[:], in_=onescol_f), inc="LA", inck=16)
        p.run("sync", lambda e: e.dma_start(out=s_invN[:], in_=invN_row), inc="LA", inck=16)
        p.run("sync", lambda e: e.dma_start(out=s_trim[:], in_=trimask), inc="LA", inck=16)
        st0 = ExitStack()
        s_offw = st0.enter_context(nc.sbuf_tensor("s_offw", [128, 512], F32))
        s_offs_i = st0.enter_context(nc.sbuf_tensor("s_offs_i", [128, 512], I32))
        p.run("sync", lambda e: e.dma_start(out=s_offw[:], in_=halves), inc="LA", inck=16)
        p.wait("vector", "LA", p.cnt["LA"])
        p.run("vector", lambda e: e.memset(s_eps[:], EPS))
        p.run("vector", lambda e: e.memset(s_isq2[:], 128.0))
        p.run("vector", lambda e: e.memset(s_r127b[:], 127.0))
        v_c0 = p.run("vector", lambda e: e.memset(s_nrw[:], 0.0), inc="VE")
        p.wait("scalar", "VE", v_c0)
        v_c1 = p.run("scalar", lambda e: e.activation(
            s_isq2[:], s_isq2[:], AF.Sqrt, bias=0.0, scale=1.0), inc="AC")
        p.wait("vector", "AC", v_c1)
        p.run("vector", lambda e: e.reciprocal(s_isq[:], s_isq2[:]))
        nr_recip(s_isq[:], s_isq2[:], s_nrw[:])
        p.run("vector", lambda e: e.reciprocal(s_r127[:], s_r127b[:]))
        nr_recip(s_r127[:], s_r127b[:], s_nrw[:])
        p.run("vector", lambda e: e.tensor_copy(s_offs_i[:], s_offw[:]))
        p.run("vector", lambda e: e.tensor_copy(s_offw[:], s_offs_i[:]))
        p.run("vector", lambda e: e.tensor_scalar(
            s_offw[:], s_offw[:], -0.5, 257.0, OP.mult, OP.add))
        p.run("vector", lambda e: e.tensor_copy(s_toff[:], s_offw[:, 0:8]))
        v_toff = p.run("vector", lambda e: e.tensor_scalar(
            s_sdc[:], s_toff[:, 0:1], -1.0, 256.5, OP.mult, OP.add), inc="VE")
        st0.close()
        barrier()

        tern = {}
        NSL = 6
        TSLOT = ("SG", "SH", "SI", "SJ", "SK", "SL")
        st_tern = ExitStack()
        Et = st_tern.enter_context
        tern["buf"] = [Et(nc.sbuf_tensor(f"twb{i}", [128, 1024], F32))
                       for i in range(NSL)]
        tern["ta"] = Et(nc.sbuf_tensor("tta", [128, 1024], BF16))
        tern["tb"] = Et(nc.sbuf_tensor("ttb", [128, 1024], BF16))
        tern["tt"] = [Et(nc.sbuf_tensor(f"ttt{i}", [128, 1024], BF16)) for i in range(2)]
        tern["gv"] = [0] * NSL
        tern["tguard"] = [0, 0]

        def tern_src(kind, c):
            if kind == "up":
                row, hf = c // 2, c % 2
                return (wup[row * 128:(row + 1) * 128, hf * 1024:(hf + 1) * 1024],
                        tup_s[row * 128:(row + 1) * 128, hf * 1024:(hf + 1) * 1024])
            row, qq = c // 4, c % 4
            return (wdn[row * 128:(row + 1) * 128, qq * 1024:(qq + 1) * 1024],
                    tdn_s[row * 128:(row + 1) * 128, qq * 1024:(qq + 1) * 1024])

        tern["ld"] = [0] * NSL

        def tern_issue(units, i):
            kind, c = units[i]
            b = i % NSL
            src, _ = tern_src(kind, c)
            p.wait("sync", "VE", tern["gv"][b])
            p.wait("sync", TSLOT[b], p.cnt[TSLOT[b]])
            tern["ld"][b] = p.run(
                "sync", lambda e, b=b, src=src: e.dma_start(
                    out=tern["buf"][b][:], in_=src), inc=TSLOT[b], inck=16)

        def tern_stat(units, i):
            kind, c = units[i]
            b = i % NSL
            p.wait("vector", TSLOT[b], tern["ld"][b])
            tern["gv"][b] = p.run(
                "vector", lambda e, b=b, kind=kind, c=c: e.reduce_sum(
                    abspan[kind][:, c:c + 1], tern["buf"][b][:], AX,
                    apply_absolute_value=True), inc="VE")

        def tern_scalars(kind, psc, pbc):
            col = 0 if kind == "up" else 1
            v_cs = p.run("vector", lambda e, kind=kind, col=col: e.reduce_sum(
                colsum[:, col:col + 1], abspan[kind][:], AX), inc="VE")
            p.wait("tensor", "VE", v_cs)
            p.wait("tensor", "AC", p.cnt["AC"])
            v_m1 = p.run("tensor", lambda e, col=col, psc=psc: e.matmul(
                psc[0:1, col:col + 1], colsum[:, col:col + 1], s_onesc_f[:],
                start=True, stop=True), inc="PE")
            p.wait("vector", "PE", v_m1)
            v_ev = p.run("vector", lambda e, col=col, psc=psc: e.tensor_copy(
                tot_row[:, col:col + 1], psc[0:1, col:col + 1]), inc="VE")
            p.wait("tensor", "VE", v_ev)
            v_m2 = p.run("tensor", lambda e, col=col, pbc=pbc: e.matmul(
                pbc[:, col:col + 1], s_invN[:], tot_row[:, col:col + 1],
                start=True, stop=True), inc="PE")
            p.wait("vector", "PE", v_m2)
            p.run("vector", lambda e, kind=kind, col=col, pbc=pbc: e.tensor_copy(
                s_sw[kind][:], pbc[:, col:col + 1]))
            p.run("vector", lambda e, kind=kind: e.tensor_scalar(
                s_sw[kind][:], s_sw[kind][:], EPS, None, OP.max))
            p.run("vector", lambda e, kind=kind: e.tensor_scalar(
                s_thr[kind][:], s_sw[kind][:], 0.5, None, OP.mult))
            v_t = p.run("vector", lambda e, kind=kind: e.tensor_scalar(
                s_nthr[kind][:], s_sw[kind][:], -0.5, None, OP.mult), inc="VE")
            tern["rdy_" + kind] = v_t

        def tern_quant(units, i):
            kind, c = units[i]
            b = i % NSL
            tb2 = i % 2
            _, dst = tern_src(kind, c)
            p.wait("vector", TSLOT[b], tern["ld"][b])
            p.wait("vector", "VE", tern["rdy_" + kind])
            p.wait("vector", "TS", p.cnt["TS"] if tern["tguard"][tb2] else 0)
            p.run("vector", lambda e, b=b, kind=kind: e.tensor_scalar(
                tern["ta"][:], tern["buf"][b][:], s_thr[kind][:], None, OP.is_gt))
            p.run("vector", lambda e, b=b, kind=kind: e.tensor_scalar(
                tern["tb"][:], tern["buf"][b][:], s_nthr[kind][:], None, OP.is_lt))
            vt_ = p.run("vector", lambda e, tb2=tb2: e.tensor_tensor(
                tern["tt"][tb2][:], tern["ta"][:], tern["tb"][:], OP.subtract),
                inc="VE")
            tern["gv"][b] = vt_
            p.wait("gpsimd", "VE", vt_)
            vg = p.run("gpsimd", lambda e, tb2=tb2, dst=dst: e.dma_start(
                out=dst, in_=tern["tt"][tb2][:]), inc="TS", inck=16)
            tern["tguard"][tb2] = vg

        stat_units = [("up", c) for c in range(64)] + [("dn", c) for c in range(64)]
        quant_units = list(stat_units)
        cur = {"stat": [0, 0], "quant": [0, 0]}

        def _do(n, key, units, consume):
            for _ in range(n):
                if cur[key][1] >= len(units):
                    break
                while cur[key][0] < min(cur[key][1] + NSL, len(units)):
                    tern_issue(units, cur[key][0])
                    cur[key][0] += 1
                consume(units, cur[key][1])
                cur[key][1] += 1

        def do_stats(n):
            _do(n, "stat", stat_units, tern_stat)

        def do_quants(n):
            _do(n, "quant", quant_units, tern_quant)


        st1 = ExitStack()
        E1 = st1.enter_context
        xt = [E1(nc.sbuf_tensor(f"xt{i}", [128, D], F32)) for i in range(3)]
        dump = E1(nc.sbuf_tensor("dump", [128, D], F32))
        xlnt = [E1(nc.sbuf_tensor(f"xlnt{i}", [128, D], BF16)) for i in range(2)]

        gve = [0, 0, 0]
        gac = [0, 0, 0]
        for t in range(NT):
            b = t % 3
            p.wait("sync", "VE", gve[b])
            p.wait("sync", "AC", gac[b])
            vls = slot_load(b, xt[b][:], x_in[t * 128:(t + 1) * 128, :])
            p.wait("scalar", SLOTS[b], vls)
            va = p.run("scalar", lambda e, b=b, t=t: e.activation(
                dump[:], xt[b][:], AF.Square, bias=0.0, scale=1.0,
                accum_out=pan["sq"][:, t:t + 1]), inc="AC")
            p.wait("vector", SLOTS[b], vls)
            vr = p.run("vector", lambda e, b=b, t=t: e.reduce_sum(
                pan["sum"][:, t:t + 1], xt[b][:], AX), inc="VE")
            gve[b], gac[b] = vr, va
        v_stats_ac, v_stats_ve = p.cnt["AC"], p.cnt["VE"]

        def panel_math(P, v_red_ve, v_sq_ac):
            p.wait("vector", "AC", v_sq_ac)
            p.run("vector", lambda e: e.tensor_scalar(
                P["mean"][:], P["sum"][:], 1.0 / D, None, OP.mult))
            p.run("vector", lambda e: e.tensor_scalar(
                P["var"][:], P["sq"][:], 1.0 / D, None, OP.mult))
            p.run("vector", lambda e: e.tensor_tensor(
                P["rstd"][:], P["mean"][:], P["mean"][:], OP.mult))
            v_pv = p.run("vector", lambda e: e.tensor_tensor(
                P["var"][:], P["var"][:], P["rstd"][:], OP.subtract), inc="VE")
            p.wait("scalar", "VE", v_pv)
            v_sq = p.run("scalar", lambda e: e.activation(
                P["rstd"][:], P["var"][:], AF.Sqrt, bias=s_eps[:], scale=1.0), inc="AC")
            p.wait("vector", "AC", v_sq)
            p.run("vector", lambda e: e.reciprocal(P["rstd2"][:], P["rstd"][:]))
            nr_recip(P["rstd2"][:], P["rstd"][:], P["sum2"][:])
            p.run("vector", lambda e: e.tensor_tensor(
                P["nmr"][:], P["mean"][:], P["rstd2"][:], OP.mult))
            return p.run("vector", lambda e: e.tensor_scalar(
                P["nmr"][:], P["nmr"][:], -1.0, None, OP.mult), inc="VE")

        v_nmr = panel_math(pan, v_stats_ve, v_stats_ac)

        p.wait("scalar", "VE", v_nmr)
        xg = [0, 0]
        g2 = [v_stats_ac, v_stats_ac, v_stats_ac]
        p.wait("sync", "VE", v_stats_ve)
        for t in range(NT):
            b, ob = t % 3, t % 2
            p.wait("sync", "AC", g2[b])
            vls = slot_load(b, xt[b][:], x_in[t * 128:(t + 1) * 128, :])
            p.wait("scalar", SLOTS[b], vls)
            p.wait("scalar", "XS", p.cnt["XS"])
            va = p.run("scalar", lambda e, b=b, ob=ob, t=t: e.activation(
                xlnt[ob][:], xt[b][:], AF.Identity,
                bias=pan["nmr"][:, t:t + 1], scale=pan["rstd2"][:, t:t + 1]), inc="AC")
            g2[b] = va
            p.wait("gpsimd", "AC", va)
            vg = p.run("gpsimd", lambda e, ob=ob, t=t: e.dma_start(
                out=xln_s[t * 128:(t + 1) * 128, :], in_=xlnt[ob][:]), inc="XS", inck=16)
        v_xln_gd = p.cnt["XS"]
        st1.close()
        barrier()

        st_ckv = ExitStack()
        ckv = [st_ckv.enter_context(nc.sbuf_tensor(f"ckv{c}", [128, S], BF16))
               for c in range(4)]
        st_krp = ExitStack()
        krp = st_krp.enter_context(nc.sbuf_tensor("krp", [64, S], BF16))
        st_cq = ExitStack()
        cq = [st_cq.enter_context(nc.sbuf_tensor(f"cq{c}", [128, S], BF16))
              for c in range(12)]
        st_rope = ExitStack()
        Er = st_rope.enter_context
        s_cos = Er(nc.sbuf_tensor("s_cos", [128, S], BF16))
        s_sin = Er(nc.sbuf_tensor("s_sin", [128, S], BF16))
        ropeA = Er(nc.sbuf_tensor("ropeA", [128, 512], F32))
        ropeB = Er(nc.sbuf_tensor("ropeB", [128, 512], F32))
        st_p2ps = ExitStack()
        Ep = st_p2ps.enter_context
        pj = [Ep(nc.psum_tensor(f"pj{i}", [128, 512], F32)) for i in range(8)]
        pr = [pj[4], pj[5], pj[6], pj[7]]
        t_psc = pj[2]
        t_pbc = pj[3]

        st_xf = ExitStack()
        xf = [st_xf.enter_context(nc.sbuf_tensor(f"xf{c}", [128, S], BF16, side="right"))
              for c in range(16)]
        st_wqd = ExitStack()
        wqd_t = [st_wqd.enter_context(nc.sbuf_tensor(f"wqdt{k}", [128, QL // 2], BF16))
                 for k in range(16)]

        p.wait("sync", "XS", v_xln_gd)
        xf_ld = []
        for c in range(16):
            vls = p.run("sync", lambda e, c=c: e.dma_start(
                out=xf[c][:], in_=xln_s[:, c * 128:(c + 1) * 128],
                transpose=True), inc="XF", inck=16)
            xf_ld.append(vls)
        v_xf_all = xf_ld[-1]

        p.wait("tensor", "XF", v_xf_all)
        ev_g = [0, 0]
        for half in range(2):
            colbase = half * (QL // 2)
            p.wait("gpsimd", "PE", p.cnt["PE"])
            wld = []
            for k in range(16):
                vg = p.run("gpsimd", lambda e, k=k, colbase=colbase: e.dma_start(
                    out=wqd_t[k][:],
                    in_=wqd[k * 128:(k + 1) * 128, colbase:colbase + QL // 2]),
                    inc="WG", inck=16)
                wld.append(vg)
            p.wait("tensor", "WG", wld[-1])
            for mh in range(6):
                m = half * 6 + mh
                gg = m % 2
                p.wait("tensor", "AC", ev_g[gg])
                mms = []
                for k in range(16):
                    for n4 in range(4):
                        mms.append((pj[4 * gg + n4][:],
                                    wqd_t[k][:, mh * 128:(mh + 1) * 128],
                                    xf[k][:, n4 * 512:(n4 + 1) * 512],
                                    k == 0, k == 15))
                v_mm = mm_seq(mms)
                p.wait("scalar", "PE", v_mm)
                for n4 in range(4):
                    ev_g[gg] = p.run("scalar", lambda e, m=m, n4=n4, gg=gg: e.copy(
                        cq[m][:, n4 * 512:(n4 + 1) * 512], pj[4 * gg + n4][:]),
                        inc="AC")
                do_stats(22)
        do_stats(128)
        st_wqd.close()

        st_wkv = ExitStack()
        wkv_t = [st_wkv.enter_context(nc.sbuf_tensor(f"wkvt{k}", [128, KVL], BF16))
                 for k in range(16)]
        p.wait("gpsimd", "PE", p.cnt["PE"])
        wld2 = []
        for k in range(16):
            vg = p.run("gpsimd", lambda e, k=k: e.dma_start(
                out=wkv_t[k][:], in_=wkvd[k * 128:(k + 1) * 128, :]), inc="WG", inck=16)
            wld2.append(vg)
        p.wait("tensor", "WG", wld2[-1])
        for m in range(4):
            gg = m % 2
            p.wait("tensor", "AC", ev_g[gg])
            mms = []
            for k in range(16):
                for n4 in range(4):
                    mms.append((pj[4 * gg + n4][:],
                                wkv_t[k][:, m * 128:(m + 1) * 128],
                                xf[k][:, n4 * 512:(n4 + 1) * 512],
                                k == 0, k == 15))
            v_mm = mm_seq(mms)
            p.wait("scalar", "PE", v_mm)
            for n4 in range(4):
                ev_g[gg] = p.run("scalar", lambda e, m=m, n4=n4, gg=gg: e.copy(
                    ckv[m][:, n4 * 512:(n4 + 1) * 512], pj[4 * gg + n4][:]),
                    inc="AC")
        st_wkv.close()

        st_kr = ExitStack()
        Ek = st_kr.enter_context
        wkr_t = [Ek(nc.sbuf_tensor(f"wkrt{k}", [128, RD], BF16)) for k in range(16)]
        wkrr_t = [Ek(nc.sbuf_tensor(f"wkrrt{k}", [128, RD], BF16)) for k in range(16)]

        p.wait("gpsimd", "PE", p.cnt["PE"])
        v_cos = p.run("gpsimd", lambda e: e.dma_start(out=s_cos[:], in_=cosd), inc="CZ", inck=16)
        v_sin = p.run("gpsimd", lambda e: e.dma_start(out=s_sin[:], in_=sind), inc="CZ", inck=16)
        wld3 = []
        for k in range(16):
            vg = p.run("gpsimd", lambda e, k=k: e.dma_start(
                out=wkr_t[k][:], in_=wkr[k * 128:(k + 1) * 128, :]), inc="WG", inck=16)
            wld3.append(vg)
        p.wait("vector", "WG", wld3[-1])
        for k in range(16):
            p.run("vector", lambda e, k=k: e.tensor_scalar(
                wkrr_t[k][:, 0:32], wkr_t[k][:, 32:64], -1.0, None, OP.mult))
            if k == 15:
                v_wkrr = p.run("vector", lambda e, k=k: e.tensor_copy(
                    wkrr_t[k][:, 32:64], wkr_t[k][:, 0:32]), inc="VE")
            else:
                p.run("vector", lambda e, k=k: e.tensor_copy(
                    wkrr_t[k][:, 32:64], wkr_t[k][:, 0:32]))
        p.wait("tensor", "VE", v_wkrr)
        p.wait("tensor", "CZ", max(v_cos, v_sin))
        g_prp = [0, 0]
        p.wait("tensor", "AC", p.cnt["AC"])
        for n4 in range(4):
            pp = n4 % 2
            p.wait("tensor", "VE", g_prp[pp])
            mms = []
            for k in range(16):
                mms.append((pr[2 * pp][0:64, :], wkr_t[k][:],
                            xf[k][:, n4 * 512:(n4 + 1) * 512], k == 0, k == 15))
            for k in range(16):
                mms.append((pr[2 * pp + 1][0:64, :], wkrr_t[k][:],
                            xf[k][:, n4 * 512:(n4 + 1) * 512], k == 0, k == 15))
            v_krmm = mm_seq(mms)
            p.wait("vector", "PE", v_krmm)
            p.run("vector", lambda e, n4=n4, pp=pp: e.tensor_tensor(
                ropeA[0:64, :], pr[2 * pp][0:64, :],
                s_cos[0:64, n4 * 512:(n4 + 1) * 512], OP.mult))
            p.run("vector", lambda e, n4=n4, pp=pp: e.tensor_tensor(
                ropeB[0:64, :], pr[2 * pp + 1][0:64, :],
                s_sin[0:64, n4 * 512:(n4 + 1) * 512], OP.mult))
            v_krp = p.run("vector", lambda e, n4=n4: e.tensor_tensor(
                krp[:, n4 * 512:(n4 + 1) * 512], ropeA[0:64, :],
                ropeB[0:64, :], OP.add), inc="VE")
            g_prp[pp] = v_krp
        v_krp_done = v_krp
        st_kr.close()
        v_pe_xf_done = p.cnt["PE"]
        st_xf.close()
        do_stats(999)
        tern_scalars("up", t_psc, t_pbc)
        tern_scalars("dn", t_psc, t_pbc)

        st_qk = ExitStack()
        qh = [st_qk.enter_context(nc.sbuf_tensor(f"qh{h}", [128, S], BF16, side="right"))
              for h in range(H)]
        st_wqc = ExitStack()
        wqc_t = [st_wqc.enter_context(nc.sbuf_tensor(f"wqct{k}", [128, 1024], BF16))
                 for k in range(12)]
        p.wait("gpsimd", "PE", v_pe_xf_done)
        p.wait("gpsimd", "VE", v_krp_done)
        wld4 = []
        for k in range(12):
            vg = p.run("gpsimd", lambda e, k=k: e.dma_start(
                out=wqc_t[k][:], in_=wqc[k * 128:(k + 1) * 128, :]), inc="WG", inck=16)
            wld4.append(vg)
        p.wait("tensor", "WG", wld4[-1])
        p.wait("tensor", "AC", p.cnt["AC"])
        p.wait("tensor", "VE", tern["rdy_dn"])
        for m in range(8):
            gg = m % 2
            p.wait("tensor", "AC", ev_g[gg])
            mms = []
            for k in range(12):
                for n4 in range(4):
                    mms.append((pj[4 * gg + n4][:],
                                wqc_t[k][:, m * 128:(m + 1) * 128],
                                cq[k][:, n4 * 512:(n4 + 1) * 512],
                                k == 0, k == 11))
            v_mm = mm_seq(mms)
            p.wait("scalar", "PE", v_mm)
            for n4 in range(4):
                p.run("scalar", lambda e, m=m, n4=n4, gg=gg: e.copy(
                    qh[2 * m][0:64, n4 * 512:(n4 + 1) * 512],
                    pj[4 * gg + n4][0:64, :]))
                ev_g[gg] = p.run("scalar", lambda e, m=m, n4=n4, gg=gg: e.copy(
                    qh[2 * m + 1][0:64, n4 * 512:(n4 + 1) * 512],
                    pj[4 * gg + n4][64:128, :]), inc="AC")
            do_quants(4)
        v_pe_qc_done = p.cnt["PE"]
        st_wqc.close()

        st_wqr = ExitStack()
        Eq = st_wqr.enter_context
        wqr_t = [Eq(nc.sbuf_tensor(f"wqrt{k}", [128, 512], BF16)) for k in range(12)]
        wqrr_t = [Eq(nc.sbuf_tensor(f"wqrrt{k}", [128, 512], BF16)) for k in range(12)]
        p.wait("tensor", "AC", p.cnt["AC"])
        g_prp = [0, 0]
        step = 0
        v_half_pe = v_pe_qc_done
        for hh in range(2):
            p.wait("gpsimd", "PE", v_half_pe)
            if hh:
                p.wait("gpsimd", "VE", v_wqrr)
            wld5 = []
            for k in range(12):
                vg = p.run("gpsimd", lambda e, k=k, hh=hh: e.dma_start(
                    out=wqr_t[k][:],
                    in_=wqr[k * 128:(k + 1) * 128, hh * 512:(hh + 1) * 512]),
                    inc="WG", inck=16)
                wld5.append(vg)
            p.wait("vector", "WG", wld5[-1])
            for k in range(12):
                wt = wqr_t[k][:].rearrange("p (h t c) -> p h t c", h=8, t=2, c=32)
                rt = wqrr_t[k][:].rearrange("p (h t c) -> p h t c", h=8, t=2, c=32)
                p.run("vector", lambda e, wt=wt, rt=rt: e.tensor_scalar(
                    rt[:, :, 0, :], wt[:, :, 1, :], -1.0, None, OP.mult))
                if k == 11:
                    v_wqrr = p.run("vector", lambda e, wt=wt, rt=rt: e.tensor_copy(
                        rt[:, :, 1, :], wt[:, :, 0, :]), inc="VE")
                else:
                    p.run("vector", lambda e, wt=wt, rt=rt: e.tensor_copy(
                        rt[:, :, 1, :], wt[:, :, 0, :]))
            p.wait("tensor", "VE", v_wqrr)
            for m in range(4 * hh, 4 * hh + 4):
                mh = m - 4 * hh
                for n4 in range(4):
                    pp = step % 2
                    step += 1
                    p.wait("tensor", "VE", g_prp[pp])
                    mms = []
                    for k in range(12):
                        mms.append((pr[2 * pp][:], wqr_t[k][:, mh * 128:(mh + 1) * 128],
                                    cq[k][:, n4 * 512:(n4 + 1) * 512], k == 0, k == 11))
                    for k in range(12):
                        mms.append((pr[2 * pp + 1][:],
                                    wqrr_t[k][:, mh * 128:(mh + 1) * 128],
                                    cq[k][:, n4 * 512:(n4 + 1) * 512], k == 0, k == 11))
                    v_qr = mm_seq(mms)
                    p.wait("vector", "PE", v_qr)
                    p.run("vector", lambda e, n4=n4, pp=pp: e.tensor_tensor(
                        ropeA[:], pr[2 * pp][:],
                        s_cos[:, n4 * 512:(n4 + 1) * 512], OP.mult))
                    p.run("vector", lambda e, n4=n4, pp=pp: e.tensor_tensor(
                        ropeB[:], pr[2 * pp + 1][:],
                        s_sin[:, n4 * 512:(n4 + 1) * 512], OP.mult))
                    p.run("vector", lambda e, m=m, n4=n4: e.tensor_tensor(
                        qh[2 * m][64:128, n4 * 512:(n4 + 1) * 512],
                        ropeA[0:64, :], ropeB[0:64, :], OP.add))
                    v_qrope = p.run("vector", lambda e, m=m, n4=n4: e.tensor_tensor(
                        qh[2 * m + 1][64:128, n4 * 512:(n4 + 1) * 512],
                        ropeA[64:128, :], ropeB[64:128, :], OP.add), inc="VE")
                    g_prp[pp] = v_qrope
                    do_quants(1)
            v_half_pe = v_qr
        st_wqr.close()
        st_rope.close()
        v_pe_cq_done = p.cnt["PE"]
        st_cq.close()

        st_kh = ExitStack()
        kh = [st_kh.enter_context(nc.sbuf_tensor(f"kh{h}", [128, S], BF16, side="right"))
              for h in range(H)]
        st_wkc = ExitStack()
        wkc_t = [st_wkc.enter_context(nc.sbuf_tensor(f"wkct{k}", [128, 1024], BF16))
                 for k in range(4)]
        p.wait("gpsimd", "PE", v_pe_cq_done)
        p.wait("gpsimd", "VE", v_qrope)
        wld6 = []
        for k in range(4):
            vg = p.run("gpsimd", lambda e, k=k: e.dma_start(
                out=wkc_t[k][:], in_=wkc[k * 128:(k + 1) * 128, :]), inc="WG", inck=16)
            wld6.append(vg)
        p.wait("tensor", "WG", wld6[-1])
        p.wait("tensor", "VE", v_qrope)
        for m in range(8):
            gg = m % 2
            p.wait("tensor", "AC", ev_g[gg])
            mms = []
            for k in range(4):
                for n4 in range(4):
                    mms.append((pj[4 * gg + n4][:],
                                wkc_t[k][:, m * 128:(m + 1) * 128],
                                ckv[k][:, n4 * 512:(n4 + 1) * 512], k == 0, k == 3))
            v_mm = mm_seq(mms)
            p.wait("scalar", "PE", v_mm)
            for n4 in range(4):
                p.run("scalar", lambda e, m=m, n4=n4, gg=gg: e.copy(
                    kh[2 * m][0:64, n4 * 512:(n4 + 1) * 512],
                    pj[4 * gg + n4][0:64, :]))
                ev_g[gg] = p.run("scalar", lambda e, m=m, n4=n4, gg=gg: e.copy(
                    kh[2 * m + 1][0:64, n4 * 512:(n4 + 1) * 512],
                    pj[4 * gg + n4][64:128, :]), inc="AC")
            do_quants(2)
        p.wait("vector", "VE", v_krp_done)
        for h in range(H):
            if h == H - 1:
                v_khcp = p.run("vector", lambda e, h=h: e.tensor_copy(
                    kh[h][64:128, :], krp[:]), inc="VE")
            else:
                p.run("vector", lambda e, h=h: e.tensor_copy(
                    kh[h][64:128, :], krp[:]))
        st_wkc.close()

        st_wv = ExitStack()
        Ev = st_wv.enter_context
        wv_t = [Ev(nc.sbuf_tensor(f"wvt{k}", [128, D], BF16)) for k in range(4)]
        vout = [Ev(nc.sbuf_tensor(f"vout{i}", [128, D], BF16)) for i in range(2)]
        p.wait("gpsimd", "PE", p.cnt["PE"])
        wld7 = []
        for k in range(4):
            vg = p.run("gpsimd", lambda e, k=k: e.dma_start(
                out=wv_t[k][:], in_=wv[k * 128:(k + 1) * 128, :]), inc="WG", inck=16)
            wld7.append(vg)
        p.wait("tensor", "WG", wld7[-1])
        VSS = ("VSA", "VSB")
        VTS = ("VTA", "VTB")
        for t in range(NT):
            ob = t % 2
            gg = t % 2
            p.wait("tensor", "AC", ev_g[gg])
            mms = []
            for k in range(4):
                for n4 in range(4):
                    mms.append((pj[4 * gg + n4][:],
                                ckv[k][:, t * 128:(t + 1) * 128],
                                wv_t[k][:, n4 * 512:(n4 + 1) * 512], k == 0, k == 3))
            v_mm = mm_seq(mms)
            p.wait("scalar", "PE", v_mm)
            p.wait("scalar", VSS[ob], 16 * (t // 2))
            for n4 in range(4):
                ev_g[gg] = p.run("scalar", lambda e, ob=ob, n4=n4, gg=gg: e.copy(
                    vout[ob][:, n4 * 512:(n4 + 1) * 512], pj[4 * gg + n4][:]),
                    inc="AC")
            p.wait("gpsimd", "AC", ev_g[gg])
            p.run("gpsimd", lambda e, ob=ob, t=t: e.dma_start(
                out=v_s[t * 128:(t + 1) * 128, :], in_=vout[ob][:]),
                inc=VSS[ob], inck=16)
            do_quants(1)
        st_wv.close()
        st_krp.close()
        st_ckv.close()
        st_p2ps.close()
        barrier(exclude=("VSA", "VSB", "SG", "SH", "SI", "SJ", "SK", "SL", "TS"),
                overrides={"VE": v_khcp})
        v_p2_pe = p.cnt["PE"]
        v_p2_ve = v_khcp
        v_p2_ac = p.cnt["AC"]

        st3 = ExitStack()
        E3 = st3.enter_context
        vth = [E3(nc.sbuf_tensor(f"vth{i}", [128, S], BF16)) for i in range(4)]
        ae = [E3(nc.sbuf_tensor(f"ae{i}", [128, 512], BF16)) for i in range(3)]
        srow = E3(nc.sbuf_tensor("srow", [1, 512], F32))
        srr = [E3(nc.sbuf_tensor("srr0", [1, 512], F32))]
        rbc = [E3(nc.sbuf_tensor("rbc0", [128, 512], F32))]
        cxo = [E3(nc.sbuf_tensor(f"cxo{i}", [128, 512], BF16)) for i in range(2)]
        psc = [E3(nc.psum_tensor(f"psc{i}", [128, 512], F32)) for i in range(3)]
        pctx = [E3(nc.psum_tensor(f"pctx{i}", [128, 512], F32)) for i in range(2)]
        psr = [E3(nc.psum_tensor(f"psr{i}", [1, 512], F32)) for i in range(2)]
        pbcp = E3(nc.psum_tensor("pbcp", [128, 512], F32))

        VTS4 = ("VTA", "VTB", "VTC", "VTD")
        p.wait("sync", "VE", v_p2_ve)
        p.wait("sync", "AC", v_p2_ac)
        p.wait("sync", "VSA", 16 * 8)
        p.wait("sync", "VSB", 16 * 8)

        def vth_issue(h, pe_guard):
            sl = VTS4[h % 4]
            p.wait("sync", "PE", pe_guard)
            for t in range(NT):
                p.wait("sync", sl, p.cnt[sl])
                p.run("sync", lambda e, h=h, t=t: e.dma_start(
                    out=vth[h % 4][:, t * 128:(t + 1) * 128],
                    in_=v_s[t * 128:(t + 1) * 128, h * 128:(h + 1) * 128]),
                    inc=sl, inck=16)

        for hh in range(4):
            vth_issue(hh, 0)

        g_psc = [0, 0, 0]
        g_ae = [0, 0, 0]
        g_pctx = [0, 0]
        g_psr = [0, 0]
        g_pbc = 0
        g_cxo = [0, 0]
        g_srr = 0
        p.wait("tensor", "VE", v_p2_ve)
        p.wait("tensor", "AC", v_p2_ac)
        u = 0
        for h in range(H):
            hb4 = h % 4
            p.wait("tensor", VTS4[hb4], 256 * (h // 4 + 1))
            for s4 in range(4):
                Q0 = 512 * s4
                nblocks = 4 * (s4 + 1)
                hb = u % 2
                u += 1
                score_done = {}

                def emit_score(j, h=h, hb=hb, Q0=Q0, s4=s4):
                    r = 128 * (j - 4 * s4) if j >= 4 * s4 else 0
                    sb = j % 3
                    p.wait("tensor", "AC", g_psc[sb])
                    if j == 0:
                        p.wait("tensor", "VE", g_pctx[hb])
                        p.wait("tensor", "AC", g_psr[hb])
                    score_done[j] = p.run(
                        "tensor", lambda e, h=h, j=j, sb=sb, r=r, Q0=Q0:
                        e.matmul(psc[sb][:, r:512],
                                 kh[h][:, j * 128:(j + 1) * 128],
                                 qh[h][:, Q0 + r:Q0 + 512],
                                 start=True, stop=True), inc="PE")

                emit_score(0)
                if nblocks > 1:
                    emit_score(1)
                for j in range(nblocks):
                    r = 128 * (j - 4 * s4) if j >= 4 * s4 else 0
                    sb = j % 3
                    ab = j % 3
                    p.wait("scalar", "PE", max(score_done[j], g_ae[ab]))
                    v_exp = p.run("scalar", lambda e, sb=sb, r=r, ab=ab:
                                  e.activation(ae[ab][:, r:512], psc[sb][:, r:512],
                                               AF.Exp, bias=0.0,
                                               scale=s_isq[:]), inc="AC")
                    g_psc[sb] = v_exp
                    if j >= 4 * s4:
                        p.wait("vector", "AC", v_exp)
                        v_msk = p.run("vector", lambda e, ab=ab, r=r:
                                      e.tensor_tensor(ae[ab][:, r:r + 128],
                                                      ae[ab][:, r:r + 128],
                                                      s_trim[:], OP.mult), inc="VE")
                    else:
                        v_msk = None
                    if j + 2 < nblocks:
                        emit_score(j + 2)
                    if v_msk is not None:
                        p.wait("tensor", "VE", v_msk)
                    else:
                        p.wait("tensor", "AC", v_exp)
                    st_, sp_ = (j == 0), (j == nblocks - 1)
                    p.run("tensor", lambda e, hb4=hb4, j=j, hb=hb, r=r, ab=ab,
                          st_=st_, sp_=sp_:
                          e.matmul(pctx[hb][:, r:512],
                                   vth[hb4][:, j * 128:(j + 1) * 128],
                                   ae[ab][:, r:512], start=st_, stop=sp_))
                    v_bmm = p.run("tensor", lambda e, hb=hb, r=r, ab=ab, st_=st_,
                                  sp_=sp_: e.matmul(psr[hb][:, r:512], s_ones_bf[:],
                                                    ae[ab][:, r:512],
                                                    start=st_, stop=sp_), inc="PE")
                    g_ae[ab] = v_bmm
                p.wait("scalar", "PE", max(v_bmm, g_srr))
                p.run("scalar", lambda e, hb=hb: e.activation(
                    srow[:], psr[hb][:], AF.Ln, bias=0.0, scale=1.0))
                v_rcp = p.run("scalar", lambda e: e.activation(
                    srr[0][:], srow[:], AF.Exp, bias=0.0, scale=-1.0), inc="AC")
                g_psr[hb] = v_rcp
                p.wait("tensor", "AC", v_rcp)
                p.wait("tensor", "VE", g_pbc)
                v_bc = p.run("tensor", lambda e: e.matmul(
                    pbcp[:], s_onesr_f[:], srr[0][:], start=True, stop=True),
                    inc="PE")
                g_srr = v_bc
                p.wait("vector", "PE", v_bc)
                v_rbc = p.run("vector", lambda e: e.tensor_copy(
                    rbc[0][:], pbcp[:]), inc="VE")
                g_pbc = v_rbc
                p.wait("vector", "CS", p.cnt["CS"] if g_cxo[hb] else 0)
                v_cx = p.run("vector", lambda e, hb=hb: e.tensor_tensor(
                    cxo[hb][:], pctx[hb][:], rbc[0][:], OP.mult), inc="VE")
                g_pctx[hb] = v_cx
                p.wait("gpsimd", "VE", v_cx)
                v_gd = p.run("gpsimd", lambda e, h=h, s4=s4, hb=hb: e.dma_start(
                    out=ctx_s[h * 128:(h + 1) * 128, s4 * 512:(s4 + 1) * 512],
                    in_=cxo[hb][:]), inc="CS", inck=16)
                g_cxo[hb] = v_gd
                do_quants(1)
            if h + 4 < H:
                vth_issue(h + 4, p.cnt["PE"])
        v_ctx_gd = p.cnt["CS"]
        v_p3_pe = p.cnt["PE"]
        v_p3_ve = p.cnt["VE"]
        v_p3_ac = p.cnt["AC"]
        st3.close()
        st_kh.close()
        st_qk.close()
        barrier(exclude=("SG", "SH", "SI", "SJ", "SK", "SL", "TS"))

        st4 = ExitStack()
        E4 = st4.enter_context
        ctxc = [E4(nc.sbuf_tensor(f"ctxc{c}", [128, S], BF16)) for c in range(16)]
        woc = [E4(nc.sbuf_tensor(f"woc{c}", [128, D], BF16)) for c in range(16)]
        s_bo = E4(nc.sbuf_tensor("s_bo", [1, D], F32))
        xt4 = [E4(nc.sbuf_tensor(f"xt4_{i}", [128, D], F32)) for i in range(2)]
        x1t = [E4(nc.sbuf_tensor(f"x1t{i}", [128, D], F32)) for i in range(2)]
        pout = [E4(nc.psum_tensor(f"pout{i}", [128, 512], F32)) for i in range(8)]

        p.wait("sync", "CS", v_ctx_gd)
        p.wait("sync", "VE", v_p3_ve)
        p.wait("sync", "AC", v_p3_ac)
        ctx_ld = []
        for c in range(16):
            vls = p.run("sync", lambda e, c=c: e.dma_start(
                out=ctxc[c][:], in_=ctx_s[c * 128:(c + 1) * 128, :]), inc="CT", inck=16)
            ctx_ld.append(vls)
        v_bo_ld = p.run("sync", lambda e: e.dma_start(out=s_bo[:], in_=bo),
                        inc="LA", inck=16)
        p.wait("gpsimd", "PE", v_p3_pe)
        wo_ld = []
        for c in range(16):
            vg = p.run("gpsimd", lambda e, c=c: e.dma_start(
                out=woc[c][:], in_=wo[c * 128:(c + 1) * 128, :]), inc="WG", inck=16)
            wo_ld.append(vg)
        p.wait("tensor", "LA", v_bo_ld)
        p.wait("tensor", "PE", v_p3_pe)
        p.wait("tensor", "VE", v_p3_ve)
        p.wait("tensor", "AC", v_p3_ac)

        g_pout = [0, 0]
        g_x1 = [0, 0]
        g_xt4 = [0, 0]
        for t in range(NT):
            pg = t % 2
            p.wait("sync", "AC", g_xt4[pg])
            v_xl = slot_load(pg, xt4[pg][:], x_in[t * 128:(t + 1) * 128, :])
            p.wait("tensor", "WG", wo_ld[15])
            p.wait("tensor", "CT", ctx_ld[15])
            p.wait("tensor", "VE", g_pout[pg])
            mms = []
            for dv in range(16):
                for o4 in range(4):
                    mms.append((pout[4 * pg + o4][:],
                                ctxc[dv][:, t * 128:(t + 1) * 128],
                                woc[dv][:, o4 * 512:(o4 + 1) * 512],
                                dv == 0, False))
            for o4 in range(4):
                mms.append((pout[4 * pg + o4][:], s_onesr_f[:],
                            s_bo[:, o4 * 512:(o4 + 1) * 512], False, True))
            v_wmm = mm_seq(mms)
            p.wait("vector", "PE", v_wmm)
            p.wait("vector", SLOTS[pg], v_xl)
            p.wait("vector", "X1S", p.cnt["X1S"] if g_x1[pg] else 0)
            for o4 in range(4):
                p.run("vector", lambda e, pg=pg, o4=o4: e.tensor_tensor(
                    x1t[pg][:, o4 * 512:(o4 + 1) * 512], pout[4 * pg + o4][:],
                    xt4[pg][:, o4 * 512:(o4 + 1) * 512], OP.add))
            v_x1 = p.run("vector", lambda e, pg=pg, t=t: e.reduce_sum(
                pan2["sum"][:, t:t + 1], x1t[pg][:], AX), inc="VE")
            g_pout[pg] = v_x1
            p.wait("scalar", "VE", v_x1)
            v_sq2 = p.run("scalar", lambda e, pg=pg, t=t: e.activation(
                xt4[pg][:], x1t[pg][:], AF.Square, bias=0.0, scale=1.0,
                accum_out=pan2["sq"][:, t:t + 1]), inc="AC")
            g_xt4[pg] = v_sq2
            p.wait("gpsimd", "AC", v_sq2)
            v_gd = p.run("gpsimd", lambda e, pg=pg, t=t: e.dma_start(
                out=x1_s[t * 128:(t + 1) * 128, :], in_=x1t[pg][:]), inc="X1S", inck=16)
            g_x1[pg] = v_gd
            do_quants(3)
        do_quants(999)
        v_x1_gd = p.cnt["X1S"]
        v_nmr2 = panel_math(pan2, p.cnt["VE"], p.cnt["AC"])
        v_p4_pe = p.cnt["PE"]
        v_p4_ve = p.cnt["VE"]
        v_p4_ac = p.cnt["AC"]
        st4.close()
        st_tern.close()
        barrier()

        def ffn_half(hf, v_p4_pe, v_p4_ve, v_p4_ac, v_x1_gd, v_nmr2):
            T0 = hf * 8
            R0 = hf * 1024
            st5a = ExitStack()
            E5 = st5a.enter_context
            x1l = [E5(nc.sbuf_tensor(f"x1l{i}_{hf}", [128, D], F32)) for i in range(2)]
            xln2 = [E5(nc.sbuf_tensor(f"xln2_{i}_{hf}", [128, D], F32)) for i in range(8)]
            tq = [E5(nc.sbuf_tensor(f"tq0_{hf}", [128, D], F32))] * 2
            tqk = [E5(nc.sbuf_tensor(f"tqk0_{hf}", [128, D], F32))] * 2
            tqd = E5(nc.sbuf_tensor(f"tqd_{hf}", [128, D], BF16))
            tqc = E5(nc.sbuf_tensor(f"tqc_{hf}", [128, D], BF16))
            tqi = [E5(nc.sbuf_tensor(f"tqi0_{hf}", [128, D], I32))] * 2
            xqb = [E5(nc.sbuf_tensor(f"xqb{i}_{hf}", [128, D], BF16)) for i in range(2)]

            p.wait("scalar", "VE", v_nmr2)
            p.wait("sync", "X1S", v_x1_gd)
            p.wait("sync", "PE", v_p4_pe)
            p.wait("sync", "VE", v_p4_ve)
            p.wait("sync", "AC", v_p4_ac)
            p.wait("sync", "ST", p.cnt["ST"])
            g_x1l = [0, 0]
            for t in range(8):
                gt = T0 + t
                b = t % 2
                p.wait("sync", "VE", g_x1l[b])
                v_xls = slot_load(b, x1l[b][:], x1_s[gt * 128:(gt + 1) * 128, :])
                p.wait("scalar", SLOTS[b], v_xls)
                v_n = p.run("scalar", lambda e, b=b, t=t, gt=gt: e.activation(
                    xln2[t][:], x1l[b][:], AF.Identity,
                    bias=pan2["nmr"][:, gt:gt + 1],
                    scale=pan2["rstd2"][:, gt:gt + 1]), inc="AC")
                p.wait("vector", "AC", v_n)
                v_mx = p.run("vector", lambda e, t=t: e.reduce_max(
                    maxpan1[:, t:t + 1], xln2[t][:], AX,
                    apply_absolute_value=True), inc="VE")
                g_x1l[b] = v_mx
            p.run("vector", lambda e: e.tensor_scalar(
                maxpan1[:], maxpan1[:], EPS, None, OP.max))
            p.run("vector", lambda e: e.reciprocal(panwk[:], maxpan1[:]))
            nr_recip(panwk[:], maxpan1[:], evacpan1[:])
            p.run("vector", lambda e: e.tensor_scalar(
                scalepan1[:], panwk[:], 127.0, None, OP.mult))
            p.run("vector", lambda e: e.tensor_scalar(
                evacpan1[:], maxpan1[:], s_r127[:], None, OP.mult))
            v_pan1 = p.run("vector", lambda e: e.tensor_scalar(
                evacpan1[:], evacpan1[:], s_sw["up"][:], None, OP.mult), inc="VE")

            g_xqb = [0, 0]
            g_tq = [0, 0]
            p.wait("scalar", "VE", max(v_toff, v_pan1))
            v_q = 0
            for t in range(8):
                gt = T0 + t
                b = t % 2
                p.wait("scalar", "VE", v_q)
                p.run("scalar", lambda e, t=t, b=b: e.activation(
                    tq[b][:], xln2[t][:], AF.Identity,
                    bias=s_toff[:, 0:1], scale=scalepan1[:, t:t + 1]))
                p.run("scalar", lambda e, b=b: e.copy(tqi[b][:], tq[b][:]))
                v_t1 = p.run("scalar", lambda e, b=b: e.copy(tqk[b][:], tqi[b][:]),
                             inc="AC")
                p.wait("vector", "AC", v_t1)
                p.wait("vector", "QS", p.cnt["QS"] if g_xqb[b] else 0)
                p.run("vector", lambda e, b=b: e.tensor_sub(tqd[:], tq[b][:], tqk[b][:]))
                p.run("vector", lambda e: e.tensor_scalar(
                    tqd[:], tqd[:], s_sdc[:], None, OP.add))
                p.run("vector", lambda e: e.tensor_scalar(
                    tqc[:], tqd[:], 1.0, None, OP.is_gt))
                p.run("vector", lambda e: e.scalar_tensor_tensor(
                    tqc[:], tqd[:], 2.0, tqc[:], OP.is_gt, OP.add))
                p.run("vector", lambda e: e.scalar_tensor_tensor(
                    tqc[:], tqd[:], 3.0, tqc[:], OP.is_gt, OP.add))
                p.run("vector", lambda e: e.scalar_tensor_tensor(
                    tqc[:], tqd[:], 0.0, tqc[:], OP.is_ge, OP.add))
                p.run("vector", lambda e: e.scalar_tensor_tensor(
                    tqc[:], tqd[:], -1.0, tqc[:], OP.is_ge, OP.add))
                p.run("vector", lambda e: e.scalar_tensor_tensor(
                    tqc[:], tqd[:], -2.0, tqc[:], OP.is_ge, OP.add))
                v_q = p.run("vector", lambda e, b=b: e.scalar_tensor_tensor(
                    xqb[b][:], tqc[:], -259.0, tqk[b][:], OP.add, OP.add), inc="VE")
                g_tq[b] = v_q
                p.wait("gpsimd", "VE", v_q)
                g_xqb[b] = p.run("gpsimd", lambda e, b=b, gt=gt: e.dma_start(
                    out=xq_s[gt * 128:(gt + 1) * 128, :], in_=xqb[b][:]),
                    inc="QS", inck=16)
                if t == 3:
                    v_xq_mid = p.cnt["QS"]
            v_xq_gd = p.cnt["QS"]
            v_a_ve = p.cnt["VE"]
            v_a_ac = p.cnt["AC"]
            st5a.close()

            st5b = ExitStack()
            E5 = st5b.enter_context
            ht = [E5(nc.sbuf_tensor(f"ht{t}_{hf}", [128, FI], F32)) for t in range(8)]
            trelu = E5(nc.sbuf_tensor(f"trelu_{hf}", [128, 512], F32))
            pup = [E5(nc.psum_tensor(f"pup{i}_{hf}", [128, 512], F32)) for i in range(4)]
            st5bio = ExitStack()
            E5i = st5bio.enter_context
            xqf = [E5i(nc.sbuf_tensor(f"xqf{c}_{hf}", [128, 1024], BF16)) for c in range(16)]
            rup = [[E5i(nc.sbuf_tensor(f"rup{g}_{k}_{hf}", [128, 512], BF16))
                    for k in range(16)] for g in range(2)]

            def rup_load(ip):
                g = ip % 2
                sem = ("RWA", "RWB")[g]
                last = None
                for hc in range(16):
                    last = p.run("sync", lambda e, g=g, hc=hc, ip=ip: e.dma_start(
                        out=rup[g][hc][:],
                        in_=tup_s[ip * 512:(ip + 1) * 512, hc * 128:(hc + 1) * 128],
                        transpose=True), inc=sem, inck=16)
                return last

            rup_rdy = [0, 0]
            rup_pe_hist = [0, 0]
            rup_rdy[0] = rup_load(0)
            rup_rdy[1] = rup_load(1)
            xqf_ld = []
            p.wait("sync", "QS", v_xq_mid)
            for c in range(16):
                vls = p.run("sync", lambda e, c=c, R0=R0: e.dma_start(
                    out=xqf[c][:, 0:512], in_=xq_s[R0:R0 + 512, c * 128:(c + 1) * 128],
                    transpose=True), inc="XQF", inck=16)
                xqf_ld.append(vls)
            p.wait("sync", "QS", v_xq_gd)
            p.wait("sync", "VE", v_a_ve)
            p.wait("sync", "AC", v_a_ac)
            for c in range(16):
                vls = p.run("sync", lambda e, c=c, R0=R0: e.dma_start(
                    out=xqf[c][:, 512:1024],
                    in_=xq_s[R0 + 512:R0 + 1024, c * 128:(c + 1) * 128],
                    transpose=True), inc="XQF", inck=16)
                xqf_ld.append(vls)

            g_pup = [0, 0, 0, 0]
            p.wait("tensor", "PE", v_p4_pe)
            p.wait("tensor", "VE", v_p4_ve)
            p.wait("tensor", "AC", v_p4_ac)
            for ip in range(8):
                g = ip % 2
                if 2 <= ip + 1 < 8:
                    p.wait("sync", "PE", rup_pe_hist[(ip + 1) % 2])
                    rup_rdy[(ip + 1) % 2] = rup_load(ip + 1)
                for t in range(8):
                    pb = t % 4
                    p.wait("tensor", ("RWA", "RWB")[g], rup_rdy[g])
                    p.wait("tensor", "XQF", xqf_ld[-1])
                    p.wait("tensor", "AC", g_pup[pb])
                    mms = []
                    for hc in range(16):
                        mms.append((pup[pb][:], xqf[hc][:, t * 128:(t + 1) * 128],
                                    rup[g][hc][:], hc == 0, hc == 15))
                    v_up = mm_seq(mms)
                    p.wait("scalar", "PE", v_up)
                    p.run("scalar", lambda e, pb=pb, t=t: e.activation(
                        trelu[:], pup[pb][:], AF.Relu, bias=0.0,
                        scale=evacpan1[:, t:t + 1]))
                    v_hsq = p.run("scalar", lambda e, t=t, ip=ip: e.activation(
                        ht[t][:, ip * 512:(ip + 1) * 512], trelu[:], AF.Square,
                        bias=0.0, scale=1.0), inc="AC")
                    g_pup[pb] = v_hsq
                    p.wait("vector", "AC", v_hsq)
                    v_hm = p.run("vector", lambda e, t=t, ip=ip: e.reduce_max(
                        hmaxpan[:, t * 8 + ip:t * 8 + ip + 1],
                        ht[t][:, ip * 512:(ip + 1) * 512], AX), inc="VE")
                rup_pe_hist[g] = p.cnt["PE"]

            for t in range(8):
                p.run("vector", lambda e, t=t: e.reduce_max(
                    max2pan[:, t:t + 1], hmaxpan[:, t * 8:(t + 1) * 8], AX))
            p.run("vector", lambda e: e.tensor_scalar(
                max2pan[:], max2pan[:], EPS, None, OP.max))
            p.run("vector", lambda e: e.reciprocal(panwk[:], max2pan[:]))
            nr_recip(panwk[:], max2pan[:], evacpan2[:])
            p.run("vector", lambda e: e.tensor_scalar(
                scalepan2[:], panwk[:], 127.0, None, OP.mult))
            p.run("vector", lambda e: e.tensor_scalar(
                evacpan2[:], max2pan[:], s_r127[:], None, OP.mult))
            v_pan2 = p.run("vector", lambda e: e.tensor_scalar(
                evacpan2[:], evacpan2[:], s_sw["dn"][:], None, OP.mult), inc="VE")

            st5bio.close()
            FH = FI // 2
            tq2 = [E5(nc.sbuf_tensor(f"tq2{i}_{hf}", [128, FH], F32)) for i in range(2)]
            tq2k = [E5(nc.sbuf_tensor(f"tq2k{i}_{hf}", [128, FH], F32)) for i in range(2)]
            tq2d = E5(nc.sbuf_tensor(f"tq2d_{hf}", [128, FH], BF16))
            tq2c = E5(nc.sbuf_tensor(f"tq2c_{hf}", [128, FH], BF16))
            tq2i = [E5(nc.sbuf_tensor(f"tq2i{i}_{hf}", [128, FH], I32)) for i in range(2)]
            hqb = [E5(nc.sbuf_tensor(f"hqb{i}_{hf}", [128, FH], BF16)) for i in range(2)]
            g_hqb = [0, 0]
            g_tq2 = [0, 0]
            p.wait("scalar", "VE", v_pan2)
            v_q2 = 0
            for idx in range(16):
                t, q2 = idx // 2, idx % 2
                gt = T0 + t
                b = idx % 2
                p.wait("scalar", "VE", g_tq2[b])
                p.run("scalar", lambda e, t=t, q2=q2, b=b: e.activation(
                    tq2[b][:], ht[t][:, q2 * FH:(q2 + 1) * FH], AF.Identity,
                    bias=s_toff[:, 0:1], scale=scalepan2[:, t:t + 1]))
                p.run("scalar", lambda e, b=b: e.copy(tq2i[b][:], tq2[b][:]))
                v_t1 = p.run("scalar", lambda e, b=b: e.copy(tq2k[b][:], tq2i[b][:]),
                             inc="AC")
                p.wait("vector", "AC", v_t1)
                p.wait("vector", "HS", p.cnt["HS"] if g_hqb[b] else 0)
                p.run("vector", lambda e, b=b: e.tensor_sub(
                    tq2d[:], tq2[b][:], tq2k[b][:]))
                p.run("vector", lambda e: e.tensor_scalar(
                    tq2d[:], tq2d[:], s_sdc[:], None, OP.add))
                p.run("vector", lambda e: e.tensor_scalar(
                    tq2c[:], tq2d[:], 1.0, None, OP.is_gt))
                p.run("vector", lambda e: e.scalar_tensor_tensor(
                    tq2c[:], tq2d[:], 2.0, tq2c[:], OP.is_gt, OP.add))
                p.run("vector", lambda e: e.scalar_tensor_tensor(
                    tq2c[:], tq2d[:], 3.0, tq2c[:], OP.is_gt, OP.add))
                p.run("vector", lambda e: e.scalar_tensor_tensor(
                    tq2c[:], tq2d[:], 0.0, tq2c[:], OP.is_ge, OP.add))
                p.run("vector", lambda e: e.scalar_tensor_tensor(
                    tq2c[:], tq2d[:], -1.0, tq2c[:], OP.is_ge, OP.add))
                p.run("vector", lambda e: e.scalar_tensor_tensor(
                    tq2c[:], tq2d[:], -2.0, tq2c[:], OP.is_ge, OP.add))
                v_q2 = p.run("vector", lambda e, b=b: e.scalar_tensor_tensor(
                    hqb[b][:], tq2c[:], -259.0, tq2k[b][:], OP.add, OP.add), inc="VE")
                g_tq2[b] = v_q2
                p.wait("gpsimd", "VE", v_q2)
                g_hqb[b] = p.run("gpsimd", lambda e, b=b, gt=gt, q2=q2: e.dma_start(
                    out=hq_s[gt * 128:(gt + 1) * 128, q2 * FH:(q2 + 1) * FH],
                    in_=hqb[b][:]), inc="HS", inck=16)
                if idx == 7:
                    v_hq_mid = p.cnt["HS"]
                if idx == 11:
                    v_hq_t5 = p.cnt["HS"]
            v_hq_gd = p.cnt["HS"]
            v_b_ve = p.cnt["VE"]
            v_b_ac = p.cnt["AC"]
            v_b_pe = p.cnt["PE"]
            st5b.close()

            st5c = ExitStack()
            E5 = st5c.enter_context
            hqf = [E5(nc.sbuf_tensor(f"hqf{c}_{hf}", [128, 1024], BF16)) for c in range(32)]
            rdn = [[E5(nc.sbuf_tensor(f"rdn{g}_{k}_{hf}", [128, 512], BF16))
                    for k in range(32)] for g in range(2)]
            x1c = [E5(nc.sbuf_tensor(f"x1c{i}_{hf}", [128, 512], F32)) for i in range(2)]
            obuf = [E5(nc.sbuf_tensor(f"obuf{i}_{hf}", [128, 512], F32)) for i in range(2)]
            tdo = E5(nc.sbuf_tensor(f"tdo_{hf}", [128, 512], F32))
            pdn = [E5(nc.psum_tensor(f"pdn{i}_{hf}", [128, 512], F32)) for i in range(4)]

            def rdn_load(op4):
                g = op4 % 2
                sem = ("RWA", "RWB")[g]
                last = None
                for ic in range(32):
                    last = p.run("sync", lambda e, g=g, ic=ic, op4=op4: e.dma_start(
                        out=rdn[g][ic][:],
                        in_=tdn_s[op4 * 512:(op4 + 1) * 512, ic * 128:(ic + 1) * 128],
                        transpose=True), inc=sem, inck=16)
                return last

            hqf_ld = []
            p.wait("sync", "HS", v_hq_mid)
            for c in range(32):
                vls = p.run("sync", lambda e, c=c, R0=R0: e.dma_start(
                    out=hqf[c][:, 0:512], in_=hq_s[R0:R0 + 512, c * 128:(c + 1) * 128],
                    transpose=True), inc="XQF", inck=16)
                hqf_ld.append(vls)
            p.wait("sync", "HS", v_hq_t5)
            rdn_rdy = [0, 0]
            rdn_pe_hist = [v_b_pe, v_b_pe]
            rdn_rdy[0] = rdn_load(0)
            p.wait("sync", "HS", v_hq_gd)
            p.wait("sync", "VE", v_b_ve)
            p.wait("sync", "AC", v_b_ac)
            for c in range(32):
                vls = p.run("sync", lambda e, c=c, R0=R0: e.dma_start(
                    out=hqf[c][:, 512:1024],
                    in_=hq_s[R0 + 512:R0 + 1024, c * 128:(c + 1) * 128],
                    transpose=True), inc="XQF", inck=16)
                hqf_ld.append(vls)

            g_pdn = [0, 0, 0, 0]
            g_ob = [0, 0]
            g_x1c = [0, 0]
            g_tdo = 0
            p.wait("tensor", "AC", v_b_ac)
            for op4 in range(4):
                g = op4 % 2
                if op4 + 1 < 4:
                    p.wait("sync", "PE", rdn_pe_hist[(op4 + 1) % 2])
                    rdn_rdy[(op4 + 1) % 2] = rdn_load(op4 + 1)
                for t in range(8):
                    gt = T0 + t
                    b = t % 2
                    pb = t % 4
                    p.wait("sync", "VE", g_x1c[b])
                    v_x1r = slot_load(b, x1c[b][:],
                                      x1_s[gt * 128:(gt + 1) * 128,
                                           op4 * 512:(op4 + 1) * 512])
                    p.wait("tensor", ("RWA", "RWB")[g], rdn_rdy[g])
                    p.wait("tensor", "XQF", hqf_ld[-1])
                    p.wait("tensor", "AC", g_pdn[pb])
                    mms = []
                    for ic in range(32):
                        mms.append((pdn[pb][:], hqf[ic][:, t * 128:(t + 1) * 128],
                                    rdn[g][ic][:], ic == 0, ic == 31))
                    v_dn = mm_seq(mms)
                    p.wait("scalar", "PE", v_dn)
                    p.wait("scalar", "VE", g_tdo)
                    v_ev = p.run("scalar", lambda e, pb=pb, t=t: e.activation(
                        tdo[:], pdn[pb][:], AF.Copy, bias=0.0,
                        scale=evacpan2[:, t:t + 1]), inc="AC")
                    g_pdn[pb] = v_ev
                    p.wait("vector", "AC", v_ev)
                    p.wait("vector", SLOTS[b], v_x1r)
                    p.wait("vector", "ST", p.cnt["ST"] if g_ob[b] else 0)
                    v_o = p.run("vector", lambda e, b=b: e.tensor_tensor(
                        obuf[b][:], tdo[:], x1c[b][:], OP.add), inc="VE")
                    g_x1c[b] = v_o
                    g_tdo = v_o
                    p.wait("gpsimd", "VE", v_o)
                    v_st = p.run("gpsimd", lambda e, b=b, gt=gt, op4=op4: e.dma_start(
                        out=out_t[gt * 128:(gt + 1) * 128,
                                  op4 * 512:(op4 + 1) * 512], in_=obuf[b][:]),
                        inc="ST", inck=16)
                    g_ob[b] = v_st
                rdn_pe_hist[g] = p.cnt["PE"]
            v_dn_ve = p.cnt["VE"]
            v_dn_ac = p.cnt["AC"]
            v_dn_pe = p.cnt["PE"]
            st5c.close()
            return v_dn_pe, v_dn_ve, v_dn_ac

        for hf in range(2):
            v_p4_pe, v_p4_ve, v_p4_ac = ffn_half(
                hf, v_p4_pe, v_p4_ve, v_p4_ac, v_x1_gd, v_nmr2)
            v_x1_gd = 0

        p.wait("sync", "ST", p.cnt["ST"])

        for name in p.cnt:
            p.handles[name] = E(nc.semaphore(name))
        block = E(nc.Block())

        for ename, attr in (("sync", block.sync), ("scalar", block.scalar),
                            ("vector", block.vector), ("tensor", block.tensor),
                            ("gpsimd", block.gpsimd)):
            def section(eng, ename=ename):
                for f in p.ops[ename]:
                    f(eng)
            attr(section)

    return nc


def _host_constants():
    import jax
    import jax.numpy as jnp
    ROPE_BASE = 10000.0
    with jax.default_device(jax.local_devices(backend="cpu")[0]):
        inv_freq = 1.0 / ROPE_BASE ** (jnp.arange(0, RD, 2, dtype=jnp.float32) / RD)
        t = jnp.arange(S, dtype=jnp.float32)
        freqs = jnp.outer(t, inv_freq)
        emb = jnp.concatenate((freqs, freqs), axis=-1)
        cos = np.asarray(jnp.cos(emb).astype(jnp.float32))
        sin = np.asarray(jnp.sin(emb).astype(jnp.float32))
    cosd = np.zeros((128, S), np.float32)
    sind = np.zeros((128, S), np.float32)
    cosd[0:64] = cos.T
    cosd[64:128] = cos.T
    sind[0:64] = sin.T
    sind[64:128] = sin.T
    tm = np.triu(np.ones((128, 128), np.float32)).astype(ml_dtypes.bfloat16)
    return {
        "cosd": cosd,
        "sind": sind,
        "trimask": tm,
        "onescol_bf": np.ones((128, 1), ml_dtypes.bfloat16),
        "onesrow_f": np.ones((1, 128), np.float32),
        "onescol_f": np.ones((128, 1), np.float32),
        "invN_row": np.full((1, 128), 1.0 / (FI * D), np.float32),
        "halves": np.full((128, 512), 1.5, np.float32),
    }


_CACHE = {}


def kernel(L0_attn_x, L0_attn_W_kv_down, L0_attn_W_q_down, L0_attn_W_kc,
           L0_attn_W_v, L0_attn_W_qc, L0_attn_W_kr, L0_attn_W_qr, L0_attn_W_o,
           L0_attn_b_o, L0_ffn_w_up, L0_ffn_w_down):
    if "nc" not in _CACHE:
        _CACHE["nc"] = build_nc()
    nc = _CACHE["nc"]
    x = np.ascontiguousarray(np.asarray(L0_attn_x, np.float32))
    common = dict(
        wkvd=np.ascontiguousarray(np.asarray(L0_attn_W_kv_down, np.float32)),
        wqd=np.ascontiguousarray(np.asarray(L0_attn_W_q_down, np.float32)),
        wkc=np.ascontiguousarray(np.asarray(L0_attn_W_kc, np.float32)),
        wv=np.ascontiguousarray(np.asarray(L0_attn_W_v, np.float32)),
        wqc=np.ascontiguousarray(np.asarray(L0_attn_W_qc, np.float32)),
        wkr=np.ascontiguousarray(np.asarray(L0_attn_W_kr, np.float32)),
        wqr=np.ascontiguousarray(np.asarray(L0_attn_W_qr, np.float32)),
        wo=np.ascontiguousarray(np.asarray(L0_attn_W_o, np.float32)),
        bo=np.ascontiguousarray(np.asarray(L0_attn_b_o, np.float32).reshape(1, D)),
        wup=np.ascontiguousarray(np.asarray(L0_ffn_w_up, np.float32)),
        wdn=np.ascontiguousarray(np.asarray(L0_ffn_w_down, np.float32)),
        **_host_constants(),
    )
    in_maps = []
    for b in range(8):
        m = dict(common)
        m["x_in"] = np.ascontiguousarray(x[b])
        in_maps.append(m)
    res = run_bass_kernel_spmd(nc, in_maps, core_ids=list(range(8)))
    return np.stack([np.asarray(res.results[b]["out"]) for b in range(8)], axis=0)


# version 18
# speedup vs baseline: 1.0421x; 1.0224x over previous
import numpy as np
import ml_dtypes
from contextlib import ExitStack

import concourse.bass as bass
import concourse.mybir as mybir
from concourse.alu_op_type import AluOpType as OP
from concourse.bass_utils import run_bass_kernel_spmd

F32 = mybir.dt.float32
BF16 = mybir.dt.bfloat16
I32 = mybir.dt.int32
AX = mybir.AxisListType.X
AF = mybir.ActivationFunctionType

S = 2048
D = 2048
H = 16
RD = 64
KVL = 512
QL = 1536
FI = 4096
NT = 16
INV_SQRT_HD = float(1.0 / np.sqrt(128.0))
EPS = 1e-5

ENGS = ("sync", "scalar", "vector", "tensor", "gpsimd")


class Prog:
    def __init__(self):
        self.ops = {e: [] for e in ENGS}
        self.cnt = {k: 0 for k in ("PE", "AC", "VE", "ST", "SA", "SB", "SC",
                                   "LA", "XF", "VT", "CT", "RW", "WG",
                                   "XS", "VS", "CS", "X1S", "QS", "HS", "TS", "CZ",
                                   "RWA", "RWB", "XQF", "SE", "SF", "SG", "SH",
                                   "SI", "SJ", "SK", "SL",
                                   "VSA", "VSB", "VTA", "VTB", "VTC", "VTD")}
        self.handles = {}

    def emit(self, eng, fn):
        self.ops[eng].append(fn)

    def bump(self, name, k):
        self.cnt[name] = self.cnt.get(name, 0) + k
        return self.cnt[name]

    def wait(self, eng, name, val):
        if not val or val <= 0:
            return
        h = self.handles

        def f(e, name=name, val=val):
            e.wait_ge(h[name], val)
        self.emit(eng, f)

    AUTO = {"vector": "VE", "scalar": "AC"}

    def run(self, eng, fn, inc=None, inck=1):
        h = self.handles
        own = self.AUTO.get(eng)
        if own is not None:
            prev = self.cnt[own]
            if inc is None:
                inc, inck = own, 1
            assert inc == own, f"{eng} op must inc {own}"
            v = self.bump(inc, inck)

            def f(e, fn=fn, prev=prev, own=own, inck=inck):
                if prev > 0:
                    e.wait_ge(h[own], prev)
                fn(e).then_inc(h[own], inck)
            self.emit(eng, f)
            return v
        if inc is None:
            self.emit(eng, fn)
            return None
        v = self.bump(inc, inck)

        def f(e, fn=fn, inc=inc, inck=inck):
            fn(e).then_inc(h[inc], inck)
        self.emit(eng, f)
        return v


def build_nc():
    nc = bass.Bass("TRN2", target_bir_lowering=False)

    x_in = nc.dram_tensor("x_in", [S, D], F32, kind="ExternalInput").ap()
    wkvd = nc.dram_tensor("wkvd", [D, KVL], F32, kind="ExternalInput").ap()
    wqd = nc.dram_tensor("wqd", [D, QL], F32, kind="ExternalInput").ap()
    wkc = nc.dram_tensor("wkc", [KVL, 1024], F32, kind="ExternalInput").ap()
    wv = nc.dram_tensor("wv", [KVL, D], F32, kind="ExternalInput").ap()
    wqc = nc.dram_tensor("wqc", [QL, 1024], F32, kind="ExternalInput").ap()
    wkr = nc.dram_tensor("wkr", [D, RD], F32, kind="ExternalInput").ap()
    wqr = nc.dram_tensor("wqr", [QL, 1024], F32, kind="ExternalInput").ap()
    wo = nc.dram_tensor("wo", [D, D], F32, kind="ExternalInput").ap()
    bo = nc.dram_tensor("bo", [1, D], F32, kind="ExternalInput").ap()
    wup = nc.dram_tensor("wup", [FI, D], F32, kind="ExternalInput").ap()
    wdn = nc.dram_tensor("wdn", [D, FI], F32, kind="ExternalInput").ap()
    cosd = nc.dram_tensor("cosd", [128, S], F32, kind="ExternalInput").ap()
    sind = nc.dram_tensor("sind", [128, S], F32, kind="ExternalInput").ap()
    trimask = nc.dram_tensor("trimask", [128, 128], BF16, kind="ExternalInput").ap()
    onescol_bf = nc.dram_tensor("onescol_bf", [128, 1], BF16, kind="ExternalInput").ap()
    onesrow_f = nc.dram_tensor("onesrow_f", [1, 128], F32, kind="ExternalInput").ap()
    onescol_f = nc.dram_tensor("onescol_f", [128, 1], F32, kind="ExternalInput").ap()
    invN_row = nc.dram_tensor("invN_row", [1, 128], F32, kind="ExternalInput").ap()
    halves = nc.dram_tensor("halves", [128, 512], F32, kind="ExternalInput").ap()

    xln_s = nc.dram_tensor("xln_s", [S, D], BF16, kind="Internal").ap()
    v_s = nc.dram_tensor("v_s", [S, D], BF16, kind="Internal").ap()
    ctx_s = nc.dram_tensor("ctx_s", [D, S], BF16, kind="Internal").ap()
    x1_s = nc.dram_tensor("x1_s", [S, D], F32, kind="Internal").ap()
    xq_s = nc.dram_tensor("xq_s", [S, D], BF16, kind="Internal").ap()
    hq_s = nc.dram_tensor("hq_s", [S, FI], BF16, kind="Internal").ap()
    tup_s = nc.dram_tensor("tup_s", [FI, D], BF16, kind="Internal").ap()
    tdn_s = nc.dram_tensor("tdn_s", [D, FI], BF16, kind="Internal").ap()
    out_t = nc.dram_tensor("out", [S, D], F32, kind="ExternalOutput").ap()

    p = Prog()

    def mm_seq(mms, inc="PE"):
        v = None
        for i, (o, l, r, st, sp) in enumerate(mms):
            last = (i == len(mms) - 1)
            if last:
                v = p.run("tensor", lambda e, o=o, l=l, r=r, st=st, sp=sp:
                          e.matmul(o, l, r, start=st, stop=sp), inc=inc)
            else:
                p.run("tensor", lambda e, o=o, l=l, r=r, st=st, sp=sp:
                      e.matmul(o, l, r, start=st, stop=sp))
        return v

    SLOTS = ("SA", "SB", "SC")

    def slot_load(slot, out_ap, in_ap, queue="sync"):
        sem = SLOTS[slot]
        p.wait(queue, sem, p.cnt[sem])
        return p.run(queue, lambda e, o=out_ap, i=in_ap: e.dma_start(out=o, in_=i),
                     inc=sem, inck=16)

    def nr_recip(rt, mt, tmp, steps=2):
        for _ in range(steps):
            p.run("vector", lambda e: e.tensor_mul(tmp, mt, rt))
            p.run("vector", lambda e: e.tensor_scalar(
                tmp, tmp, -1.0, 2.0, OP.mult, OP.add))
            p.run("vector", lambda e: e.tensor_mul(rt, rt, tmp))

    def barrier(exclude=(), overrides=None):
        snap = dict(p.cnt)
        if overrides:
            snap.update(overrides)
        for eng in ENGS:
            for sn in snap:
                if sn in exclude:
                    continue
                p.wait(eng, sn, snap[sn])

    with ExitStack() as top:
        E = top.enter_context
        s_toff = E(nc.sbuf_tensor("s_toff", [128, 8], F32))
        s_ones_bf = E(nc.sbuf_tensor("s_ones_bf", [128, 1], BF16))
        s_onesr_f = E(nc.sbuf_tensor("s_onesr_f", [1, 128], F32))
        s_onesc_f = E(nc.sbuf_tensor("s_onesc_f", [128, 1], F32))
        s_invN = E(nc.sbuf_tensor("s_invN", [1, 128], F32))
        s_trim = E(nc.sbuf_tensor("s_trim", [128, 128], BF16))
        s_eps = E(nc.sbuf_tensor("s_eps", [128, 1], F32))
        s_sdc = E(nc.sbuf_tensor("s_sdc", [128, 1], F32))
        s_isq = E(nc.sbuf_tensor("s_isq", [128, 1], F32))
        s_isq2 = E(nc.sbuf_tensor("s_isq2", [128, 1], F32))
        s_r127 = E(nc.sbuf_tensor("s_r127", [128, 1], F32))
        s_r127b = E(nc.sbuf_tensor("s_r127b", [128, 1], F32))
        s_nrw = E(nc.sbuf_tensor("s_nrw", [128, 1], F32))
        pan = {k: E(nc.sbuf_tensor("pan_" + k, [128, NT], F32))
               for k in ("sum", "sum2", "sq", "mean", "var", "rstd", "rstd2", "nmr")}
        pan2 = {k: E(nc.sbuf_tensor("pan2_" + k, [128, NT], F32))
                for k in ("sum", "sum2", "sq", "mean", "var", "rstd", "rstd2", "nmr")}
        s_thr = {k: E(nc.sbuf_tensor(f"s_thr_{k}", [128, 1], F32)) for k in ("up", "dn")}
        s_nthr = {k: E(nc.sbuf_tensor(f"s_nthr_{k}", [128, 1], F32)) for k in ("up", "dn")}
        s_sw = {k: E(nc.sbuf_tensor(f"s_sw_{k}", [128, 1], F32)) for k in ("up", "dn")}
        abspan = {"up": E(nc.sbuf_tensor("abspan_up", [128, 64], F32)),
                  "dn": E(nc.sbuf_tensor("abspan_dn", [128, 64], F32))}
        colsum = E(nc.sbuf_tensor("colsum", [128, 2], F32))
        tot_row = E(nc.sbuf_tensor("tot_row", [1, 2], F32))
        maxpan1 = E(nc.sbuf_tensor("maxpan1", [128, 8], F32))
        scalepan1 = E(nc.sbuf_tensor("scalepan1", [128, 8], F32))
        evacpan1 = E(nc.sbuf_tensor("evacpan1", [128, 8], F32))
        hmaxpan = E(nc.sbuf_tensor("hmaxpan", [128, 64], F32))
        max2pan = E(nc.sbuf_tensor("max2pan", [128, 8], F32))
        scalepan2 = E(nc.sbuf_tensor("scalepan2", [128, 8], F32))
        evacpan2 = E(nc.sbuf_tensor("evacpan2", [128, 8], F32))
        panwk = E(nc.sbuf_tensor("panwk", [128, 8], F32))

        p.run("sync", lambda e: e.dma_start(out=s_ones_bf[:], in_=onescol_bf), inc="LA", inck=16)
        p.run("sync", lambda e: e.dma_start(out=s_onesr_f[:], in_=onesrow_f), inc="LA", inck=16)
        p.run("sync", lambda e: e.dma_start(out=s_onesc_f[:], in_=onescol_f), inc="LA", inck=16)
        p.run("sync", lambda e: e.dma_start(out=s_invN[:], in_=invN_row), inc="LA", inck=16)
        p.run("sync", lambda e: e.dma_start(out=s_trim[:], in_=trimask), inc="LA", inck=16)
        st0 = ExitStack()
        s_offw = st0.enter_context(nc.sbuf_tensor("s_offw", [128, 512], F32))
        s_offs_i = st0.enter_context(nc.sbuf_tensor("s_offs_i", [128, 512], I32))
        p.run("sync", lambda e: e.dma_start(out=s_offw[:], in_=halves), inc="LA", inck=16)
        p.wait("vector", "LA", p.cnt["LA"])
        p.run("vector", lambda e: e.memset(s_eps[:], EPS))
        p.run("vector", lambda e: e.memset(s_isq2[:], 128.0))
        p.run("vector", lambda e: e.memset(s_r127b[:], 127.0))
        v_c0 = p.run("vector", lambda e: e.memset(s_nrw[:], 0.0), inc="VE")
        p.wait("scalar", "VE", v_c0)
        v_c1 = p.run("scalar", lambda e: e.activation(
            s_isq2[:], s_isq2[:], AF.Sqrt, bias=0.0, scale=1.0), inc="AC")
        p.wait("vector", "AC", v_c1)
        p.run("vector", lambda e: e.reciprocal(s_isq[:], s_isq2[:]))
        nr_recip(s_isq[:], s_isq2[:], s_nrw[:])
        p.run("vector", lambda e: e.reciprocal(s_r127[:], s_r127b[:]))
        nr_recip(s_r127[:], s_r127b[:], s_nrw[:])
        p.run("vector", lambda e: e.tensor_copy(s_offs_i[:], s_offw[:]))
        p.run("vector", lambda e: e.tensor_copy(s_offw[:], s_offs_i[:]))
        p.run("vector", lambda e: e.tensor_scalar(
            s_offw[:], s_offw[:], -0.5, 257.0, OP.mult, OP.add))
        p.run("vector", lambda e: e.tensor_copy(s_toff[:], s_offw[:, 0:8]))
        v_toff = p.run("vector", lambda e: e.tensor_scalar(
            s_sdc[:], s_toff[:, 0:1], -1.0, 256.5, OP.mult, OP.add), inc="VE")
        st0.close()
        barrier()

        tern = {}
        NSL = 6
        TSLOT = ("SG", "SH", "SI", "SJ", "SK", "SL")
        st_tern = ExitStack()
        Et = st_tern.enter_context
        tern["buf"] = [Et(nc.sbuf_tensor(f"twb{i}", [128, 1024], F32))
                       for i in range(NSL)]
        tern["ta"] = Et(nc.sbuf_tensor("tta", [128, 1024], BF16))
        tern["tb"] = Et(nc.sbuf_tensor("ttb", [128, 1024], BF16))
        tern["tt"] = [Et(nc.sbuf_tensor(f"ttt{i}", [128, 1024], BF16)) for i in range(2)]
        tern["gv"] = [0] * NSL
        tern["tguard"] = [0, 0]

        def tern_src(kind, c):
            if kind == "up":
                row, hf = c // 2, c % 2
                return (wup[row * 128:(row + 1) * 128, hf * 1024:(hf + 1) * 1024],
                        tup_s[row * 128:(row + 1) * 128, hf * 1024:(hf + 1) * 1024])
            row, qq = c // 4, c % 4
            return (wdn[row * 128:(row + 1) * 128, qq * 1024:(qq + 1) * 1024],
                    tdn_s[row * 128:(row + 1) * 128, qq * 1024:(qq + 1) * 1024])

        tern["ld"] = [0] * NSL

        def tern_issue(units, i):
            kind, c = units[i]
            b = i % NSL
            src, _ = tern_src(kind, c)
            p.wait("sync", "VE", tern["gv"][b])
            p.wait("sync", TSLOT[b], p.cnt[TSLOT[b]])
            tern["ld"][b] = p.run(
                "sync", lambda e, b=b, src=src: e.dma_start(
                    out=tern["buf"][b][:], in_=src), inc=TSLOT[b], inck=16)

        def tern_stat(units, i):
            kind, c = units[i]
            b = i % NSL
            p.wait("vector", TSLOT[b], tern["ld"][b])
            tern["gv"][b] = p.run(
                "vector", lambda e, b=b, kind=kind, c=c: e.reduce_sum(
                    abspan[kind][:, c:c + 1], tern["buf"][b][:], AX,
                    apply_absolute_value=True), inc="VE")

        def tern_scalars(kind, psc, pbc):
            col = 0 if kind == "up" else 1
            v_cs = p.run("vector", lambda e, kind=kind, col=col: e.reduce_sum(
                colsum[:, col:col + 1], abspan[kind][:], AX), inc="VE")
            p.wait("tensor", "VE", v_cs)
            p.wait("tensor", "AC", p.cnt["AC"])
            v_m1 = p.run("tensor", lambda e, col=col, psc=psc: e.matmul(
                psc[0:1, col:col + 1], colsum[:, col:col + 1], s_onesc_f[:],
                start=True, stop=True), inc="PE")
            p.wait("vector", "PE", v_m1)
            v_ev = p.run("vector", lambda e, col=col, psc=psc: e.tensor_copy(
                tot_row[:, col:col + 1], psc[0:1, col:col + 1]), inc="VE")
            p.wait("tensor", "VE", v_ev)
            v_m2 = p.run("tensor", lambda e, col=col, pbc=pbc: e.matmul(
                pbc[:, col:col + 1], s_invN[:], tot_row[:, col:col + 1],
                start=True, stop=True), inc="PE")
            p.wait("vector", "PE", v_m2)
            p.run("vector", lambda e, kind=kind, col=col, pbc=pbc: e.tensor_copy(
                s_sw[kind][:], pbc[:, col:col + 1]))
            p.run("vector", lambda e, kind=kind: e.tensor_scalar(
                s_sw[kind][:], s_sw[kind][:], EPS, None, OP.max))
            p.run("vector", lambda e, kind=kind: e.tensor_scalar(
                s_thr[kind][:], s_sw[kind][:], 0.5, None, OP.mult))
            v_t = p.run("vector", lambda e, kind=kind: e.tensor_scalar(
                s_nthr[kind][:], s_sw[kind][:], -0.5, None, OP.mult), inc="VE")
            tern["rdy_" + kind] = v_t

        def tern_quant(units, i):
            kind, c = units[i]
            b = i % NSL
            tb2 = i % 2
            _, dst = tern_src(kind, c)
            p.wait("vector", TSLOT[b], tern["ld"][b])
            p.wait("vector", "VE", tern["rdy_" + kind])
            p.wait("vector", "TS", p.cnt["TS"] if tern["tguard"][tb2] else 0)
            p.run("vector", lambda e, b=b, kind=kind: e.tensor_scalar(
                tern["ta"][:], tern["buf"][b][:], s_thr[kind][:], None, OP.is_gt))
            p.run("vector", lambda e, b=b, kind=kind: e.tensor_scalar(
                tern["tb"][:], tern["buf"][b][:], s_nthr[kind][:], None, OP.is_lt))
            vt_ = p.run("vector", lambda e, tb2=tb2: e.tensor_tensor(
                tern["tt"][tb2][:], tern["ta"][:], tern["tb"][:], OP.subtract),
                inc="VE")
            tern["gv"][b] = vt_
            p.wait("gpsimd", "VE", vt_)
            vg = p.run("gpsimd", lambda e, tb2=tb2, dst=dst: e.dma_start(
                out=dst, in_=tern["tt"][tb2][:]), inc="TS", inck=16)
            tern["tguard"][tb2] = vg

        stat_units = [("up", c) for c in range(64)] + [("dn", c) for c in range(64)]
        quant_units = list(stat_units)
        cur = {"stat": [0, 0], "quant": [0, 0]}

        def _do(n, key, units, consume):
            for _ in range(n):
                if cur[key][1] >= len(units):
                    break
                while cur[key][0] < min(cur[key][1] + NSL, len(units)):
                    tern_issue(units, cur[key][0])
                    cur[key][0] += 1
                consume(units, cur[key][1])
                cur[key][1] += 1

        def do_stats(n):
            _do(n, "stat", stat_units, tern_stat)

        def do_quants(n):
            _do(n, "quant", quant_units, tern_quant)


        st1 = ExitStack()
        E1 = st1.enter_context
        xt = [E1(nc.sbuf_tensor(f"xt{i}", [128, D], F32)) for i in range(3)]
        dump = E1(nc.sbuf_tensor("dump", [128, D], F32))
        xlnt = [E1(nc.sbuf_tensor(f"xlnt{i}", [128, D], BF16)) for i in range(2)]

        gve = [0, 0, 0]
        gac = [0, 0, 0]
        for t in range(NT):
            b = t % 3
            p.wait("sync", "VE", gve[b])
            p.wait("sync", "AC", gac[b])
            vls = slot_load(b, xt[b][:], x_in[t * 128:(t + 1) * 128, :])
            p.wait("scalar", SLOTS[b], vls)
            va = p.run("scalar", lambda e, b=b, t=t: e.activation(
                dump[:], xt[b][:], AF.Square, bias=0.0, scale=1.0,
                accum_out=pan["sq"][:, t:t + 1]), inc="AC")
            p.wait("vector", SLOTS[b], vls)
            vr = p.run("vector", lambda e, b=b, t=t: e.reduce_sum(
                pan["sum"][:, t:t + 1], xt[b][:], AX), inc="VE")
            gve[b], gac[b] = vr, va
        v_stats_ac, v_stats_ve = p.cnt["AC"], p.cnt["VE"]

        def panel_math(P, v_red_ve, v_sq_ac):
            p.wait("vector", "AC", v_sq_ac)
            p.run("vector", lambda e: e.tensor_scalar(
                P["mean"][:], P["sum"][:], 1.0 / D, None, OP.mult))
            p.run("vector", lambda e: e.tensor_scalar(
                P["var"][:], P["sq"][:], 1.0 / D, None, OP.mult))
            p.run("vector", lambda e: e.tensor_tensor(
                P["rstd"][:], P["mean"][:], P["mean"][:], OP.mult))
            v_pv = p.run("vector", lambda e: e.tensor_tensor(
                P["var"][:], P["var"][:], P["rstd"][:], OP.subtract), inc="VE")
            p.wait("scalar", "VE", v_pv)
            v_sq = p.run("scalar", lambda e: e.activation(
                P["rstd"][:], P["var"][:], AF.Sqrt, bias=s_eps[:], scale=1.0), inc="AC")
            p.wait("vector", "AC", v_sq)
            p.run("vector", lambda e: e.reciprocal(P["rstd2"][:], P["rstd"][:]))
            nr_recip(P["rstd2"][:], P["rstd"][:], P["sum2"][:])
            p.run("vector", lambda e: e.tensor_tensor(
                P["nmr"][:], P["mean"][:], P["rstd2"][:], OP.mult))
            return p.run("vector", lambda e: e.tensor_scalar(
                P["nmr"][:], P["nmr"][:], -1.0, None, OP.mult), inc="VE")

        v_nmr = panel_math(pan, v_stats_ve, v_stats_ac)

        p.wait("scalar", "VE", v_nmr)
        xg = [0, 0]
        g2 = [v_stats_ac, v_stats_ac, v_stats_ac]
        p.wait("sync", "VE", v_stats_ve)
        for t in range(NT):
            b, ob = t % 3, t % 2
            p.wait("sync", "AC", g2[b])
            vls = slot_load(b, xt[b][:], x_in[t * 128:(t + 1) * 128, :])
            p.wait("scalar", SLOTS[b], vls)
            p.wait("scalar", "XS", p.cnt["XS"])
            va = p.run("scalar", lambda e, b=b, ob=ob, t=t: e.activation(
                xlnt[ob][:], xt[b][:], AF.Identity,
                bias=pan["nmr"][:, t:t + 1], scale=pan["rstd2"][:, t:t + 1]), inc="AC")
            g2[b] = va
            p.wait("gpsimd", "AC", va)
            vg = p.run("gpsimd", lambda e, ob=ob, t=t: e.dma_start(
                out=xln_s[t * 128:(t + 1) * 128, :], in_=xlnt[ob][:]), inc="XS", inck=16)
        v_xln_gd = p.cnt["XS"]
        st1.close()
        barrier()

        st_ckv = ExitStack()
        ckv = [st_ckv.enter_context(nc.sbuf_tensor(f"ckv{c}", [128, S], BF16))
               for c in range(4)]
        st_krp = ExitStack()
        krp = st_krp.enter_context(nc.sbuf_tensor("krp", [64, S], BF16))
        st_cq = ExitStack()
        cq = [st_cq.enter_context(nc.sbuf_tensor(f"cq{c}", [128, S], BF16))
              for c in range(12)]
        st_rope = ExitStack()
        Er = st_rope.enter_context
        s_cos = Er(nc.sbuf_tensor("s_cos", [128, S], BF16))
        s_sin = Er(nc.sbuf_tensor("s_sin", [128, S], BF16))
        ropeA = Er(nc.sbuf_tensor("ropeA", [128, 512], F32))
        ropeB = Er(nc.sbuf_tensor("ropeB", [128, 512], F32))
        st_p2ps = ExitStack()
        Ep = st_p2ps.enter_context
        pj = [Ep(nc.psum_tensor(f"pj{i}", [128, 512], F32)) for i in range(8)]
        pr = [pj[4], pj[5], pj[6], pj[7]]
        t_psc = pj[2]
        t_pbc = pj[3]

        st_xf = ExitStack()
        xf = [st_xf.enter_context(nc.sbuf_tensor(f"xf{c}", [128, S], BF16, side="right"))
              for c in range(16)]
        st_wqd = ExitStack()
        wqd_t = [st_wqd.enter_context(nc.sbuf_tensor(f"wqdt{k}", [128, QL // 2], BF16))
                 for k in range(16)]

        p.wait("sync", "XS", v_xln_gd)
        xf_ld = []
        for c in range(16):
            vls = p.run("sync", lambda e, c=c: e.dma_start(
                out=xf[c][:], in_=xln_s[:, c * 128:(c + 1) * 128],
                transpose=True), inc="XF", inck=16)
            xf_ld.append(vls)
        v_xf_all = xf_ld[-1]

        p.wait("tensor", "XF", v_xf_all)
        ev_g = [0, 0]
        for half in range(2):
            colbase = half * (QL // 2)
            p.wait("gpsimd", "PE", p.cnt["PE"])
            wld = []
            for k in range(16):
                vg = p.run("gpsimd", lambda e, k=k, colbase=colbase: e.dma_start(
                    out=wqd_t[k][:],
                    in_=wqd[k * 128:(k + 1) * 128, colbase:colbase + QL // 2]),
                    inc="WG", inck=16)
                wld.append(vg)
            p.wait("tensor", "WG", wld[-1])
            for mh in range(6):
                m = half * 6 + mh
                gg = m % 2
                p.wait("tensor", "AC", ev_g[gg])
                mms = []
                for k in range(16):
                    for n4 in range(4):
                        mms.append((pj[4 * gg + n4][:],
                                    wqd_t[k][:, mh * 128:(mh + 1) * 128],
                                    xf[k][:, n4 * 512:(n4 + 1) * 512],
                                    k == 0, k == 15))
                v_mm = mm_seq(mms)
                p.wait("scalar", "PE", v_mm)
                for n4 in range(4):
                    ev_g[gg] = p.run("scalar", lambda e, m=m, n4=n4, gg=gg: e.copy(
                        cq[m][:, n4 * 512:(n4 + 1) * 512], pj[4 * gg + n4][:]),
                        inc="AC")
                do_stats(22)
        do_stats(128)
        st_wqd.close()

        st_wkv = ExitStack()
        wkv_t = [st_wkv.enter_context(nc.sbuf_tensor(f"wkvt{k}", [128, KVL], BF16))
                 for k in range(16)]
        p.wait("gpsimd", "PE", p.cnt["PE"])
        wld2 = []
        for k in range(16):
            vg = p.run("gpsimd", lambda e, k=k: e.dma_start(
                out=wkv_t[k][:], in_=wkvd[k * 128:(k + 1) * 128, :]), inc="WG", inck=16)
            wld2.append(vg)
        p.wait("tensor", "WG", wld2[-1])
        for m in range(4):
            gg = m % 2
            p.wait("tensor", "AC", ev_g[gg])
            mms = []
            for k in range(16):
                for n4 in range(4):
                    mms.append((pj[4 * gg + n4][:],
                                wkv_t[k][:, m * 128:(m + 1) * 128],
                                xf[k][:, n4 * 512:(n4 + 1) * 512],
                                k == 0, k == 15))
            v_mm = mm_seq(mms)
            p.wait("scalar", "PE", v_mm)
            for n4 in range(4):
                ev_g[gg] = p.run("scalar", lambda e, m=m, n4=n4, gg=gg: e.copy(
                    ckv[m][:, n4 * 512:(n4 + 1) * 512], pj[4 * gg + n4][:]),
                    inc="AC")
        st_wkv.close()

        st_kr = ExitStack()
        Ek = st_kr.enter_context
        wkr_t = [Ek(nc.sbuf_tensor(f"wkrt{k}", [128, RD], BF16)) for k in range(16)]
        wkrr_t = [Ek(nc.sbuf_tensor(f"wkrrt{k}", [128, RD], BF16)) for k in range(16)]

        p.wait("gpsimd", "PE", p.cnt["PE"])
        v_cos = p.run("gpsimd", lambda e: e.dma_start(out=s_cos[:], in_=cosd), inc="CZ", inck=16)
        v_sin = p.run("gpsimd", lambda e: e.dma_start(out=s_sin[:], in_=sind), inc="CZ", inck=16)
        wld3 = []
        for k in range(16):
            vg = p.run("gpsimd", lambda e, k=k: e.dma_start(
                out=wkr_t[k][:], in_=wkr[k * 128:(k + 1) * 128, :]), inc="WG", inck=16)
            wld3.append(vg)
        p.wait("vector", "WG", wld3[-1])
        for k in range(16):
            p.run("vector", lambda e, k=k: e.tensor_scalar(
                wkrr_t[k][:, 0:32], wkr_t[k][:, 32:64], -1.0, None, OP.mult))
            if k == 15:
                v_wkrr = p.run("vector", lambda e, k=k: e.tensor_copy(
                    wkrr_t[k][:, 32:64], wkr_t[k][:, 0:32]), inc="VE")
            else:
                p.run("vector", lambda e, k=k: e.tensor_copy(
                    wkrr_t[k][:, 32:64], wkr_t[k][:, 0:32]))
        p.wait("tensor", "VE", v_wkrr)
        p.wait("tensor", "CZ", max(v_cos, v_sin))
        g_prp = [0, 0]
        p.wait("tensor", "AC", p.cnt["AC"])
        for n4 in range(4):
            pp = n4 % 2
            p.wait("tensor", "VE", g_prp[pp])
            mms = []
            for k in range(16):
                mms.append((pr[2 * pp][0:64, :], wkr_t[k][:],
                            xf[k][:, n4 * 512:(n4 + 1) * 512], k == 0, k == 15))
            for k in range(16):
                mms.append((pr[2 * pp + 1][0:64, :], wkrr_t[k][:],
                            xf[k][:, n4 * 512:(n4 + 1) * 512], k == 0, k == 15))
            v_krmm = mm_seq(mms)
            p.wait("vector", "PE", v_krmm)
            p.run("vector", lambda e, n4=n4, pp=pp: e.tensor_tensor(
                ropeA[0:64, :], pr[2 * pp][0:64, :],
                s_cos[0:64, n4 * 512:(n4 + 1) * 512], OP.mult))
            p.run("vector", lambda e, n4=n4, pp=pp: e.tensor_tensor(
                ropeB[0:64, :], pr[2 * pp + 1][0:64, :],
                s_sin[0:64, n4 * 512:(n4 + 1) * 512], OP.mult))
            v_krp = p.run("vector", lambda e, n4=n4: e.tensor_tensor(
                krp[:, n4 * 512:(n4 + 1) * 512], ropeA[0:64, :],
                ropeB[0:64, :], OP.add), inc="VE")
            g_prp[pp] = v_krp
        v_krp_done = v_krp
        st_kr.close()
        v_pe_xf_done = p.cnt["PE"]
        st_xf.close()
        do_stats(999)
        tern_scalars("up", t_psc, t_pbc)
        tern_scalars("dn", t_psc, t_pbc)

        st_qk = ExitStack()
        qh = [st_qk.enter_context(nc.sbuf_tensor(f"qh{h}", [128, S], BF16, side="right"))
              for h in range(H)]
        st_wqc = ExitStack()
        wqc_t = [st_wqc.enter_context(nc.sbuf_tensor(f"wqct{k}", [128, 1024], BF16))
                 for k in range(12)]
        p.wait("gpsimd", "PE", v_pe_xf_done)
        p.wait("gpsimd", "VE", v_krp_done)
        wld4 = []
        for k in range(12):
            vg = p.run("gpsimd", lambda e, k=k: e.dma_start(
                out=wqc_t[k][:], in_=wqc[k * 128:(k + 1) * 128, :]), inc="WG", inck=16)
            wld4.append(vg)
        p.wait("tensor", "WG", wld4[-1])
        p.wait("tensor", "AC", p.cnt["AC"])
        p.wait("tensor", "VE", tern["rdy_dn"])
        for m in range(8):
            gg = m % 2
            p.wait("tensor", "AC", ev_g[gg])
            mms = []
            for k in range(12):
                for n4 in range(4):
                    mms.append((pj[4 * gg + n4][:],
                                wqc_t[k][:, m * 128:(m + 1) * 128],
                                cq[k][:, n4 * 512:(n4 + 1) * 512],
                                k == 0, k == 11))
            v_mm = mm_seq(mms)
            p.wait("scalar", "PE", v_mm)
            for n4 in range(4):
                p.run("scalar", lambda e, m=m, n4=n4, gg=gg: e.copy(
                    qh[2 * m][0:64, n4 * 512:(n4 + 1) * 512],
                    pj[4 * gg + n4][0:64, :]))
                ev_g[gg] = p.run("scalar", lambda e, m=m, n4=n4, gg=gg: e.copy(
                    qh[2 * m + 1][0:64, n4 * 512:(n4 + 1) * 512],
                    pj[4 * gg + n4][64:128, :]), inc="AC")
            do_quants(4)
        v_pe_qc_done = p.cnt["PE"]
        st_wqc.close()

        st_wqr = ExitStack()
        Eq = st_wqr.enter_context
        wqr_t = [Eq(nc.sbuf_tensor(f"wqrt{k}", [128, 512], BF16)) for k in range(12)]
        wqrr_t = [Eq(nc.sbuf_tensor(f"wqrrt{k}", [128, 512], BF16)) for k in range(12)]
        p.wait("tensor", "AC", p.cnt["AC"])
        g_prp = [0, 0]
        step = 0
        v_half_pe = v_pe_qc_done
        for hh in range(2):
            p.wait("gpsimd", "PE", v_half_pe)
            if hh:
                p.wait("gpsimd", "VE", v_wqrr)
            wld5 = []
            for k in range(12):
                vg = p.run("gpsimd", lambda e, k=k, hh=hh: e.dma_start(
                    out=wqr_t[k][:],
                    in_=wqr[k * 128:(k + 1) * 128, hh * 512:(hh + 1) * 512]),
                    inc="WG", inck=16)
                wld5.append(vg)
            p.wait("vector", "WG", wld5[-1])
            for k in range(12):
                wt = wqr_t[k][:].rearrange("p (h t c) -> p h t c", h=8, t=2, c=32)
                rt = wqrr_t[k][:].rearrange("p (h t c) -> p h t c", h=8, t=2, c=32)
                p.run("vector", lambda e, wt=wt, rt=rt: e.tensor_scalar(
                    rt[:, :, 0, :], wt[:, :, 1, :], -1.0, None, OP.mult))
                if k == 11:
                    v_wqrr = p.run("vector", lambda e, wt=wt, rt=rt: e.tensor_copy(
                        rt[:, :, 1, :], wt[:, :, 0, :]), inc="VE")
                else:
                    p.run("vector", lambda e, wt=wt, rt=rt: e.tensor_copy(
                        rt[:, :, 1, :], wt[:, :, 0, :]))
            p.wait("tensor", "VE", v_wqrr)
            for m in range(4 * hh, 4 * hh + 4):
                mh = m - 4 * hh
                for n4 in range(4):
                    pp = step % 2
                    step += 1
                    p.wait("tensor", "VE", g_prp[pp])
                    mms = []
                    for k in range(12):
                        mms.append((pr[2 * pp][:], wqr_t[k][:, mh * 128:(mh + 1) * 128],
                                    cq[k][:, n4 * 512:(n4 + 1) * 512], k == 0, k == 11))
                    for k in range(12):
                        mms.append((pr[2 * pp + 1][:],
                                    wqrr_t[k][:, mh * 128:(mh + 1) * 128],
                                    cq[k][:, n4 * 512:(n4 + 1) * 512], k == 0, k == 11))
                    v_qr = mm_seq(mms)
                    p.wait("vector", "PE", v_qr)
                    p.run("vector", lambda e, n4=n4, pp=pp: e.tensor_tensor(
                        ropeA[:], pr[2 * pp][:],
                        s_cos[:, n4 * 512:(n4 + 1) * 512], OP.mult))
                    p.run("vector", lambda e, n4=n4, pp=pp: e.tensor_tensor(
                        ropeB[:], pr[2 * pp + 1][:],
                        s_sin[:, n4 * 512:(n4 + 1) * 512], OP.mult))
                    p.run("vector", lambda e, m=m, n4=n4: e.tensor_tensor(
                        qh[2 * m][64:128, n4 * 512:(n4 + 1) * 512],
                        ropeA[0:64, :], ropeB[0:64, :], OP.add))
                    v_qrope = p.run("vector", lambda e, m=m, n4=n4: e.tensor_tensor(
                        qh[2 * m + 1][64:128, n4 * 512:(n4 + 1) * 512],
                        ropeA[64:128, :], ropeB[64:128, :], OP.add), inc="VE")
                    g_prp[pp] = v_qrope
                    do_quants(1)
            v_half_pe = v_qr
        st_wqr.close()
        st_rope.close()
        v_pe_cq_done = p.cnt["PE"]
        st_cq.close()

        st_kh = ExitStack()
        kh = [st_kh.enter_context(nc.sbuf_tensor(f"kh{h}", [128, S], BF16, side="right"))
              for h in range(H)]
        st_wkc = ExitStack()
        wkc_t = [st_wkc.enter_context(nc.sbuf_tensor(f"wkct{k}", [128, 1024], BF16))
                 for k in range(4)]
        p.wait("gpsimd", "PE", v_pe_cq_done)
        p.wait("gpsimd", "VE", v_qrope)
        wld6 = []
        for k in range(4):
            vg = p.run("gpsimd", lambda e, k=k: e.dma_start(
                out=wkc_t[k][:], in_=wkc[k * 128:(k + 1) * 128, :]), inc="WG", inck=16)
            wld6.append(vg)
        p.wait("tensor", "WG", wld6[-1])
        p.wait("tensor", "VE", v_qrope)
        for m in range(8):
            gg = m % 2
            p.wait("tensor", "AC", ev_g[gg])
            mms = []
            for k in range(4):
                for n4 in range(4):
                    mms.append((pj[4 * gg + n4][:],
                                wkc_t[k][:, m * 128:(m + 1) * 128],
                                ckv[k][:, n4 * 512:(n4 + 1) * 512], k == 0, k == 3))
            v_mm = mm_seq(mms)
            p.wait("scalar", "PE", v_mm)
            for n4 in range(4):
                p.run("scalar", lambda e, m=m, n4=n4, gg=gg: e.copy(
                    kh[2 * m][0:64, n4 * 512:(n4 + 1) * 512],
                    pj[4 * gg + n4][0:64, :]))
                ev_g[gg] = p.run("scalar", lambda e, m=m, n4=n4, gg=gg: e.copy(
                    kh[2 * m + 1][0:64, n4 * 512:(n4 + 1) * 512],
                    pj[4 * gg + n4][64:128, :]), inc="AC")
        p.wait("vector", "VE", v_krp_done)
        for h in range(H):
            if h == H - 1:
                v_khcp = p.run("vector", lambda e, h=h: e.tensor_copy(
                    kh[h][64:128, :], krp[:]), inc="VE")
            else:
                p.run("vector", lambda e, h=h: e.tensor_copy(
                    kh[h][64:128, :], krp[:]))
        st_wkc.close()

        st_wv = ExitStack()
        Ev = st_wv.enter_context
        wv_t = [Ev(nc.sbuf_tensor(f"wvt{k}", [128, D], BF16)) for k in range(4)]
        vout = [Ev(nc.sbuf_tensor(f"vout{i}", [128, D], BF16)) for i in range(2)]
        p.wait("gpsimd", "PE", p.cnt["PE"])
        wld7 = []
        for k in range(4):
            vg = p.run("gpsimd", lambda e, k=k: e.dma_start(
                out=wv_t[k][:], in_=wv[k * 128:(k + 1) * 128, :]), inc="WG", inck=16)
            wld7.append(vg)
        p.wait("tensor", "WG", wld7[-1])
        VSS = ("VSA", "VSB")
        VTS = ("VTA", "VTB")
        for t in range(NT):
            ob = t % 2
            gg = t % 2
            p.wait("tensor", "AC", ev_g[gg])
            mms = []
            for k in range(4):
                for n4 in range(4):
                    mms.append((pj[4 * gg + n4][:],
                                ckv[k][:, t * 128:(t + 1) * 128],
                                wv_t[k][:, n4 * 512:(n4 + 1) * 512], k == 0, k == 3))
            v_mm = mm_seq(mms)
            p.wait("scalar", "PE", v_mm)
            p.wait("scalar", VSS[ob], 16 * (t // 2))
            for n4 in range(4):
                ev_g[gg] = p.run("scalar", lambda e, ob=ob, n4=n4, gg=gg: e.copy(
                    vout[ob][:, n4 * 512:(n4 + 1) * 512], pj[4 * gg + n4][:]),
                    inc="AC")
            p.wait("gpsimd", "AC", ev_g[gg])
            p.run("gpsimd", lambda e, ob=ob, t=t: e.dma_start(
                out=v_s[t * 128:(t + 1) * 128, :], in_=vout[ob][:]),
                inc=VSS[ob], inck=16)
        st_wv.close()
        st_krp.close()
        st_ckv.close()
        st_p2ps.close()
        barrier(exclude=("VSA", "VSB", "SG", "SH", "SI", "SJ", "SK", "SL", "TS"),
                overrides={"VE": v_khcp})
        v_p2_pe = p.cnt["PE"]
        v_p2_ve = v_khcp
        v_p2_ac = p.cnt["AC"]

        st3 = ExitStack()
        E3 = st3.enter_context
        vth = [E3(nc.sbuf_tensor(f"vth{i}", [128, S], BF16)) for i in range(4)]
        ae = [E3(nc.sbuf_tensor(f"ae{i}", [128, 512], BF16)) for i in range(3)]
        srow = E3(nc.sbuf_tensor("srow", [1, 512], F32))
        srr = [E3(nc.sbuf_tensor("srr0", [1, 512], F32))]
        rbc = [E3(nc.sbuf_tensor("rbc0", [128, 512], F32))]
        cxo = [E3(nc.sbuf_tensor(f"cxo{i}", [128, 512], BF16)) for i in range(2)]
        psc = [E3(nc.psum_tensor(f"psc{i}", [128, 512], F32)) for i in range(3)]
        pctx = [E3(nc.psum_tensor(f"pctx{i}", [128, 512], F32)) for i in range(2)]
        psr = [E3(nc.psum_tensor(f"psr{i}", [1, 512], F32)) for i in range(2)]
        pbcp = E3(nc.psum_tensor("pbcp", [128, 512], F32))

        VTS4 = ("VTA", "VTB", "VTC", "VTD")
        p.wait("sync", "VE", v_p2_ve)
        p.wait("sync", "AC", v_p2_ac)
        p.wait("sync", "VSA", 16 * 8)
        p.wait("sync", "VSB", 16 * 8)

        def vth_issue(h, pe_guard):
            sl = VTS4[h % 4]
            p.wait("sync", "PE", pe_guard)
            for t in range(NT):
                p.wait("sync", sl, p.cnt[sl])
                p.run("sync", lambda e, h=h, t=t: e.dma_start(
                    out=vth[h % 4][:, t * 128:(t + 1) * 128],
                    in_=v_s[t * 128:(t + 1) * 128, h * 128:(h + 1) * 128]),
                    inc=sl, inck=16)

        for hh in range(4):
            vth_issue(hh, 0)

        g_psc = [0, 0, 0]
        g_ae = [0, 0, 0]
        g_pctx = [0, 0]
        g_psr = [0, 0]
        g_pbc = 0
        g_cxo = [0, 0]
        g_srr = 0
        p.wait("tensor", "VE", v_p2_ve)
        p.wait("tensor", "AC", v_p2_ac)
        u = 0
        for h in range(H):
            hb4 = h % 4
            p.wait("tensor", VTS4[hb4], 256 * (h // 4 + 1))
            for s4 in range(4):
                Q0 = 512 * s4
                nblocks = 4 * (s4 + 1)
                hb = u % 2
                u += 1
                score_done = {}

                def emit_score(j, h=h, hb=hb, Q0=Q0, s4=s4):
                    r = 128 * (j - 4 * s4) if j >= 4 * s4 else 0
                    sb = j % 3
                    p.wait("tensor", "AC", g_psc[sb])
                    if j == 0:
                        p.wait("tensor", "VE", g_pctx[hb])
                        p.wait("tensor", "AC", g_psr[hb])
                    score_done[j] = p.run(
                        "tensor", lambda e, h=h, j=j, sb=sb, r=r, Q0=Q0:
                        e.matmul(psc[sb][:, r:512],
                                 kh[h][:, j * 128:(j + 1) * 128],
                                 qh[h][:, Q0 + r:Q0 + 512],
                                 start=True, stop=True), inc="PE")

                emit_score(0)
                if nblocks > 1:
                    emit_score(1)
                for j in range(nblocks):
                    r = 128 * (j - 4 * s4) if j >= 4 * s4 else 0
                    sb = j % 3
                    ab = j % 3
                    p.wait("scalar", "PE", max(score_done[j], g_ae[ab]))
                    v_exp = p.run("scalar", lambda e, sb=sb, r=r, ab=ab:
                                  e.activation(ae[ab][:, r:512], psc[sb][:, r:512],
                                               AF.Exp, bias=0.0,
                                               scale=s_isq[:]), inc="AC")
                    g_psc[sb] = v_exp
                    if j >= 4 * s4:
                        p.wait("vector", "AC", v_exp)
                        v_msk = p.run("vector", lambda e, ab=ab, r=r:
                                      e.tensor_tensor(ae[ab][:, r:r + 128],
                                                      ae[ab][:, r:r + 128],
                                                      s_trim[:], OP.mult), inc="VE")
                    else:
                        v_msk = None
                    if j + 2 < nblocks:
                        emit_score(j + 2)
                    if v_msk is not None:
                        p.wait("tensor", "VE", v_msk)
                    else:
                        p.wait("tensor", "AC", v_exp)
                    st_, sp_ = (j == 0), (j == nblocks - 1)
                    p.run("tensor", lambda e, hb4=hb4, j=j, hb=hb, r=r, ab=ab,
                          st_=st_, sp_=sp_:
                          e.matmul(pctx[hb][:, r:512],
                                   vth[hb4][:, j * 128:(j + 1) * 128],
                                   ae[ab][:, r:512], start=st_, stop=sp_))
                    v_bmm = p.run("tensor", lambda e, hb=hb, r=r, ab=ab, st_=st_,
                                  sp_=sp_: e.matmul(psr[hb][:, r:512], s_ones_bf[:],
                                                    ae[ab][:, r:512],
                                                    start=st_, stop=sp_), inc="PE")
                    g_ae[ab] = v_bmm
                p.wait("scalar", "PE", max(v_bmm, g_srr))
                p.run("scalar", lambda e, hb=hb: e.activation(
                    srow[:], psr[hb][:], AF.Ln, bias=0.0, scale=1.0))
                v_rcp = p.run("scalar", lambda e: e.activation(
                    srr[0][:], srow[:], AF.Exp, bias=0.0, scale=-1.0), inc="AC")
                g_psr[hb] = v_rcp
                p.wait("tensor", "AC", v_rcp)
                p.wait("tensor", "VE", g_pbc)
                v_bc = p.run("tensor", lambda e: e.matmul(
                    pbcp[:], s_onesr_f[:], srr[0][:], start=True, stop=True),
                    inc="PE")
                g_srr = v_bc
                p.wait("vector", "PE", v_bc)
                v_rbc = p.run("vector", lambda e: e.tensor_copy(
                    rbc[0][:], pbcp[:]), inc="VE")
                g_pbc = v_rbc
                p.wait("vector", "CS", p.cnt["CS"] if g_cxo[hb] else 0)
                v_cx = p.run("vector", lambda e, hb=hb: e.tensor_tensor(
                    cxo[hb][:], pctx[hb][:], rbc[0][:], OP.mult), inc="VE")
                g_pctx[hb] = v_cx
                p.wait("gpsimd", "VE", v_cx)
                v_gd = p.run("gpsimd", lambda e, h=h, s4=s4, hb=hb: e.dma_start(
                    out=ctx_s[h * 128:(h + 1) * 128, s4 * 512:(s4 + 1) * 512],
                    in_=cxo[hb][:]), inc="CS", inck=16)
                g_cxo[hb] = v_gd
                do_quants(1)
            if h + 4 < H:
                vth_issue(h + 4, p.cnt["PE"])
        v_ctx_gd = p.cnt["CS"]
        v_p3_pe = p.cnt["PE"]
        v_p3_ve = p.cnt["VE"]
        v_p3_ac = p.cnt["AC"]
        st3.close()
        st_kh.close()
        st_qk.close()
        barrier(exclude=("SG", "SH", "SI", "SJ", "SK", "SL", "TS"))

        st4 = ExitStack()
        E4 = st4.enter_context
        ctxc = [E4(nc.sbuf_tensor(f"ctxc{c}", [128, S], BF16)) for c in range(16)]
        woc = [E4(nc.sbuf_tensor(f"woc{c}", [128, D], BF16)) for c in range(16)]
        s_bo = E4(nc.sbuf_tensor("s_bo", [1, D], F32))
        xt4 = [E4(nc.sbuf_tensor(f"xt4_{i}", [128, D], F32)) for i in range(2)]
        x1t = [E4(nc.sbuf_tensor(f"x1t{i}", [128, D], F32)) for i in range(2)]
        pout = [E4(nc.psum_tensor(f"pout{i}", [128, 512], F32)) for i in range(8)]

        p.wait("sync", "CS", v_ctx_gd)
        p.wait("sync", "VE", v_p3_ve)
        p.wait("sync", "AC", v_p3_ac)
        ctx_ld = []
        for c in range(16):
            vls = p.run("sync", lambda e, c=c: e.dma_start(
                out=ctxc[c][:], in_=ctx_s[c * 128:(c + 1) * 128, :]), inc="CT", inck=16)
            ctx_ld.append(vls)
        v_bo_ld = p.run("sync", lambda e: e.dma_start(out=s_bo[:], in_=bo),
                        inc="LA", inck=16)
        p.wait("gpsimd", "PE", v_p3_pe)
        wo_ld = []
        for c in range(16):
            vg = p.run("gpsimd", lambda e, c=c: e.dma_start(
                out=woc[c][:], in_=wo[c * 128:(c + 1) * 128, :]), inc="WG", inck=16)
            wo_ld.append(vg)
        p.wait("tensor", "LA", v_bo_ld)
        p.wait("tensor", "PE", v_p3_pe)
        p.wait("tensor", "VE", v_p3_ve)
        p.wait("tensor", "AC", v_p3_ac)

        g_pout = [0, 0]
        g_x1 = [0, 0]
        g_xt4 = [0, 0]
        for t in range(NT):
            pg = t % 2
            p.wait("sync", "AC", g_xt4[pg])
            v_xl = slot_load(pg, xt4[pg][:], x_in[t * 128:(t + 1) * 128, :])
            p.wait("tensor", "WG", wo_ld[15])
            p.wait("tensor", "CT", ctx_ld[15])
            p.wait("tensor", "VE", g_pout[pg])
            mms = []
            for dv in range(16):
                for o4 in range(4):
                    mms.append((pout[4 * pg + o4][:],
                                ctxc[dv][:, t * 128:(t + 1) * 128],
                                woc[dv][:, o4 * 512:(o4 + 1) * 512],
                                dv == 0, False))
            for o4 in range(4):
                mms.append((pout[4 * pg + o4][:], s_onesr_f[:],
                            s_bo[:, o4 * 512:(o4 + 1) * 512], False, True))
            v_wmm = mm_seq(mms)
            p.wait("vector", "PE", v_wmm)
            p.wait("vector", SLOTS[pg], v_xl)
            p.wait("vector", "X1S", p.cnt["X1S"] if g_x1[pg] else 0)
            for o4 in range(4):
                p.run("vector", lambda e, pg=pg, o4=o4: e.tensor_tensor(
                    x1t[pg][:, o4 * 512:(o4 + 1) * 512], pout[4 * pg + o4][:],
                    xt4[pg][:, o4 * 512:(o4 + 1) * 512], OP.add))
            v_x1 = p.run("vector", lambda e, pg=pg, t=t: e.reduce_sum(
                pan2["sum"][:, t:t + 1], x1t[pg][:], AX), inc="VE")
            g_pout[pg] = v_x1
            p.wait("scalar", "VE", v_x1)
            v_sq2 = p.run("scalar", lambda e, pg=pg, t=t: e.activation(
                xt4[pg][:], x1t[pg][:], AF.Square, bias=0.0, scale=1.0,
                accum_out=pan2["sq"][:, t:t + 1]), inc="AC")
            g_xt4[pg] = v_sq2
            p.wait("gpsimd", "AC", v_sq2)
            v_gd = p.run("gpsimd", lambda e, pg=pg, t=t: e.dma_start(
                out=x1_s[t * 128:(t + 1) * 128, :], in_=x1t[pg][:]), inc="X1S", inck=16)
            g_x1[pg] = v_gd
            do_quants(3)
        do_quants(999)
        v_x1_gd = p.cnt["X1S"]
        v_nmr2 = panel_math(pan2, p.cnt["VE"], p.cnt["AC"])
        v_p4_pe = p.cnt["PE"]
        v_p4_ve = p.cnt["VE"]
        v_p4_ac = p.cnt["AC"]
        st4.close()
        st_tern.close()
        barrier()

        def ffn_half(hf, v_p4_pe, v_p4_ve, v_p4_ac, v_x1_gd, v_nmr2):
            T0 = hf * 8
            R0 = hf * 1024
            st5a = ExitStack()
            E5 = st5a.enter_context
            x1l = [E5(nc.sbuf_tensor(f"x1l{i}_{hf}", [128, D], F32)) for i in range(2)]
            xln2 = [E5(nc.sbuf_tensor(f"xln2_{i}_{hf}", [128, D], F32)) for i in range(8)]
            tq = [E5(nc.sbuf_tensor(f"tq0_{hf}", [128, D], F32))] * 2
            tqk = [E5(nc.sbuf_tensor(f"tqk0_{hf}", [128, D], F32))] * 2
            tqd = E5(nc.sbuf_tensor(f"tqd_{hf}", [128, D], BF16))
            tqc = E5(nc.sbuf_tensor(f"tqc_{hf}", [128, D], BF16))
            tqi = [E5(nc.sbuf_tensor(f"tqi0_{hf}", [128, D], I32))] * 2
            xqb = [E5(nc.sbuf_tensor(f"xqb{i}_{hf}", [128, D], BF16)) for i in range(2)]

            p.wait("scalar", "VE", v_nmr2)
            p.wait("sync", "X1S", v_x1_gd)
            p.wait("sync", "PE", v_p4_pe)
            p.wait("sync", "VE", v_p4_ve)
            p.wait("sync", "AC", v_p4_ac)
            p.wait("sync", "ST", p.cnt["ST"])
            g_x1l = [0, 0]
            for t in range(8):
                gt = T0 + t
                b = t % 2
                p.wait("sync", "VE", g_x1l[b])
                v_xls = slot_load(b, x1l[b][:], x1_s[gt * 128:(gt + 1) * 128, :])
                p.wait("scalar", SLOTS[b], v_xls)
                v_n = p.run("scalar", lambda e, b=b, t=t, gt=gt: e.activation(
                    xln2[t][:], x1l[b][:], AF.Identity,
                    bias=pan2["nmr"][:, gt:gt + 1],
                    scale=pan2["rstd2"][:, gt:gt + 1]), inc="AC")
                p.wait("vector", "AC", v_n)
                v_mx = p.run("vector", lambda e, t=t: e.reduce_max(
                    maxpan1[:, t:t + 1], xln2[t][:], AX,
                    apply_absolute_value=True), inc="VE")
                g_x1l[b] = v_mx
            p.run("vector", lambda e: e.tensor_scalar(
                maxpan1[:], maxpan1[:], EPS, None, OP.max))
            p.run("vector", lambda e: e.reciprocal(panwk[:], maxpan1[:]))
            nr_recip(panwk[:], maxpan1[:], evacpan1[:])
            p.run("vector", lambda e: e.tensor_scalar(
                scalepan1[:], panwk[:], 127.0, None, OP.mult))
            p.run("vector", lambda e: e.tensor_scalar(
                evacpan1[:], maxpan1[:], s_r127[:], None, OP.mult))
            v_pan1 = p.run("vector", lambda e: e.tensor_scalar(
                evacpan1[:], evacpan1[:], s_sw["up"][:], None, OP.mult), inc="VE")

            g_xqb = [0, 0]
            g_tq = [0, 0]
            p.wait("scalar", "VE", max(v_toff, v_pan1))
            v_q = 0
            for t in range(8):
                gt = T0 + t
                b = t % 2
                p.wait("scalar", "VE", v_q)
                p.run("scalar", lambda e, t=t, b=b: e.activation(
                    tq[b][:], xln2[t][:], AF.Identity,
                    bias=s_toff[:, 0:1], scale=scalepan1[:, t:t + 1]))
                p.run("scalar", lambda e, b=b: e.copy(tqi[b][:], tq[b][:]))
                v_t1 = p.run("scalar", lambda e, b=b: e.copy(tqk[b][:], tqi[b][:]),
                             inc="AC")
                p.wait("vector", "AC", v_t1)
                p.wait("vector", "QS", p.cnt["QS"] if g_xqb[b] else 0)
                p.run("vector", lambda e, b=b: e.tensor_sub(tqd[:], tq[b][:], tqk[b][:]))
                p.run("vector", lambda e: e.tensor_scalar(
                    tqd[:], tqd[:], s_sdc[:], None, OP.add))
                p.run("vector", lambda e: e.tensor_scalar(
                    tqc[:], tqd[:], 1.0, None, OP.is_gt))
                p.run("vector", lambda e: e.scalar_tensor_tensor(
                    tqc[:], tqd[:], 2.0, tqc[:], OP.is_gt, OP.add))
                p.run("vector", lambda e: e.scalar_tensor_tensor(
                    tqc[:], tqd[:], 3.0, tqc[:], OP.is_gt, OP.add))
                p.run("vector", lambda e: e.scalar_tensor_tensor(
                    tqc[:], tqd[:], 0.0, tqc[:], OP.is_ge, OP.add))
                p.run("vector", lambda e: e.scalar_tensor_tensor(
                    tqc[:], tqd[:], -1.0, tqc[:], OP.is_ge, OP.add))
                p.run("vector", lambda e: e.scalar_tensor_tensor(
                    tqc[:], tqd[:], -2.0, tqc[:], OP.is_ge, OP.add))
                v_q = p.run("vector", lambda e, b=b: e.scalar_tensor_tensor(
                    xqb[b][:], tqc[:], -259.0, tqk[b][:], OP.add, OP.add), inc="VE")
                g_tq[b] = v_q
                p.wait("gpsimd", "VE", v_q)
                g_xqb[b] = p.run("gpsimd", lambda e, b=b, gt=gt: e.dma_start(
                    out=xq_s[gt * 128:(gt + 1) * 128, :], in_=xqb[b][:]),
                    inc="QS", inck=16)
                if t == 3:
                    v_xq_mid = p.cnt["QS"]
            v_xq_gd = p.cnt["QS"]
            v_a_ve = p.cnt["VE"]
            v_a_ac = p.cnt["AC"]
            st5a.close()

            st5b = ExitStack()
            E5 = st5b.enter_context
            ht = [E5(nc.sbuf_tensor(f"ht{t}_{hf}", [128, FI], F32)) for t in range(8)]
            trelu = E5(nc.sbuf_tensor(f"trelu_{hf}", [128, 512], F32))
            pup = [E5(nc.psum_tensor(f"pup{i}_{hf}", [128, 512], F32)) for i in range(4)]
            st5bio = ExitStack()
            E5i = st5bio.enter_context
            xqf = [E5i(nc.sbuf_tensor(f"xqf{c}_{hf}", [128, 1024], BF16)) for c in range(16)]
            rup = [[E5i(nc.sbuf_tensor(f"rup{g}_{k}_{hf}", [128, 512], BF16))
                    for k in range(16)] for g in range(2)]

            def rup_load(ip):
                g = ip % 2
                sem = ("RWA", "RWB")[g]
                last = None
                for hc in range(16):
                    last = p.run("sync", lambda e, g=g, hc=hc, ip=ip: e.dma_start(
                        out=rup[g][hc][:],
                        in_=tup_s[ip * 512:(ip + 1) * 512, hc * 128:(hc + 1) * 128],
                        transpose=True), inc=sem, inck=16)
                return last

            rup_rdy = [0, 0]
            rup_pe_hist = [0, 0]
            rup_rdy[0] = rup_load(0)
            rup_rdy[1] = rup_load(1)
            xqf_ld = []
            p.wait("sync", "QS", v_xq_mid)
            for c in range(16):
                vls = p.run("sync", lambda e, c=c, R0=R0: e.dma_start(
                    out=xqf[c][:, 0:512], in_=xq_s[R0:R0 + 512, c * 128:(c + 1) * 128],
                    transpose=True), inc="XQF", inck=16)
                xqf_ld.append(vls)
            p.wait("sync", "QS", v_xq_gd)
            p.wait("sync", "VE", v_a_ve)
            p.wait("sync", "AC", v_a_ac)
            for c in range(16):
                vls = p.run("sync", lambda e, c=c, R0=R0: e.dma_start(
                    out=xqf[c][:, 512:1024],
                    in_=xq_s[R0 + 512:R0 + 1024, c * 128:(c + 1) * 128],
                    transpose=True), inc="XQF", inck=16)
                xqf_ld.append(vls)

            g_pup = [0, 0, 0, 0]
            p.wait("tensor", "PE", v_p4_pe)
            p.wait("tensor", "VE", v_p4_ve)
            p.wait("tensor", "AC", v_p4_ac)
            for ip in range(8):
                g = ip % 2
                if 2 <= ip + 1 < 8:
                    p.wait("sync", "PE", rup_pe_hist[(ip + 1) % 2])
                    rup_rdy[(ip + 1) % 2] = rup_load(ip + 1)
                for t in range(8):
                    pb = t % 4
                    p.wait("tensor", ("RWA", "RWB")[g], rup_rdy[g])
                    p.wait("tensor", "XQF", xqf_ld[-1])
                    p.wait("tensor", "AC", g_pup[pb])
                    mms = []
                    for hc in range(16):
                        mms.append((pup[pb][:], xqf[hc][:, t * 128:(t + 1) * 128],
                                    rup[g][hc][:], hc == 0, hc == 15))
                    v_up = mm_seq(mms)
                    p.wait("scalar", "PE", v_up)
                    p.run("scalar", lambda e, pb=pb, t=t: e.activation(
                        trelu[:], pup[pb][:], AF.Relu, bias=0.0,
                        scale=evacpan1[:, t:t + 1]))
                    v_hsq = p.run("scalar", lambda e, t=t, ip=ip: e.activation(
                        ht[t][:, ip * 512:(ip + 1) * 512], trelu[:], AF.Square,
                        bias=0.0, scale=1.0), inc="AC")
                    g_pup[pb] = v_hsq
                    p.wait("vector", "AC", v_hsq)
                    v_hm = p.run("vector", lambda e, t=t, ip=ip: e.reduce_max(
                        hmaxpan[:, t * 8 + ip:t * 8 + ip + 1],
                        ht[t][:, ip * 512:(ip + 1) * 512], AX), inc="VE")
                rup_pe_hist[g] = p.cnt["PE"]

            for t in range(8):
                p.run("vector", lambda e, t=t: e.reduce_max(
                    max2pan[:, t:t + 1], hmaxpan[:, t * 8:(t + 1) * 8], AX))
            p.run("vector", lambda e: e.tensor_scalar(
                max2pan[:], max2pan[:], EPS, None, OP.max))
            p.run("vector", lambda e: e.reciprocal(panwk[:], max2pan[:]))
            nr_recip(panwk[:], max2pan[:], evacpan2[:])
            p.run("vector", lambda e: e.tensor_scalar(
                scalepan2[:], panwk[:], 127.0, None, OP.mult))
            p.run("vector", lambda e: e.tensor_scalar(
                evacpan2[:], max2pan[:], s_r127[:], None, OP.mult))
            v_pan2 = p.run("vector", lambda e: e.tensor_scalar(
                evacpan2[:], evacpan2[:], s_sw["dn"][:], None, OP.mult), inc="VE")

            st5bio.close()
            FH = FI // 2
            tq2 = [E5(nc.sbuf_tensor(f"tq2{i}_{hf}", [128, FH], F32)) for i in range(2)]
            tq2k = [E5(nc.sbuf_tensor(f"tq2k{i}_{hf}", [128, FH], F32)) for i in range(2)]
            tq2d = E5(nc.sbuf_tensor(f"tq2d_{hf}", [128, FH], BF16))
            tq2c = E5(nc.sbuf_tensor(f"tq2c_{hf}", [128, FH], BF16))
            tq2i = [E5(nc.sbuf_tensor(f"tq2i{i}_{hf}", [128, FH], I32)) for i in range(2)]
            hqb = [E5(nc.sbuf_tensor(f"hqb{i}_{hf}", [128, FH], BF16)) for i in range(2)]
            g_hqb = [0, 0]
            g_tq2 = [0, 0]
            p.wait("scalar", "VE", v_pan2)
            v_q2 = 0
            for idx in range(16):
                t, q2 = idx // 2, idx % 2
                gt = T0 + t
                b = idx % 2
                p.wait("scalar", "VE", g_tq2[b])
                p.run("scalar", lambda e, t=t, q2=q2, b=b: e.activation(
                    tq2[b][:], ht[t][:, q2 * FH:(q2 + 1) * FH], AF.Identity,
                    bias=s_toff[:, 0:1], scale=scalepan2[:, t:t + 1]))
                p.run("scalar", lambda e, b=b: e.copy(tq2i[b][:], tq2[b][:]))
                v_t1 = p.run("scalar", lambda e, b=b: e.copy(tq2k[b][:], tq2i[b][:]),
                             inc="AC")
                p.wait("vector", "AC", v_t1)
                p.wait("vector", "HS", p.cnt["HS"] if g_hqb[b] else 0)
                p.run("vector", lambda e, b=b: e.tensor_sub(
                    tq2d[:], tq2[b][:], tq2k[b][:]))
                p.run("vector", lambda e: e.tensor_scalar(
                    tq2d[:], tq2d[:], s_sdc[:], None, OP.add))
                p.run("vector", lambda e: e.tensor_scalar(
                    tq2c[:], tq2d[:], 1.0, None, OP.is_gt))
                p.run("vector", lambda e: e.scalar_tensor_tensor(
                    tq2c[:], tq2d[:], 2.0, tq2c[:], OP.is_gt, OP.add))
                p.run("vector", lambda e: e.scalar_tensor_tensor(
                    tq2c[:], tq2d[:], 3.0, tq2c[:], OP.is_gt, OP.add))
                p.run("vector", lambda e: e.scalar_tensor_tensor(
                    tq2c[:], tq2d[:], 0.0, tq2c[:], OP.is_ge, OP.add))
                p.run("vector", lambda e: e.scalar_tensor_tensor(
                    tq2c[:], tq2d[:], -1.0, tq2c[:], OP.is_ge, OP.add))
                p.run("vector", lambda e: e.scalar_tensor_tensor(
                    tq2c[:], tq2d[:], -2.0, tq2c[:], OP.is_ge, OP.add))
                v_q2 = p.run("vector", lambda e, b=b: e.scalar_tensor_tensor(
                    hqb[b][:], tq2c[:], -259.0, tq2k[b][:], OP.add, OP.add), inc="VE")
                g_tq2[b] = v_q2
                p.wait("gpsimd", "VE", v_q2)
                g_hqb[b] = p.run("gpsimd", lambda e, b=b, gt=gt, q2=q2: e.dma_start(
                    out=hq_s[gt * 128:(gt + 1) * 128, q2 * FH:(q2 + 1) * FH],
                    in_=hqb[b][:]), inc="HS", inck=16)
                if idx == 7:
                    v_hq_mid = p.cnt["HS"]
                if idx == 11:
                    v_hq_t5 = p.cnt["HS"]
            v_hq_gd = p.cnt["HS"]
            v_b_ve = p.cnt["VE"]
            v_b_ac = p.cnt["AC"]
            v_b_pe = p.cnt["PE"]
            st5b.close()

            st5c = ExitStack()
            E5 = st5c.enter_context
            hqf = [E5(nc.sbuf_tensor(f"hqf{c}_{hf}", [128, 1024], BF16)) for c in range(32)]
            rdn = [[E5(nc.sbuf_tensor(f"rdn{g}_{k}_{hf}", [128, 512], BF16))
                    for k in range(32)] for g in range(2)]
            x1c = [E5(nc.sbuf_tensor(f"x1c{i}_{hf}", [128, 512], F32)) for i in range(2)]
            obuf = [E5(nc.sbuf_tensor(f"obuf{i}_{hf}", [128, 512], F32)) for i in range(2)]
            tdo = E5(nc.sbuf_tensor(f"tdo_{hf}", [128, 512], F32))
            pdn = [E5(nc.psum_tensor(f"pdn{i}_{hf}", [128, 512], F32)) for i in range(4)]

            def rdn_load(op4):
                g = op4 % 2
                sem = ("RWA", "RWB")[g]
                last = None
                for ic in range(32):
                    last = p.run("sync", lambda e, g=g, ic=ic, op4=op4: e.dma_start(
                        out=rdn[g][ic][:],
                        in_=tdn_s[op4 * 512:(op4 + 1) * 512, ic * 128:(ic + 1) * 128],
                        transpose=True), inc=sem, inck=16)
                return last

            hqf_ld = []
            p.wait("sync", "HS", v_hq_mid)
            for c in range(32):
                vls = p.run("sync", lambda e, c=c, R0=R0: e.dma_start(
                    out=hqf[c][:, 0:512], in_=hq_s[R0:R0 + 512, c * 128:(c + 1) * 128],
                    transpose=True), inc="XQF", inck=16)
                hqf_ld.append(vls)
            p.wait("sync", "HS", v_hq_t5)
            rdn_rdy = [0, 0]
            rdn_pe_hist = [v_b_pe, v_b_pe]
            rdn_rdy[0] = rdn_load(0)
            p.wait("sync", "HS", v_hq_gd)
            p.wait("sync", "VE", v_b_ve)
            p.wait("sync", "AC", v_b_ac)
            for c in range(32):
                vls = p.run("sync", lambda e, c=c, R0=R0: e.dma_start(
                    out=hqf[c][:, 512:1024],
                    in_=hq_s[R0 + 512:R0 + 1024, c * 128:(c + 1) * 128],
                    transpose=True), inc="XQF", inck=16)
                hqf_ld.append(vls)

            g_pdn = [0, 0, 0, 0]
            g_ob = [0, 0]
            g_x1c = [0, 0]
            g_tdo = 0
            p.wait("tensor", "AC", v_b_ac)
            for op4 in range(4):
                g = op4 % 2
                if op4 + 1 < 4:
                    p.wait("sync", "PE", rdn_pe_hist[(op4 + 1) % 2])
                    rdn_rdy[(op4 + 1) % 2] = rdn_load(op4 + 1)
                for t in range(8):
                    gt = T0 + t
                    b = t % 2
                    pb = t % 4
                    p.wait("sync", "VE", g_x1c[b])
                    v_x1r = slot_load(b, x1c[b][:],
                                      x1_s[gt * 128:(gt + 1) * 128,
                                           op4 * 512:(op4 + 1) * 512])
                    p.wait("tensor", ("RWA", "RWB")[g], rdn_rdy[g])
                    p.wait("tensor", "XQF", hqf_ld[-1])
                    p.wait("tensor", "AC", g_pdn[pb])
                    mms = []
                    for ic in range(32):
                        mms.append((pdn[pb][:], hqf[ic][:, t * 128:(t + 1) * 128],
                                    rdn[g][ic][:], ic == 0, ic == 31))
                    v_dn = mm_seq(mms)
                    p.wait("scalar", "PE", v_dn)
                    p.wait("scalar", "VE", g_tdo)
                    v_ev = p.run("scalar", lambda e, pb=pb, t=t: e.activation(
                        tdo[:], pdn[pb][:], AF.Copy, bias=0.0,
                        scale=evacpan2[:, t:t + 1]), inc="AC")
                    g_pdn[pb] = v_ev
                    p.wait("vector", "AC", v_ev)
                    p.wait("vector", SLOTS[b], v_x1r)
                    p.wait("vector", "ST", p.cnt["ST"] if g_ob[b] else 0)
                    v_o = p.run("vector", lambda e, b=b: e.tensor_tensor(
                        obuf[b][:], tdo[:], x1c[b][:], OP.add), inc="VE")
                    g_x1c[b] = v_o
                    g_tdo = v_o
                    p.wait("gpsimd", "VE", v_o)
                    v_st = p.run("gpsimd", lambda e, b=b, gt=gt, op4=op4: e.dma_start(
                        out=out_t[gt * 128:(gt + 1) * 128,
                                  op4 * 512:(op4 + 1) * 512], in_=obuf[b][:]),
                        inc="ST", inck=16)
                    g_ob[b] = v_st
                rdn_pe_hist[g] = p.cnt["PE"]
            v_dn_ve = p.cnt["VE"]
            v_dn_ac = p.cnt["AC"]
            v_dn_pe = p.cnt["PE"]
            st5c.close()
            return v_dn_pe, v_dn_ve, v_dn_ac

        for hf in range(2):
            v_p4_pe, v_p4_ve, v_p4_ac = ffn_half(
                hf, v_p4_pe, v_p4_ve, v_p4_ac, v_x1_gd, v_nmr2)
            v_x1_gd = 0

        p.wait("sync", "ST", p.cnt["ST"])

        for name in p.cnt:
            p.handles[name] = E(nc.semaphore(name))
        block = E(nc.Block())

        for ename, attr in (("sync", block.sync), ("scalar", block.scalar),
                            ("vector", block.vector), ("tensor", block.tensor),
                            ("gpsimd", block.gpsimd)):
            def section(eng, ename=ename):
                for f in p.ops[ename]:
                    f(eng)
            attr(section)

    return nc


def _host_constants():
    import jax
    import jax.numpy as jnp
    ROPE_BASE = 10000.0
    with jax.default_device(jax.local_devices(backend="cpu")[0]):
        inv_freq = 1.0 / ROPE_BASE ** (jnp.arange(0, RD, 2, dtype=jnp.float32) / RD)
        t = jnp.arange(S, dtype=jnp.float32)
        freqs = jnp.outer(t, inv_freq)
        emb = jnp.concatenate((freqs, freqs), axis=-1)
        cos = np.asarray(jnp.cos(emb).astype(jnp.float32))
        sin = np.asarray(jnp.sin(emb).astype(jnp.float32))
    cosd = np.zeros((128, S), np.float32)
    sind = np.zeros((128, S), np.float32)
    cosd[0:64] = cos.T
    cosd[64:128] = cos.T
    sind[0:64] = sin.T
    sind[64:128] = sin.T
    tm = np.triu(np.ones((128, 128), np.float32)).astype(ml_dtypes.bfloat16)
    return {
        "cosd": cosd,
        "sind": sind,
        "trimask": tm,
        "onescol_bf": np.ones((128, 1), ml_dtypes.bfloat16),
        "onesrow_f": np.ones((1, 128), np.float32),
        "onescol_f": np.ones((128, 1), np.float32),
        "invN_row": np.full((1, 128), 1.0 / (FI * D), np.float32),
        "halves": np.full((128, 512), 1.5, np.float32),
    }


_CACHE = {}


def kernel(L0_attn_x, L0_attn_W_kv_down, L0_attn_W_q_down, L0_attn_W_kc,
           L0_attn_W_v, L0_attn_W_qc, L0_attn_W_kr, L0_attn_W_qr, L0_attn_W_o,
           L0_attn_b_o, L0_ffn_w_up, L0_ffn_w_down):
    if "nc" not in _CACHE:
        _CACHE["nc"] = build_nc()
    nc = _CACHE["nc"]
    x = np.ascontiguousarray(np.asarray(L0_attn_x, np.float32))
    common = dict(
        wkvd=np.ascontiguousarray(np.asarray(L0_attn_W_kv_down, np.float32)),
        wqd=np.ascontiguousarray(np.asarray(L0_attn_W_q_down, np.float32)),
        wkc=np.ascontiguousarray(np.asarray(L0_attn_W_kc, np.float32)),
        wv=np.ascontiguousarray(np.asarray(L0_attn_W_v, np.float32)),
        wqc=np.ascontiguousarray(np.asarray(L0_attn_W_qc, np.float32)),
        wkr=np.ascontiguousarray(np.asarray(L0_attn_W_kr, np.float32)),
        wqr=np.ascontiguousarray(np.asarray(L0_attn_W_qr, np.float32)),
        wo=np.ascontiguousarray(np.asarray(L0_attn_W_o, np.float32)),
        bo=np.ascontiguousarray(np.asarray(L0_attn_b_o, np.float32).reshape(1, D)),
        wup=np.ascontiguousarray(np.asarray(L0_ffn_w_up, np.float32)),
        wdn=np.ascontiguousarray(np.asarray(L0_ffn_w_down, np.float32)),
        **_host_constants(),
    )
    in_maps = []
    for b in range(8):
        m = dict(common)
        m["x_in"] = np.ascontiguousarray(x[b])
        in_maps.append(m)
    res = run_bass_kernel_spmd(nc, in_maps, core_ids=list(range(8)))
    return np.stack([np.asarray(res.results[b]["out"]) for b in range(8)], axis=0)


# version 19
# speedup vs baseline: 1.0550x; 1.0124x over previous
import numpy as np
import ml_dtypes
from contextlib import ExitStack

import concourse.bass as bass
import concourse.mybir as mybir
from concourse.alu_op_type import AluOpType as OP
from concourse.bass_utils import run_bass_kernel_spmd

F32 = mybir.dt.float32
BF16 = mybir.dt.bfloat16
I32 = mybir.dt.int32
AX = mybir.AxisListType.X
AF = mybir.ActivationFunctionType

S = 2048
D = 2048
H = 16
RD = 64
KVL = 512
QL = 1536
FI = 4096
NT = 16
INV_SQRT_HD = float(1.0 / np.sqrt(128.0))
EPS = 1e-5

ENGS = ("sync", "scalar", "vector", "tensor", "gpsimd")


class Prog:
    def __init__(self):
        self.ops = {e: [] for e in ENGS}
        self.cnt = {k: 0 for k in ("PE", "AC", "VE", "ST", "SA", "SB", "SC",
                                   "LA", "XF", "VT", "CT", "RW", "WG",
                                   "XS", "VS", "CS", "X1S", "QS", "HS", "TS", "CZ",
                                   "RWA", "RWB", "XQF", "SE", "SF", "SG", "SH",
                                   "SI", "SJ", "SK", "SL",
                                   "VSA", "VSB", "VTA", "VTB", "VTC", "VTD")}
        self.handles = {}

    def emit(self, eng, fn):
        self.ops[eng].append(fn)

    def bump(self, name, k):
        self.cnt[name] = self.cnt.get(name, 0) + k
        return self.cnt[name]

    def wait(self, eng, name, val):
        if not val or val <= 0:
            return
        h = self.handles

        def f(e, name=name, val=val):
            e.wait_ge(h[name], val)
        self.emit(eng, f)

    AUTO = {"vector": "VE", "scalar": "AC"}

    def run(self, eng, fn, inc=None, inck=1):
        h = self.handles
        own = self.AUTO.get(eng)
        if own is not None:
            prev = self.cnt[own]
            if inc is None:
                inc, inck = own, 1
            assert inc == own, f"{eng} op must inc {own}"
            v = self.bump(inc, inck)

            def f(e, fn=fn, prev=prev, own=own, inck=inck):
                if prev > 0:
                    e.wait_ge(h[own], prev)
                fn(e).then_inc(h[own], inck)
            self.emit(eng, f)
            return v
        if inc is None:
            self.emit(eng, fn)
            return None
        v = self.bump(inc, inck)

        def f(e, fn=fn, inc=inc, inck=inck):
            fn(e).then_inc(h[inc], inck)
        self.emit(eng, f)
        return v


def build_nc():
    nc = bass.Bass("TRN2", target_bir_lowering=False)

    x_in = nc.dram_tensor("x_in", [S, D], F32, kind="ExternalInput").ap()
    wkvd = nc.dram_tensor("wkvd", [D, KVL], F32, kind="ExternalInput").ap()
    wqd = nc.dram_tensor("wqd", [D, QL], F32, kind="ExternalInput").ap()
    wkc = nc.dram_tensor("wkc", [KVL, 1024], F32, kind="ExternalInput").ap()
    wv = nc.dram_tensor("wv", [KVL, D], F32, kind="ExternalInput").ap()
    wqc = nc.dram_tensor("wqc", [QL, 1024], F32, kind="ExternalInput").ap()
    wkr = nc.dram_tensor("wkr", [D, RD], F32, kind="ExternalInput").ap()
    wqr = nc.dram_tensor("wqr", [QL, 1024], F32, kind="ExternalInput").ap()
    wo = nc.dram_tensor("wo", [D, D], F32, kind="ExternalInput").ap()
    bo = nc.dram_tensor("bo", [1, D], F32, kind="ExternalInput").ap()
    wup = nc.dram_tensor("wup", [FI, D], F32, kind="ExternalInput").ap()
    wdn = nc.dram_tensor("wdn", [D, FI], F32, kind="ExternalInput").ap()
    cosd = nc.dram_tensor("cosd", [128, S], F32, kind="ExternalInput").ap()
    sind = nc.dram_tensor("sind", [128, S], F32, kind="ExternalInput").ap()
    trimask = nc.dram_tensor("trimask", [128, 128], BF16, kind="ExternalInput").ap()
    onescol_bf = nc.dram_tensor("onescol_bf", [128, 1], BF16, kind="ExternalInput").ap()
    onesrow_f = nc.dram_tensor("onesrow_f", [1, 128], F32, kind="ExternalInput").ap()
    onescol_f = nc.dram_tensor("onescol_f", [128, 1], F32, kind="ExternalInput").ap()
    invN_row = nc.dram_tensor("invN_row", [1, 128], F32, kind="ExternalInput").ap()
    halves = nc.dram_tensor("halves", [128, 512], F32, kind="ExternalInput").ap()

    xln_s = nc.dram_tensor("xln_s", [S, D], BF16, kind="Internal").ap()
    v_s = nc.dram_tensor("v_s", [S, D], BF16, kind="Internal").ap()
    ctx_s = nc.dram_tensor("ctx_s", [D, S], BF16, kind="Internal").ap()
    x1_s = nc.dram_tensor("x1_s", [S, D], F32, kind="Internal").ap()
    xq_s = nc.dram_tensor("xq_s", [S, D], BF16, kind="Internal").ap()
    hq_s = nc.dram_tensor("hq_s", [S, FI], BF16, kind="Internal").ap()
    tup_s = nc.dram_tensor("tup_s", [FI, D], BF16, kind="Internal").ap()
    tdn_s = nc.dram_tensor("tdn_s", [D, FI], BF16, kind="Internal").ap()
    out_t = nc.dram_tensor("out", [S, D], F32, kind="ExternalOutput").ap()

    p = Prog()

    def mm_seq(mms, inc="PE"):
        v = None
        for i, (o, l, r, st, sp) in enumerate(mms):
            last = (i == len(mms) - 1)
            if last:
                v = p.run("tensor", lambda e, o=o, l=l, r=r, st=st, sp=sp:
                          e.matmul(o, l, r, start=st, stop=sp), inc=inc)
            else:
                p.run("tensor", lambda e, o=o, l=l, r=r, st=st, sp=sp:
                      e.matmul(o, l, r, start=st, stop=sp))
        return v

    SLOTS = ("SA", "SB", "SC")

    def slot_load(slot, out_ap, in_ap, queue="sync"):
        sem = SLOTS[slot]
        p.wait(queue, sem, p.cnt[sem])
        return p.run(queue, lambda e, o=out_ap, i=in_ap: e.dma_start(out=o, in_=i),
                     inc=sem, inck=16)

    def nr_recip(rt, mt, tmp, steps=2):
        for _ in range(steps):
            p.run("vector", lambda e: e.tensor_mul(tmp, mt, rt))
            p.run("vector", lambda e: e.tensor_scalar(
                tmp, tmp, -1.0, 2.0, OP.mult, OP.add))
            p.run("vector", lambda e: e.tensor_mul(rt, rt, tmp))

    def barrier(exclude=(), overrides=None):
        snap = dict(p.cnt)
        if overrides:
            snap.update(overrides)
        for eng in ENGS:
            for sn in snap:
                if sn in exclude:
                    continue
                p.wait(eng, sn, snap[sn])

    with ExitStack() as top:
        E = top.enter_context
        s_toff = E(nc.sbuf_tensor("s_toff", [128, 8], F32))
        s_ones_bf = E(nc.sbuf_tensor("s_ones_bf", [128, 1], BF16))
        s_onesr_f = E(nc.sbuf_tensor("s_onesr_f", [1, 128], F32))
        s_onesc_f = E(nc.sbuf_tensor("s_onesc_f", [128, 1], F32))
        s_invN = E(nc.sbuf_tensor("s_invN", [1, 128], F32))
        s_trim = E(nc.sbuf_tensor("s_trim", [128, 128], BF16))
        s_eps = E(nc.sbuf_tensor("s_eps", [128, 1], F32))
        s_sdc = E(nc.sbuf_tensor("s_sdc", [128, 1], F32))
        s_isq = E(nc.sbuf_tensor("s_isq", [128, 1], F32))
        s_isq2 = E(nc.sbuf_tensor("s_isq2", [128, 1], F32))
        s_r127 = E(nc.sbuf_tensor("s_r127", [128, 1], F32))
        s_r127b = E(nc.sbuf_tensor("s_r127b", [128, 1], F32))
        s_nrw = E(nc.sbuf_tensor("s_nrw", [128, 1], F32))
        pan = {k: E(nc.sbuf_tensor("pan_" + k, [128, NT], F32))
               for k in ("sum", "sum2", "sq", "mean", "var", "rstd", "rstd2", "nmr")}
        pan2 = {k: E(nc.sbuf_tensor("pan2_" + k, [128, NT], F32))
                for k in ("sum", "sum2", "sq", "mean", "var", "rstd", "rstd2", "nmr")}
        s_thr = {k: E(nc.sbuf_tensor(f"s_thr_{k}", [128, 1], F32)) for k in ("up", "dn")}
        s_nthr = {k: E(nc.sbuf_tensor(f"s_nthr_{k}", [128, 1], F32)) for k in ("up", "dn")}
        s_sw = {k: E(nc.sbuf_tensor(f"s_sw_{k}", [128, 1], F32)) for k in ("up", "dn")}
        abspan = {"up": E(nc.sbuf_tensor("abspan_up", [128, 64], F32)),
                  "dn": E(nc.sbuf_tensor("abspan_dn", [128, 64], F32))}
        colsum = E(nc.sbuf_tensor("colsum", [128, 2], F32))
        tot_row = E(nc.sbuf_tensor("tot_row", [1, 2], F32))
        maxpan1 = E(nc.sbuf_tensor("maxpan1", [128, 8], F32))
        scalepan1 = E(nc.sbuf_tensor("scalepan1", [128, 8], F32))
        evacpan1 = E(nc.sbuf_tensor("evacpan1", [128, 8], F32))
        hmaxpan = E(nc.sbuf_tensor("hmaxpan", [128, 64], F32))
        max2pan = E(nc.sbuf_tensor("max2pan", [128, 8], F32))
        scalepan2 = E(nc.sbuf_tensor("scalepan2", [128, 8], F32))
        evacpan2 = E(nc.sbuf_tensor("evacpan2", [128, 8], F32))
        panwk = E(nc.sbuf_tensor("panwk", [128, 8], F32))

        p.run("sync", lambda e: e.dma_start(out=s_ones_bf[:], in_=onescol_bf), inc="LA", inck=16)
        p.run("sync", lambda e: e.dma_start(out=s_onesr_f[:], in_=onesrow_f), inc="LA", inck=16)
        p.run("sync", lambda e: e.dma_start(out=s_onesc_f[:], in_=onescol_f), inc="LA", inck=16)
        p.run("sync", lambda e: e.dma_start(out=s_invN[:], in_=invN_row), inc="LA", inck=16)
        p.run("sync", lambda e: e.dma_start(out=s_trim[:], in_=trimask), inc="LA", inck=16)
        st0 = ExitStack()
        s_offw = st0.enter_context(nc.sbuf_tensor("s_offw", [128, 512], F32))
        s_offs_i = st0.enter_context(nc.sbuf_tensor("s_offs_i", [128, 512], I32))
        p.run("sync", lambda e: e.dma_start(out=s_offw[:], in_=halves), inc="LA", inck=16)
        p.wait("vector", "LA", p.cnt["LA"])
        p.run("vector", lambda e: e.memset(s_eps[:], EPS))
        p.run("vector", lambda e: e.memset(s_isq2[:], 128.0))
        p.run("vector", lambda e: e.memset(s_r127b[:], 127.0))
        v_c0 = p.run("vector", lambda e: e.memset(s_nrw[:], 0.0), inc="VE")
        p.wait("scalar", "VE", v_c0)
        v_c1 = p.run("scalar", lambda e: e.activation(
            s_isq2[:], s_isq2[:], AF.Sqrt, bias=0.0, scale=1.0), inc="AC")
        p.wait("vector", "AC", v_c1)
        p.run("vector", lambda e: e.reciprocal(s_isq[:], s_isq2[:]))
        nr_recip(s_isq[:], s_isq2[:], s_nrw[:])
        p.run("vector", lambda e: e.reciprocal(s_r127[:], s_r127b[:]))
        nr_recip(s_r127[:], s_r127b[:], s_nrw[:])
        p.run("vector", lambda e: e.tensor_copy(s_offs_i[:], s_offw[:]))
        p.run("vector", lambda e: e.tensor_copy(s_offw[:], s_offs_i[:]))
        p.run("vector", lambda e: e.tensor_scalar(
            s_offw[:], s_offw[:], -0.5, 257.0, OP.mult, OP.add))
        p.run("vector", lambda e: e.tensor_copy(s_toff[:], s_offw[:, 0:8]))
        v_toff = p.run("vector", lambda e: e.tensor_scalar(
            s_sdc[:], s_toff[:, 0:1], -1.0, 256.5, OP.mult, OP.add), inc="VE")
        st0.close()
        barrier()

        tern = {}
        NSL = 6
        TSLOT = ("SG", "SH", "SI", "SJ", "SK", "SL")
        st_tern = ExitStack()
        Et = st_tern.enter_context
        tern["buf"] = [Et(nc.sbuf_tensor(f"twb{i}", [128, 1024], F32))
                       for i in range(NSL)]
        tern["ta"] = Et(nc.sbuf_tensor("tta", [128, 1024], BF16))
        tern["tb"] = Et(nc.sbuf_tensor("ttb", [128, 1024], BF16))
        tern["tt"] = [Et(nc.sbuf_tensor(f"ttt{i}", [128, 1024], BF16)) for i in range(2)]
        tern["gv"] = [0] * NSL
        tern["tguard"] = [0, 0]

        def tern_src(kind, c):
            if kind == "up":
                row, hf = c // 2, c % 2
                return (wup[row * 128:(row + 1) * 128, hf * 1024:(hf + 1) * 1024],
                        tup_s[row * 128:(row + 1) * 128, hf * 1024:(hf + 1) * 1024])
            row, qq = c // 4, c % 4
            return (wdn[row * 128:(row + 1) * 128, qq * 1024:(qq + 1) * 1024],
                    tdn_s[row * 128:(row + 1) * 128, qq * 1024:(qq + 1) * 1024])

        tern["ld"] = [0] * NSL

        def tern_issue(units, i):
            kind, c = units[i]
            b = i % NSL
            src, _ = tern_src(kind, c)
            p.wait("sync", "VE", tern["gv"][b])
            p.wait("sync", TSLOT[b], p.cnt[TSLOT[b]])
            tern["ld"][b] = p.run(
                "sync", lambda e, b=b, src=src: e.dma_start(
                    out=tern["buf"][b][:], in_=src), inc=TSLOT[b], inck=16)

        def tern_stat(units, i):
            kind, c = units[i]
            b = i % NSL
            p.wait("vector", TSLOT[b], tern["ld"][b])
            tern["gv"][b] = p.run(
                "vector", lambda e, b=b, kind=kind, c=c: e.reduce_sum(
                    abspan[kind][:, c:c + 1], tern["buf"][b][:], AX,
                    apply_absolute_value=True), inc="VE")

        def tern_scalars(kind, psc, pbc):
            col = 0 if kind == "up" else 1
            v_cs = p.run("vector", lambda e, kind=kind, col=col: e.reduce_sum(
                colsum[:, col:col + 1], abspan[kind][:], AX), inc="VE")
            p.wait("tensor", "VE", v_cs)
            p.wait("tensor", "AC", p.cnt["AC"])
            v_m1 = p.run("tensor", lambda e, col=col, psc=psc: e.matmul(
                psc[0:1, col:col + 1], colsum[:, col:col + 1], s_onesc_f[:],
                start=True, stop=True), inc="PE")
            p.wait("vector", "PE", v_m1)
            v_ev = p.run("vector", lambda e, col=col, psc=psc: e.tensor_copy(
                tot_row[:, col:col + 1], psc[0:1, col:col + 1]), inc="VE")
            p.wait("tensor", "VE", v_ev)
            v_m2 = p.run("tensor", lambda e, col=col, pbc=pbc: e.matmul(
                pbc[:, col:col + 1], s_invN[:], tot_row[:, col:col + 1],
                start=True, stop=True), inc="PE")
            p.wait("vector", "PE", v_m2)
            p.run("vector", lambda e, kind=kind, col=col, pbc=pbc: e.tensor_copy(
                s_sw[kind][:], pbc[:, col:col + 1]))
            p.run("vector", lambda e, kind=kind: e.tensor_scalar(
                s_sw[kind][:], s_sw[kind][:], EPS, None, OP.max))
            p.run("vector", lambda e, kind=kind: e.tensor_scalar(
                s_thr[kind][:], s_sw[kind][:], 0.5, None, OP.mult))
            v_t = p.run("vector", lambda e, kind=kind: e.tensor_scalar(
                s_nthr[kind][:], s_sw[kind][:], -0.5, None, OP.mult), inc="VE")
            tern["rdy_" + kind] = v_t

        def tern_quant(units, i):
            kind, c = units[i]
            b = i % NSL
            tb2 = i % 2
            _, dst = tern_src(kind, c)
            p.wait("vector", TSLOT[b], tern["ld"][b])
            p.wait("vector", "VE", tern["rdy_" + kind])
            p.wait("vector", "TS", p.cnt["TS"] if tern["tguard"][tb2] else 0)
            p.run("vector", lambda e, b=b, kind=kind: e.tensor_scalar(
                tern["ta"][:], tern["buf"][b][:], s_thr[kind][:], None, OP.is_gt))
            p.run("vector", lambda e, b=b, kind=kind: e.tensor_scalar(
                tern["tb"][:], tern["buf"][b][:], s_nthr[kind][:], None, OP.is_lt))
            vt_ = p.run("vector", lambda e, tb2=tb2: e.tensor_tensor(
                tern["tt"][tb2][:], tern["ta"][:], tern["tb"][:], OP.subtract),
                inc="VE")
            tern["gv"][b] = vt_
            p.wait("gpsimd", "VE", vt_)
            vg = p.run("gpsimd", lambda e, tb2=tb2, dst=dst: e.dma_start(
                out=dst, in_=tern["tt"][tb2][:]), inc="TS", inck=16)
            tern["tguard"][tb2] = vg

        stat_units = [("up", c) for c in range(64)] + [("dn", c) for c in range(64)]
        quant_units = list(stat_units)
        cur = {"stat": [0, 0], "quant": [0, 0]}

        def _do(n, key, units, consume):
            for _ in range(n):
                if cur[key][1] >= len(units):
                    break
                while cur[key][0] < min(cur[key][1] + NSL, len(units)):
                    tern_issue(units, cur[key][0])
                    cur[key][0] += 1
                consume(units, cur[key][1])
                cur[key][1] += 1

        def do_stats(n):
            _do(n, "stat", stat_units, tern_stat)

        def do_quants(n):
            _do(n, "quant", quant_units, tern_quant)


        st1 = ExitStack()
        E1 = st1.enter_context
        xt = [E1(nc.sbuf_tensor(f"xt{i}", [128, D], F32)) for i in range(3)]
        dump = E1(nc.sbuf_tensor("dump", [128, D], F32))
        xlnt = [E1(nc.sbuf_tensor(f"xlnt{i}", [128, D], BF16)) for i in range(2)]

        gve = [0, 0, 0]
        gac = [0, 0, 0]
        for t in range(NT):
            b = t % 3
            p.wait("sync", "VE", gve[b])
            p.wait("sync", "AC", gac[b])
            vls = slot_load(b, xt[b][:], x_in[t * 128:(t + 1) * 128, :])
            p.wait("scalar", SLOTS[b], vls)
            va = p.run("scalar", lambda e, b=b, t=t: e.activation(
                dump[:], xt[b][:], AF.Square, bias=0.0, scale=1.0,
                accum_out=pan["sq"][:, t:t + 1]), inc="AC")
            p.wait("vector", SLOTS[b], vls)
            vr = p.run("vector", lambda e, b=b, t=t: e.reduce_sum(
                pan["sum"][:, t:t + 1], xt[b][:], AX), inc="VE")
            gve[b], gac[b] = vr, va
        v_stats_ac, v_stats_ve = p.cnt["AC"], p.cnt["VE"]

        def panel_math(P, v_red_ve, v_sq_ac):
            p.wait("vector", "AC", v_sq_ac)
            p.run("vector", lambda e: e.tensor_scalar(
                P["mean"][:], P["sum"][:], 1.0 / D, None, OP.mult))
            p.run("vector", lambda e: e.tensor_scalar(
                P["var"][:], P["sq"][:], 1.0 / D, None, OP.mult))
            p.run("vector", lambda e: e.tensor_tensor(
                P["rstd"][:], P["mean"][:], P["mean"][:], OP.mult))
            v_pv = p.run("vector", lambda e: e.tensor_tensor(
                P["var"][:], P["var"][:], P["rstd"][:], OP.subtract), inc="VE")
            p.wait("scalar", "VE", v_pv)
            v_sq = p.run("scalar", lambda e: e.activation(
                P["rstd"][:], P["var"][:], AF.Sqrt, bias=s_eps[:], scale=1.0), inc="AC")
            p.wait("vector", "AC", v_sq)
            p.run("vector", lambda e: e.reciprocal(P["rstd2"][:], P["rstd"][:]))
            nr_recip(P["rstd2"][:], P["rstd"][:], P["sum2"][:])
            p.run("vector", lambda e: e.tensor_tensor(
                P["nmr"][:], P["mean"][:], P["rstd2"][:], OP.mult))
            return p.run("vector", lambda e: e.tensor_scalar(
                P["nmr"][:], P["nmr"][:], -1.0, None, OP.mult), inc="VE")

        v_nmr = panel_math(pan, v_stats_ve, v_stats_ac)

        p.wait("scalar", "VE", v_nmr)
        xg = [0, 0]
        g2 = [v_stats_ac, v_stats_ac, v_stats_ac]
        p.wait("sync", "VE", v_stats_ve)
        for t in range(NT):
            b, ob = t % 3, t % 2
            p.wait("sync", "AC", g2[b])
            vls = slot_load(b, xt[b][:], x_in[t * 128:(t + 1) * 128, :])
            p.wait("scalar", SLOTS[b], vls)
            p.wait("scalar", "XS", p.cnt["XS"])
            va = p.run("scalar", lambda e, b=b, ob=ob, t=t: e.activation(
                xlnt[ob][:], xt[b][:], AF.Identity,
                bias=pan["nmr"][:, t:t + 1], scale=pan["rstd2"][:, t:t + 1]), inc="AC")
            g2[b] = va
            p.wait("gpsimd", "AC", va)
            vg = p.run("gpsimd", lambda e, ob=ob, t=t: e.dma_start(
                out=xln_s[t * 128:(t + 1) * 128, :], in_=xlnt[ob][:]), inc="XS", inck=16)
        v_xln_gd = p.cnt["XS"]
        st1.close()
        barrier()

        st_ckv = ExitStack()
        ckv = [st_ckv.enter_context(nc.sbuf_tensor(f"ckv{c}", [128, S], BF16))
               for c in range(4)]
        st_krp = ExitStack()
        krp = st_krp.enter_context(nc.sbuf_tensor("krp", [64, S], BF16))
        st_cq = ExitStack()
        cq = [st_cq.enter_context(nc.sbuf_tensor(f"cq{c}", [128, S], BF16))
              for c in range(12)]
        st_rope = ExitStack()
        Er = st_rope.enter_context
        s_cos = Er(nc.sbuf_tensor("s_cos", [128, S], BF16))
        s_sin = Er(nc.sbuf_tensor("s_sin", [128, S], BF16))
        ropeA = Er(nc.sbuf_tensor("ropeA", [128, 512], F32))
        ropeB = Er(nc.sbuf_tensor("ropeB", [128, 512], F32))
        st_p2ps = ExitStack()
        Ep = st_p2ps.enter_context
        pj = [Ep(nc.psum_tensor(f"pj{i}", [128, 512], F32)) for i in range(8)]
        pr = [pj[4], pj[5], pj[6], pj[7]]
        t_psc = pj[2]
        t_pbc = pj[3]

        st_xf = ExitStack()
        xf = [st_xf.enter_context(nc.sbuf_tensor(f"xf{c}", [128, S], BF16, side="right"))
              for c in range(16)]
        st_wqd = ExitStack()
        wqd_t = [st_wqd.enter_context(nc.sbuf_tensor(f"wqdt{k}", [128, QL // 2], BF16))
                 for k in range(16)]

        p.wait("sync", "XS", v_xln_gd)
        xf_ld = []
        for c in range(16):
            vls = p.run("sync", lambda e, c=c: e.dma_start(
                out=xf[c][:], in_=xln_s[:, c * 128:(c + 1) * 128],
                transpose=True), inc="XF", inck=16)
            xf_ld.append(vls)
        v_xf_all = xf_ld[-1]

        p.wait("tensor", "XF", v_xf_all)
        ev_g = [0, 0]
        for half in range(2):
            colbase = half * (QL // 2)
            p.wait("gpsimd", "PE", p.cnt["PE"])
            wld = []
            for k in range(16):
                vg = p.run("gpsimd", lambda e, k=k, colbase=colbase: e.dma_start(
                    out=wqd_t[k][:],
                    in_=wqd[k * 128:(k + 1) * 128, colbase:colbase + QL // 2]),
                    inc="WG", inck=16)
                wld.append(vg)
            p.wait("tensor", "WG", wld[-1])
            for mh in range(6):
                m = half * 6 + mh
                gg = m % 2
                p.wait("tensor", "AC", ev_g[gg])
                mms = []
                for k in range(16):
                    for n4 in range(4):
                        mms.append((pj[4 * gg + n4][:],
                                    wqd_t[k][:, mh * 128:(mh + 1) * 128],
                                    xf[k][:, n4 * 512:(n4 + 1) * 512],
                                    k == 0, k == 15))
                v_mm = mm_seq(mms)
                p.wait("scalar", "PE", v_mm)
                for n4 in range(4):
                    ev_g[gg] = p.run("scalar", lambda e, m=m, n4=n4, gg=gg: e.copy(
                        cq[m][:, n4 * 512:(n4 + 1) * 512], pj[4 * gg + n4][:]),
                        inc="AC")
                do_stats(22)
        do_stats(128)
        st_wqd.close()

        st_wkv = ExitStack()
        wkv_t = [st_wkv.enter_context(nc.sbuf_tensor(f"wkvt{k}", [128, KVL], BF16))
                 for k in range(16)]
        p.wait("gpsimd", "PE", p.cnt["PE"])
        wld2 = []
        for k in range(16):
            vg = p.run("gpsimd", lambda e, k=k: e.dma_start(
                out=wkv_t[k][:], in_=wkvd[k * 128:(k + 1) * 128, :]), inc="WG", inck=16)
            wld2.append(vg)
        p.wait("tensor", "WG", wld2[-1])
        for m in range(4):
            gg = m % 2
            p.wait("tensor", "AC", ev_g[gg])
            mms = []
            for k in range(16):
                for n4 in range(4):
                    mms.append((pj[4 * gg + n4][:],
                                wkv_t[k][:, m * 128:(m + 1) * 128],
                                xf[k][:, n4 * 512:(n4 + 1) * 512],
                                k == 0, k == 15))
            v_mm = mm_seq(mms)
            p.wait("scalar", "PE", v_mm)
            for n4 in range(4):
                ev_g[gg] = p.run("scalar", lambda e, m=m, n4=n4, gg=gg: e.copy(
                    ckv[m][:, n4 * 512:(n4 + 1) * 512], pj[4 * gg + n4][:]),
                    inc="AC")
        st_wkv.close()

        st_kr = ExitStack()
        Ek = st_kr.enter_context
        wkr_t = [Ek(nc.sbuf_tensor(f"wkrt{k}", [128, RD], BF16)) for k in range(16)]
        wkrr_t = [Ek(nc.sbuf_tensor(f"wkrrt{k}", [128, RD], BF16)) for k in range(16)]

        p.wait("gpsimd", "PE", p.cnt["PE"])
        v_cos = p.run("gpsimd", lambda e: e.dma_start(out=s_cos[:], in_=cosd), inc="CZ", inck=16)
        v_sin = p.run("gpsimd", lambda e: e.dma_start(out=s_sin[:], in_=sind), inc="CZ", inck=16)
        wld3 = []
        for k in range(16):
            vg = p.run("gpsimd", lambda e, k=k: e.dma_start(
                out=wkr_t[k][:], in_=wkr[k * 128:(k + 1) * 128, :]), inc="WG", inck=16)
            wld3.append(vg)
        p.wait("vector", "WG", wld3[-1])
        for k in range(16):
            p.run("vector", lambda e, k=k: e.tensor_scalar(
                wkrr_t[k][:, 0:32], wkr_t[k][:, 32:64], -1.0, None, OP.mult))
            if k == 15:
                v_wkrr = p.run("vector", lambda e, k=k: e.tensor_copy(
                    wkrr_t[k][:, 32:64], wkr_t[k][:, 0:32]), inc="VE")
            else:
                p.run("vector", lambda e, k=k: e.tensor_copy(
                    wkrr_t[k][:, 32:64], wkr_t[k][:, 0:32]))
        p.wait("tensor", "VE", v_wkrr)
        p.wait("tensor", "CZ", max(v_cos, v_sin))
        g_prp = [0, 0]
        p.wait("tensor", "AC", p.cnt["AC"])
        for n4 in range(4):
            pp = n4 % 2
            p.wait("tensor", "VE", g_prp[pp])
            mms = []
            for k in range(16):
                mms.append((pr[2 * pp][0:64, :], wkr_t[k][:],
                            xf[k][:, n4 * 512:(n4 + 1) * 512], k == 0, k == 15))
            for k in range(16):
                mms.append((pr[2 * pp + 1][0:64, :], wkrr_t[k][:],
                            xf[k][:, n4 * 512:(n4 + 1) * 512], k == 0, k == 15))
            v_krmm = mm_seq(mms)
            p.wait("vector", "PE", v_krmm)
            p.run("vector", lambda e, n4=n4, pp=pp: e.tensor_tensor(
                ropeA[0:64, :], pr[2 * pp][0:64, :],
                s_cos[0:64, n4 * 512:(n4 + 1) * 512], OP.mult))
            p.run("vector", lambda e, n4=n4, pp=pp: e.tensor_tensor(
                ropeB[0:64, :], pr[2 * pp + 1][0:64, :],
                s_sin[0:64, n4 * 512:(n4 + 1) * 512], OP.mult))
            v_krp = p.run("vector", lambda e, n4=n4: e.tensor_tensor(
                krp[:, n4 * 512:(n4 + 1) * 512], ropeA[0:64, :],
                ropeB[0:64, :], OP.add), inc="VE")
            g_prp[pp] = v_krp
        v_krp_done = v_krp
        st_kr.close()
        v_pe_xf_done = p.cnt["PE"]
        st_xf.close()
        do_stats(999)
        tern_scalars("up", t_psc, t_pbc)
        tern_scalars("dn", t_psc, t_pbc)

        st_qk = ExitStack()
        qh = [st_qk.enter_context(nc.sbuf_tensor(f"qh{h}", [128, S], BF16, side="right"))
              for h in range(H)]
        st_wqc = ExitStack()
        wqc_t = [st_wqc.enter_context(nc.sbuf_tensor(f"wqct{k}", [128, 1024], BF16))
                 for k in range(12)]
        p.wait("gpsimd", "PE", v_pe_xf_done)
        p.wait("gpsimd", "VE", v_krp_done)
        wld4 = []
        for k in range(12):
            vg = p.run("gpsimd", lambda e, k=k: e.dma_start(
                out=wqc_t[k][:], in_=wqc[k * 128:(k + 1) * 128, :]), inc="WG", inck=16)
            wld4.append(vg)
        p.wait("tensor", "WG", wld4[-1])
        p.wait("tensor", "AC", p.cnt["AC"])
        p.wait("tensor", "VE", tern["rdy_dn"])
        for m in range(8):
            gg = m % 2
            p.wait("tensor", "AC", ev_g[gg])
            mms = []
            for k in range(12):
                for n4 in range(4):
                    mms.append((pj[4 * gg + n4][:],
                                wqc_t[k][:, m * 128:(m + 1) * 128],
                                cq[k][:, n4 * 512:(n4 + 1) * 512],
                                k == 0, k == 11))
            v_mm = mm_seq(mms)
            p.wait("scalar", "PE", v_mm)
            for n4 in range(4):
                p.run("scalar", lambda e, m=m, n4=n4, gg=gg: e.copy(
                    qh[2 * m][0:64, n4 * 512:(n4 + 1) * 512],
                    pj[4 * gg + n4][0:64, :]))
                ev_g[gg] = p.run("scalar", lambda e, m=m, n4=n4, gg=gg: e.copy(
                    qh[2 * m + 1][0:64, n4 * 512:(n4 + 1) * 512],
                    pj[4 * gg + n4][64:128, :]), inc="AC")
            do_quants(4)
        v_pe_qc_done = p.cnt["PE"]
        st_wqc.close()

        st_wqr = ExitStack()
        Eq = st_wqr.enter_context
        wqr_t = [Eq(nc.sbuf_tensor(f"wqrt{k}", [128, 512], BF16)) for k in range(12)]
        wqrr_t = [Eq(nc.sbuf_tensor(f"wqrrt{k}", [128, 512], BF16)) for k in range(12)]
        p.wait("tensor", "AC", p.cnt["AC"])
        g_prp = [0, 0]
        step = 0
        v_half_pe = v_pe_qc_done
        for hh in range(2):
            p.wait("gpsimd", "PE", v_half_pe)
            if hh:
                p.wait("gpsimd", "VE", v_wqrr)
            wld5 = []
            for k in range(12):
                vg = p.run("gpsimd", lambda e, k=k, hh=hh: e.dma_start(
                    out=wqr_t[k][:],
                    in_=wqr[k * 128:(k + 1) * 128, hh * 512:(hh + 1) * 512]),
                    inc="WG", inck=16)
                wld5.append(vg)
            p.wait("vector", "WG", wld5[-1])
            for k in range(12):
                wt = wqr_t[k][:].rearrange("p (h t c) -> p h t c", h=8, t=2, c=32)
                rt = wqrr_t[k][:].rearrange("p (h t c) -> p h t c", h=8, t=2, c=32)
                p.run("vector", lambda e, wt=wt, rt=rt: e.tensor_scalar(
                    rt[:, :, 0, :], wt[:, :, 1, :], -1.0, None, OP.mult))
                if k == 11:
                    v_wqrr = p.run("vector", lambda e, wt=wt, rt=rt: e.tensor_copy(
                        rt[:, :, 1, :], wt[:, :, 0, :]), inc="VE")
                else:
                    p.run("vector", lambda e, wt=wt, rt=rt: e.tensor_copy(
                        rt[:, :, 1, :], wt[:, :, 0, :]))
            p.wait("tensor", "VE", v_wqrr)
            for m in range(4 * hh, 4 * hh + 4):
                mh = m - 4 * hh
                for n4 in range(4):
                    pp = step % 2
                    step += 1
                    p.wait("tensor", "VE", g_prp[pp])
                    mms = []
                    for k in range(12):
                        mms.append((pr[2 * pp][:], wqr_t[k][:, mh * 128:(mh + 1) * 128],
                                    cq[k][:, n4 * 512:(n4 + 1) * 512], k == 0, k == 11))
                    for k in range(12):
                        mms.append((pr[2 * pp + 1][:],
                                    wqrr_t[k][:, mh * 128:(mh + 1) * 128],
                                    cq[k][:, n4 * 512:(n4 + 1) * 512], k == 0, k == 11))
                    v_qr = mm_seq(mms)
                    p.wait("vector", "PE", v_qr)
                    p.run("vector", lambda e, n4=n4, pp=pp: e.tensor_tensor(
                        ropeA[:], pr[2 * pp][:],
                        s_cos[:, n4 * 512:(n4 + 1) * 512], OP.mult))
                    p.run("vector", lambda e, n4=n4, pp=pp: e.tensor_tensor(
                        ropeB[:], pr[2 * pp + 1][:],
                        s_sin[:, n4 * 512:(n4 + 1) * 512], OP.mult))
                    p.run("vector", lambda e, m=m, n4=n4: e.tensor_tensor(
                        qh[2 * m][64:128, n4 * 512:(n4 + 1) * 512],
                        ropeA[0:64, :], ropeB[0:64, :], OP.add))
                    v_qrope = p.run("vector", lambda e, m=m, n4=n4: e.tensor_tensor(
                        qh[2 * m + 1][64:128, n4 * 512:(n4 + 1) * 512],
                        ropeA[64:128, :], ropeB[64:128, :], OP.add), inc="VE")
                    g_prp[pp] = v_qrope
                    do_quants(1)
            v_half_pe = v_qr
        st_wqr.close()
        st_rope.close()
        v_pe_cq_done = p.cnt["PE"]
        st_cq.close()

        st_kh = ExitStack()
        kh = [st_kh.enter_context(nc.sbuf_tensor(f"kh{h}", [128, S], BF16, side="right"))
              for h in range(H)]
        st_wkc = ExitStack()
        wkc_t = [st_wkc.enter_context(nc.sbuf_tensor(f"wkct{k}", [128, 1024], BF16))
                 for k in range(4)]
        p.wait("gpsimd", "PE", v_pe_cq_done)
        p.wait("gpsimd", "VE", v_qrope)
        wld6 = []
        for k in range(4):
            vg = p.run("gpsimd", lambda e, k=k: e.dma_start(
                out=wkc_t[k][:], in_=wkc[k * 128:(k + 1) * 128, :]), inc="WG", inck=16)
            wld6.append(vg)
        p.wait("tensor", "WG", wld6[-1])
        p.wait("tensor", "VE", v_qrope)
        for m in range(8):
            gg = m % 2
            p.wait("tensor", "AC", ev_g[gg])
            mms = []
            for k in range(4):
                for n4 in range(4):
                    mms.append((pj[4 * gg + n4][:],
                                wkc_t[k][:, m * 128:(m + 1) * 128],
                                ckv[k][:, n4 * 512:(n4 + 1) * 512], k == 0, k == 3))
            v_mm = mm_seq(mms)
            p.wait("scalar", "PE", v_mm)
            for n4 in range(4):
                p.run("scalar", lambda e, m=m, n4=n4, gg=gg: e.copy(
                    kh[2 * m][0:64, n4 * 512:(n4 + 1) * 512],
                    pj[4 * gg + n4][0:64, :]))
                ev_g[gg] = p.run("scalar", lambda e, m=m, n4=n4, gg=gg: e.copy(
                    kh[2 * m + 1][0:64, n4 * 512:(n4 + 1) * 512],
                    pj[4 * gg + n4][64:128, :]), inc="AC")
        p.wait("vector", "VE", v_krp_done)
        for h in range(H):
            if h == H - 1:
                v_khcp = p.run("vector", lambda e, h=h: e.tensor_copy(
                    kh[h][64:128, :], krp[:]), inc="VE")
            else:
                p.run("vector", lambda e, h=h: e.tensor_copy(
                    kh[h][64:128, :], krp[:]))
        st_wkc.close()

        st_wv = ExitStack()
        Ev = st_wv.enter_context
        wv_t = [Ev(nc.sbuf_tensor(f"wvt{k}", [128, D], BF16)) for k in range(4)]
        vout = [Ev(nc.sbuf_tensor(f"vout{i}", [128, D], BF16)) for i in range(2)]
        p.wait("gpsimd", "PE", p.cnt["PE"])
        wld7 = []
        for k in range(4):
            vg = p.run("gpsimd", lambda e, k=k: e.dma_start(
                out=wv_t[k][:], in_=wv[k * 128:(k + 1) * 128, :]), inc="WG", inck=16)
            wld7.append(vg)
        p.wait("tensor", "WG", wld7[-1])
        VSS = ("VSA", "VSB")
        VTS = ("VTA", "VTB")
        for t in range(NT):
            ob = t % 2
            gg = t % 2
            p.wait("tensor", "AC", ev_g[gg])
            mms = []
            for k in range(4):
                for n4 in range(4):
                    mms.append((pj[4 * gg + n4][:],
                                ckv[k][:, t * 128:(t + 1) * 128],
                                wv_t[k][:, n4 * 512:(n4 + 1) * 512], k == 0, k == 3))
            v_mm = mm_seq(mms)
            p.wait("scalar", "PE", v_mm)
            p.wait("scalar", VSS[ob], 16 * (t // 2))
            for n4 in range(4):
                ev_g[gg] = p.run("scalar", lambda e, ob=ob, n4=n4, gg=gg: e.copy(
                    vout[ob][:, n4 * 512:(n4 + 1) * 512], pj[4 * gg + n4][:]),
                    inc="AC")
            p.wait("gpsimd", "AC", ev_g[gg])
            p.run("gpsimd", lambda e, ob=ob, t=t: e.dma_start(
                out=v_s[t * 128:(t + 1) * 128, :], in_=vout[ob][:]),
                inc=VSS[ob], inck=16)
        st_wv.close()
        st_krp.close()
        st_ckv.close()
        st_p2ps.close()
        barrier(exclude=("VSA", "VSB", "SG", "SH", "SI", "SJ", "SK", "SL", "TS"),
                overrides={"VE": v_khcp})
        v_p2_pe = p.cnt["PE"]
        v_p2_ve = v_khcp
        v_p2_ac = p.cnt["AC"]

        st3 = ExitStack()
        E3 = st3.enter_context
        vth = [E3(nc.sbuf_tensor(f"vth{i}", [128, S], BF16)) for i in range(4)]
        ae = [E3(nc.sbuf_tensor(f"ae{i}", [128, 512], BF16)) for i in range(4)]
        srow = E3(nc.sbuf_tensor("srow", [1, 512], F32))
        srr = [E3(nc.sbuf_tensor("srr0", [1, 512], F32))]
        rbc = [E3(nc.sbuf_tensor("rbc0", [128, 512], F32))]
        cxo = [E3(nc.sbuf_tensor(f"cxo{i}", [128, 512], BF16)) for i in range(2)]
        psc = [E3(nc.psum_tensor(f"psc{i}", [128, 512], F32)) for i in range(4)]
        pctx = [E3(nc.psum_tensor(f"pctx{i}", [128, 512], F32)) for i in range(2)]
        psr = [E3(nc.psum_tensor(f"psr{i}", [128, 512], F32)) for i in range(2)]

        VTS4 = ("VTA", "VTB", "VTC", "VTD")
        p.wait("sync", "VE", v_p2_ve)
        p.wait("sync", "AC", v_p2_ac)
        p.wait("sync", "VSA", 16 * 8)
        p.wait("sync", "VSB", 16 * 8)

        def vth_issue(h, pe_guard):
            sl = VTS4[h % 4]
            p.wait("sync", "PE", pe_guard)
            for t in range(NT):
                p.wait("sync", sl, p.cnt[sl])
                p.run("sync", lambda e, h=h, t=t: e.dma_start(
                    out=vth[h % 4][:, t * 128:(t + 1) * 128],
                    in_=v_s[t * 128:(t + 1) * 128, h * 128:(h + 1) * 128]),
                    inc=sl, inck=16)

        for hh in range(4):
            vth_issue(hh, 0)

        g_psc = [0, 0, 0, 0]
        g_ae = [0, 0, 0, 0]
        g_pctx = [0, 0]
        g_psr = [0, 0]
        g_psrv = [0, 0]
        g_pbc = 0
        g_cxo = [0, 0]
        g_srr = 0
        p.wait("tensor", "VE", v_p2_ve)
        p.wait("tensor", "AC", v_p2_ac)
        for hp in range(8):
            heads = (2 * hp, 2 * hp + 1)
            for h in heads:
                p.wait("tensor", VTS4[h % 4], 256 * (h // 4 + 1))
            for s4 in range(4):
                Q0 = 512 * s4
                nblocks = 4 * (s4 + 1)
                U = [dict(h=heads[ui], hb=ui, sd={}) for ui in range(2)]

                def emit_score(un, j, Q0=Q0, s4=s4):
                    r = 128 * (j - 4 * s4) if j >= 4 * s4 else 0
                    sb = 2 * un["hb"] + j % 2
                    p.wait("tensor", "AC", g_psc[sb])
                    if j == 0:
                        p.wait("tensor", "VE", g_pctx[un["hb"]])
                        p.wait("tensor", "VE", g_psrv[un["hb"]])
                        p.wait("tensor", "AC", g_psr[un["hb"]])
                    un["sd"][j] = p.run(
                        "tensor", lambda e, h=un["h"], j=j, sb=sb, r=r, Q0=Q0:
                        e.matmul(psc[sb][:, r:512],
                                 kh[h][:, j * 128:(j + 1) * 128],
                                 qh[h][:, Q0 + r:Q0 + 512],
                                 start=True, stop=True), inc="PE")

                for un in U:
                    emit_score(un, 0)
                if nblocks > 1:
                    for un in U:
                        emit_score(un, 1)
                for j in range(nblocks):
                    r = 128 * (j - 4 * s4) if j >= 4 * s4 else 0
                    for un in U:
                        hb = un["hb"]
                        sb = 2 * hb + j % 2
                        ab = 2 * hb + j % 2
                        p.wait("scalar", "PE", max(un["sd"][j], g_ae[ab]))
                        v_exp = p.run("scalar", lambda e, sb=sb, r=r, ab=ab:
                                      e.activation(ae[ab][:, r:512],
                                                   psc[sb][:, r:512],
                                                   AF.Exp, bias=0.0,
                                                   scale=s_isq[:]), inc="AC")
                        g_psc[sb] = v_exp
                        if j >= 4 * s4:
                            p.wait("vector", "AC", v_exp)
                            v_msk = p.run("vector", lambda e, ab=ab, r=r:
                                          e.tensor_tensor(ae[ab][:, r:r + 128],
                                                          ae[ab][:, r:r + 128],
                                                          s_trim[:], OP.mult),
                                          inc="VE")
                        else:
                            v_msk = None
                        if j + 2 < nblocks:
                            emit_score(un, j + 2)
                        if v_msk is not None:
                            p.wait("tensor", "VE", v_msk)
                        else:
                            p.wait("tensor", "AC", v_exp)
                        st_, sp_ = (j == 0), (j == nblocks - 1)
                        p.run("tensor", lambda e, hb4=un["h"] % 4, j=j, hb=hb, r=r,
                              ab=ab, st_=st_, sp_=sp_:
                              e.matmul(pctx[hb][:, r:512],
                                       vth[hb4][:, j * 128:(j + 1) * 128],
                                       ae[ab][:, r:512], start=st_, stop=sp_))
                        v_bmm = p.run("tensor", lambda e, hb=hb, r=r, ab=ab,
                                      st_=st_, sp_=sp_:
                                      e.matmul(psr[hb][0:1, r:512], s_ones_bf[:],
                                               ae[ab][:, r:512],
                                               start=st_, stop=sp_), inc="PE")
                        g_ae[ab] = v_bmm
                        un["bmm"] = v_bmm
                for un in U:
                    hb = un["hb"]
                    h = un["h"]
                    p.wait("scalar", "PE", max(un["bmm"], g_srr))
                    p.run("scalar", lambda e, hb=hb: e.activation(
                        srow[:], psr[hb][0:1, :], AF.Ln, bias=0.0, scale=1.0))
                    v_rcp = p.run("scalar", lambda e: e.activation(
                        srr[0][:], srow[:], AF.Exp, bias=0.0, scale=-1.0), inc="AC")
                    g_psr[hb] = v_rcp
                    p.wait("tensor", "AC", v_rcp)
                    p.wait("tensor", "VE", g_pbc if g_pbc else 0)
                    v_bc = p.run("tensor", lambda e, hb=hb: e.matmul(
                        psr[hb][:, :], s_onesr_f[:], srr[0][:],
                        start=True, stop=True), inc="PE")
                    g_srr = v_bc
                    p.wait("vector", "PE", v_bc)
                    v_rbc = p.run("vector", lambda e, hb=hb: e.tensor_copy(
                        rbc[0][:], psr[hb][:, :]), inc="VE")
                    g_pbc = v_rbc
                    g_psrv[hb] = v_rbc
                    p.wait("vector", "CS", p.cnt["CS"] if g_cxo[hb] else 0)
                    v_cx = p.run("vector", lambda e, hb=hb: e.tensor_tensor(
                        cxo[hb][:], pctx[hb][:], rbc[0][:], OP.mult), inc="VE")
                    g_pctx[hb] = v_cx
                    p.wait("gpsimd", "VE", v_cx)
                    v_gd = p.run("gpsimd", lambda e, h=h, s4=s4, hb=hb: e.dma_start(
                        out=ctx_s[h * 128:(h + 1) * 128, s4 * 512:(s4 + 1) * 512],
                        in_=cxo[hb][:]), inc="CS", inck=16)
                    g_cxo[hb] = v_gd
                    do_quants(1)
            for h in heads:
                if h + 4 < H:
                    vth_issue(h + 4, p.cnt["PE"])
        v_ctx_gd = p.cnt["CS"]
        v_p3_pe = p.cnt["PE"]
        v_p3_ve = p.cnt["VE"]
        v_p3_ac = p.cnt["AC"]
        st3.close()
        st_kh.close()
        st_qk.close()
        barrier(exclude=("SG", "SH", "SI", "SJ", "SK", "SL", "TS"))

        st4 = ExitStack()
        E4 = st4.enter_context
        ctxc = [E4(nc.sbuf_tensor(f"ctxc{c}", [128, S], BF16)) for c in range(16)]
        woc = [E4(nc.sbuf_tensor(f"woc{c}", [128, D], BF16)) for c in range(16)]
        s_bo = E4(nc.sbuf_tensor("s_bo", [1, D], F32))
        xt4 = [E4(nc.sbuf_tensor(f"xt4_{i}", [128, D], F32)) for i in range(2)]
        x1t = [E4(nc.sbuf_tensor(f"x1t{i}", [128, D], F32)) for i in range(2)]
        pout = [E4(nc.psum_tensor(f"pout{i}", [128, 512], F32)) for i in range(8)]

        p.wait("sync", "CS", v_ctx_gd)
        p.wait("sync", "VE", v_p3_ve)
        p.wait("sync", "AC", v_p3_ac)
        ctx_ld = []
        for c in range(16):
            vls = p.run("sync", lambda e, c=c: e.dma_start(
                out=ctxc[c][:], in_=ctx_s[c * 128:(c + 1) * 128, :]), inc="CT", inck=16)
            ctx_ld.append(vls)
        v_bo_ld = p.run("sync", lambda e: e.dma_start(out=s_bo[:], in_=bo),
                        inc="LA", inck=16)
        p.wait("gpsimd", "PE", v_p3_pe)
        wo_ld = []
        for c in range(16):
            vg = p.run("gpsimd", lambda e, c=c: e.dma_start(
                out=woc[c][:], in_=wo[c * 128:(c + 1) * 128, :]), inc="WG", inck=16)
            wo_ld.append(vg)
        p.wait("tensor", "LA", v_bo_ld)
        p.wait("tensor", "PE", v_p3_pe)
        p.wait("tensor", "VE", v_p3_ve)
        p.wait("tensor", "AC", v_p3_ac)

        g_pout = [0, 0]
        g_x1 = [0, 0]
        g_xt4 = [0, 0]
        for t in range(NT):
            pg = t % 2
            p.wait("sync", "AC", g_xt4[pg])
            v_xl = slot_load(pg, xt4[pg][:], x_in[t * 128:(t + 1) * 128, :])
            p.wait("tensor", "WG", wo_ld[15])
            p.wait("tensor", "CT", ctx_ld[15])
            p.wait("tensor", "VE", g_pout[pg])
            mms = []
            for dv in range(16):
                for o4 in range(4):
                    mms.append((pout[4 * pg + o4][:],
                                ctxc[dv][:, t * 128:(t + 1) * 128],
                                woc[dv][:, o4 * 512:(o4 + 1) * 512],
                                dv == 0, False))
            for o4 in range(4):
                mms.append((pout[4 * pg + o4][:], s_onesr_f[:],
                            s_bo[:, o4 * 512:(o4 + 1) * 512], False, True))
            v_wmm = mm_seq(mms)
            p.wait("vector", "PE", v_wmm)
            p.wait("vector", SLOTS[pg], v_xl)
            p.wait("vector", "X1S", p.cnt["X1S"] if g_x1[pg] else 0)
            for o4 in range(4):
                p.run("vector", lambda e, pg=pg, o4=o4: e.tensor_tensor(
                    x1t[pg][:, o4 * 512:(o4 + 1) * 512], pout[4 * pg + o4][:],
                    xt4[pg][:, o4 * 512:(o4 + 1) * 512], OP.add))
            v_x1 = p.run("vector", lambda e, pg=pg, t=t: e.reduce_sum(
                pan2["sum"][:, t:t + 1], x1t[pg][:], AX), inc="VE")
            g_pout[pg] = v_x1
            p.wait("scalar", "VE", v_x1)
            v_sq2 = p.run("scalar", lambda e, pg=pg, t=t: e.activation(
                xt4[pg][:], x1t[pg][:], AF.Square, bias=0.0, scale=1.0,
                accum_out=pan2["sq"][:, t:t + 1]), inc="AC")
            g_xt4[pg] = v_sq2
            p.wait("gpsimd", "AC", v_sq2)
            v_gd = p.run("gpsimd", lambda e, pg=pg, t=t: e.dma_start(
                out=x1_s[t * 128:(t + 1) * 128, :], in_=x1t[pg][:]), inc="X1S", inck=16)
            g_x1[pg] = v_gd
            do_quants(3)
        do_quants(999)
        v_x1_gd = p.cnt["X1S"]
        v_nmr2 = panel_math(pan2, p.cnt["VE"], p.cnt["AC"])
        v_p4_pe = p.cnt["PE"]
        v_p4_ve = p.cnt["VE"]
        v_p4_ac = p.cnt["AC"]
        st4.close()
        st_tern.close()
        barrier()

        def ffn_half(hf, v_p4_pe, v_p4_ve, v_p4_ac, v_x1_gd, v_nmr2):
            T0 = hf * 8
            R0 = hf * 1024
            st5a = ExitStack()
            E5 = st5a.enter_context
            x1l = [E5(nc.sbuf_tensor(f"x1l{i}_{hf}", [128, D], F32)) for i in range(2)]
            xln2 = [E5(nc.sbuf_tensor(f"xln2_{i}_{hf}", [128, D], F32)) for i in range(8)]
            tq = [E5(nc.sbuf_tensor(f"tq0_{hf}", [128, D], F32))] * 2
            tqk = [E5(nc.sbuf_tensor(f"tqk0_{hf}", [128, D], F32))] * 2
            tqd = E5(nc.sbuf_tensor(f"tqd_{hf}", [128, D], BF16))
            tqc = E5(nc.sbuf_tensor(f"tqc_{hf}", [128, D], BF16))
            tqi = [E5(nc.sbuf_tensor(f"tqi0_{hf}", [128, D], I32))] * 2
            xqb = [E5(nc.sbuf_tensor(f"xqb{i}_{hf}", [128, D], BF16)) for i in range(2)]

            p.wait("scalar", "VE", v_nmr2)
            p.wait("sync", "X1S", v_x1_gd)
            p.wait("sync", "PE", v_p4_pe)
            p.wait("sync", "VE", v_p4_ve)
            p.wait("sync", "AC", v_p4_ac)
            p.wait("sync", "ST", p.cnt["ST"])
            g_x1l = [0, 0]
            for t in range(8):
                gt = T0 + t
                b = t % 2
                p.wait("sync", "VE", g_x1l[b])
                v_xls = slot_load(b, x1l[b][:], x1_s[gt * 128:(gt + 1) * 128, :])
                p.wait("scalar", SLOTS[b], v_xls)
                v_n = p.run("scalar", lambda e, b=b, t=t, gt=gt: e.activation(
                    xln2[t][:], x1l[b][:], AF.Identity,
                    bias=pan2["nmr"][:, gt:gt + 1],
                    scale=pan2["rstd2"][:, gt:gt + 1]), inc="AC")
                p.wait("vector", "AC", v_n)
                v_mx = p.run("vector", lambda e, t=t: e.reduce_max(
                    maxpan1[:, t:t + 1], xln2[t][:], AX,
                    apply_absolute_value=True), inc="VE")
                g_x1l[b] = v_mx
            p.run("vector", lambda e: e.tensor_scalar(
                maxpan1[:], maxpan1[:], EPS, None, OP.max))
            p.run("vector", lambda e: e.reciprocal(panwk[:], maxpan1[:]))
            nr_recip(panwk[:], maxpan1[:], evacpan1[:])
            p.run("vector", lambda e: e.tensor_scalar(
                scalepan1[:], panwk[:], 127.0, None, OP.mult))
            p.run("vector", lambda e: e.tensor_scalar(
                evacpan1[:], maxpan1[:], s_r127[:], None, OP.mult))
            v_pan1 = p.run("vector", lambda e: e.tensor_scalar(
                evacpan1[:], evacpan1[:], s_sw["up"][:], None, OP.mult), inc="VE")

            g_xqb = [0, 0]
            g_tq = [0, 0]
            p.wait("scalar", "VE", max(v_toff, v_pan1))
            v_q = 0
            for t in range(8):
                gt = T0 + t
                b = t % 2
                p.wait("scalar", "VE", v_q)
                p.run("scalar", lambda e, t=t, b=b: e.activation(
                    tq[b][:], xln2[t][:], AF.Identity,
                    bias=s_toff[:, 0:1], scale=scalepan1[:, t:t + 1]))
                p.run("scalar", lambda e, b=b: e.copy(tqi[b][:], tq[b][:]))
                v_t1 = p.run("scalar", lambda e, b=b: e.copy(tqk[b][:], tqi[b][:]),
                             inc="AC")
                p.wait("vector", "AC", v_t1)
                p.wait("vector", "QS", p.cnt["QS"] if g_xqb[b] else 0)
                p.run("vector", lambda e, b=b: e.tensor_sub(tqd[:], tq[b][:], tqk[b][:]))
                p.run("vector", lambda e: e.tensor_scalar(
                    tqd[:], tqd[:], s_sdc[:], None, OP.add))
                p.run("vector", lambda e: e.tensor_scalar(
                    tqc[:], tqd[:], 1.0, None, OP.is_gt))
                p.run("vector", lambda e: e.scalar_tensor_tensor(
                    tqc[:], tqd[:], 2.0, tqc[:], OP.is_gt, OP.add))
                p.run("vector", lambda e: e.scalar_tensor_tensor(
                    tqc[:], tqd[:], 3.0, tqc[:], OP.is_gt, OP.add))
                p.run("vector", lambda e: e.scalar_tensor_tensor(
                    tqc[:], tqd[:], 0.0, tqc[:], OP.is_ge, OP.add))
                p.run("vector", lambda e: e.scalar_tensor_tensor(
                    tqc[:], tqd[:], -1.0, tqc[:], OP.is_ge, OP.add))
                p.run("vector", lambda e: e.scalar_tensor_tensor(
                    tqc[:], tqd[:], -2.0, tqc[:], OP.is_ge, OP.add))
                v_q = p.run("vector", lambda e, b=b: e.scalar_tensor_tensor(
                    xqb[b][:], tqc[:], -259.0, tqk[b][:], OP.add, OP.add), inc="VE")
                g_tq[b] = v_q
                p.wait("gpsimd", "VE", v_q)
                g_xqb[b] = p.run("gpsimd", lambda e, b=b, gt=gt: e.dma_start(
                    out=xq_s[gt * 128:(gt + 1) * 128, :], in_=xqb[b][:]),
                    inc="QS", inck=16)
                if t == 3:
                    v_xq_mid = p.cnt["QS"]
            v_xq_gd = p.cnt["QS"]
            v_a_ve = p.cnt["VE"]
            v_a_ac = p.cnt["AC"]
            st5a.close()

            st5b = ExitStack()
            E5 = st5b.enter_context
            ht = [E5(nc.sbuf_tensor(f"ht{t}_{hf}", [128, FI], F32)) for t in range(8)]
            trelu = E5(nc.sbuf_tensor(f"trelu_{hf}", [128, 512], F32))
            pup = [E5(nc.psum_tensor(f"pup{i}_{hf}", [128, 512], F32)) for i in range(4)]
            st5bio = ExitStack()
            E5i = st5bio.enter_context
            xqf = [E5i(nc.sbuf_tensor(f"xqf{c}_{hf}", [128, 1024], BF16)) for c in range(16)]
            rup = [[E5i(nc.sbuf_tensor(f"rup{g}_{k}_{hf}", [128, 512], BF16))
                    for k in range(16)] for g in range(2)]

            def rup_load(ip):
                g = ip % 2
                sem = ("RWA", "RWB")[g]
                last = None
                for hc in range(16):
                    last = p.run("sync", lambda e, g=g, hc=hc, ip=ip: e.dma_start(
                        out=rup[g][hc][:],
                        in_=tup_s[ip * 512:(ip + 1) * 512, hc * 128:(hc + 1) * 128],
                        transpose=True), inc=sem, inck=16)
                return last

            rup_rdy = [0, 0]
            rup_pe_hist = [0, 0]
            rup_rdy[0] = rup_load(0)
            rup_rdy[1] = rup_load(1)
            xqf_ld = []
            p.wait("sync", "QS", v_xq_mid)
            for c in range(16):
                vls = p.run("sync", lambda e, c=c, R0=R0: e.dma_start(
                    out=xqf[c][:, 0:512], in_=xq_s[R0:R0 + 512, c * 128:(c + 1) * 128],
                    transpose=True), inc="XQF", inck=16)
                xqf_ld.append(vls)
            p.wait("sync", "QS", v_xq_gd)
            p.wait("sync", "VE", v_a_ve)
            p.wait("sync", "AC", v_a_ac)
            for c in range(16):
                vls = p.run("sync", lambda e, c=c, R0=R0: e.dma_start(
                    out=xqf[c][:, 512:1024],
                    in_=xq_s[R0 + 512:R0 + 1024, c * 128:(c + 1) * 128],
                    transpose=True), inc="XQF", inck=16)
                xqf_ld.append(vls)

            g_pup = [0, 0, 0, 0]
            p.wait("tensor", "PE", v_p4_pe)
            p.wait("tensor", "VE", v_p4_ve)
            p.wait("tensor", "AC", v_p4_ac)
            for ip in range(8):
                g = ip % 2
                if 2 <= ip + 1 < 8:
                    p.wait("sync", "PE", rup_pe_hist[(ip + 1) % 2])
                    rup_rdy[(ip + 1) % 2] = rup_load(ip + 1)
                for t in range(8):
                    pb = t % 4
                    p.wait("tensor", ("RWA", "RWB")[g], rup_rdy[g])
                    p.wait("tensor", "XQF", xqf_ld[-1])
                    p.wait("tensor", "AC", g_pup[pb])
                    mms = []
                    for hc in range(16):
                        mms.append((pup[pb][:], xqf[hc][:, t * 128:(t + 1) * 128],
                                    rup[g][hc][:], hc == 0, hc == 15))
                    v_up = mm_seq(mms)
                    p.wait("scalar", "PE", v_up)
                    p.run("scalar", lambda e, pb=pb, t=t: e.activation(
                        trelu[:], pup[pb][:], AF.Relu, bias=0.0,
                        scale=evacpan1[:, t:t + 1]))
                    v_hsq = p.run("scalar", lambda e, t=t, ip=ip: e.activation(
                        ht[t][:, ip * 512:(ip + 1) * 512], trelu[:], AF.Square,
                        bias=0.0, scale=1.0), inc="AC")
                    g_pup[pb] = v_hsq
                    p.wait("vector", "AC", v_hsq)
                    v_hm = p.run("vector", lambda e, t=t, ip=ip: e.reduce_max(
                        hmaxpan[:, t * 8 + ip:t * 8 + ip + 1],
                        ht[t][:, ip * 512:(ip + 1) * 512], AX), inc="VE")
                rup_pe_hist[g] = p.cnt["PE"]

            for t in range(8):
                p.run("vector", lambda e, t=t: e.reduce_max(
                    max2pan[:, t:t + 1], hmaxpan[:, t * 8:(t + 1) * 8], AX))
            p.run("vector", lambda e: e.tensor_scalar(
                max2pan[:], max2pan[:], EPS, None, OP.max))
            p.run("vector", lambda e: e.reciprocal(panwk[:], max2pan[:]))
            nr_recip(panwk[:], max2pan[:], evacpan2[:])
            p.run("vector", lambda e: e.tensor_scalar(
                scalepan2[:], panwk[:], 127.0, None, OP.mult))
            p.run("vector", lambda e: e.tensor_scalar(
                evacpan2[:], max2pan[:], s_r127[:], None, OP.mult))
            v_pan2 = p.run("vector", lambda e: e.tensor_scalar(
                evacpan2[:], evacpan2[:], s_sw["dn"][:], None, OP.mult), inc="VE")

            st5bio.close()
            FH = FI // 2
            tq2 = [E5(nc.sbuf_tensor(f"tq2{i}_{hf}", [128, FH], F32)) for i in range(2)]
            tq2k = [E5(nc.sbuf_tensor(f"tq2k{i}_{hf}", [128, FH], F32)) for i in range(2)]
            tq2d = E5(nc.sbuf_tensor(f"tq2d_{hf}", [128, FH], BF16))
            tq2c = E5(nc.sbuf_tensor(f"tq2c_{hf}", [128, FH], BF16))
            tq2i = [E5(nc.sbuf_tensor(f"tq2i{i}_{hf}", [128, FH], I32)) for i in range(2)]
            hqb = [E5(nc.sbuf_tensor(f"hqb{i}_{hf}", [128, FH], BF16)) for i in range(2)]
            g_hqb = [0, 0]
            g_tq2 = [0, 0]
            p.wait("scalar", "VE", v_pan2)
            v_q2 = 0
            for idx in range(16):
                t, q2 = idx // 2, idx % 2
                gt = T0 + t
                b = idx % 2
                p.wait("scalar", "VE", g_tq2[b])
                p.run("scalar", lambda e, t=t, q2=q2, b=b: e.activation(
                    tq2[b][:], ht[t][:, q2 * FH:(q2 + 1) * FH], AF.Identity,
                    bias=s_toff[:, 0:1], scale=scalepan2[:, t:t + 1]))
                p.run("scalar", lambda e, b=b: e.copy(tq2i[b][:], tq2[b][:]))
                v_t1 = p.run("scalar", lambda e, b=b: e.copy(tq2k[b][:], tq2i[b][:]),
                             inc="AC")
                p.wait("vector", "AC", v_t1)
                p.wait("vector", "HS", p.cnt["HS"] if g_hqb[b] else 0)
                p.run("vector", lambda e, b=b: e.tensor_sub(
                    tq2d[:], tq2[b][:], tq2k[b][:]))
                p.run("vector", lambda e: e.tensor_scalar(
                    tq2d[:], tq2d[:], s_sdc[:], None, OP.add))
                p.run("vector", lambda e: e.tensor_scalar(
                    tq2c[:], tq2d[:], 1.0, None, OP.is_gt))
                p.run("vector", lambda e: e.scalar_tensor_tensor(
                    tq2c[:], tq2d[:], 2.0, tq2c[:], OP.is_gt, OP.add))
                p.run("vector", lambda e: e.scalar_tensor_tensor(
                    tq2c[:], tq2d[:], 3.0, tq2c[:], OP.is_gt, OP.add))
                p.run("vector", lambda e: e.scalar_tensor_tensor(
                    tq2c[:], tq2d[:], 0.0, tq2c[:], OP.is_ge, OP.add))
                p.run("vector", lambda e: e.scalar_tensor_tensor(
                    tq2c[:], tq2d[:], -1.0, tq2c[:], OP.is_ge, OP.add))
                p.run("vector", lambda e: e.scalar_tensor_tensor(
                    tq2c[:], tq2d[:], -2.0, tq2c[:], OP.is_ge, OP.add))
                v_q2 = p.run("vector", lambda e, b=b: e.scalar_tensor_tensor(
                    hqb[b][:], tq2c[:], -259.0, tq2k[b][:], OP.add, OP.add), inc="VE")
                g_tq2[b] = v_q2
                p.wait("gpsimd", "VE", v_q2)
                g_hqb[b] = p.run("gpsimd", lambda e, b=b, gt=gt, q2=q2: e.dma_start(
                    out=hq_s[gt * 128:(gt + 1) * 128, q2 * FH:(q2 + 1) * FH],
                    in_=hqb[b][:]), inc="HS", inck=16)
                if idx == 7:
                    v_hq_mid = p.cnt["HS"]
                if idx == 11:
                    v_hq_t5 = p.cnt["HS"]
            v_hq_gd = p.cnt["HS"]
            v_b_ve = p.cnt["VE"]
            v_b_ac = p.cnt["AC"]
            v_b_pe = p.cnt["PE"]
            st5b.close()

            st5c = ExitStack()
            E5 = st5c.enter_context
            hqf = [E5(nc.sbuf_tensor(f"hqf{c}_{hf}", [128, 1024], BF16)) for c in range(32)]
            rdn = [[E5(nc.sbuf_tensor(f"rdn{g}_{k}_{hf}", [128, 512], BF16))
                    for k in range(32)] for g in range(2)]
            x1c = [E5(nc.sbuf_tensor(f"x1c{i}_{hf}", [128, 512], F32)) for i in range(2)]
            obuf = [E5(nc.sbuf_tensor(f"obuf{i}_{hf}", [128, 512], F32)) for i in range(2)]
            tdo = E5(nc.sbuf_tensor(f"tdo_{hf}", [128, 512], F32))
            pdn = [E5(nc.psum_tensor(f"pdn{i}_{hf}", [128, 512], F32)) for i in range(4)]

            def rdn_load(op4):
                g = op4 % 2
                sem = ("RWA", "RWB")[g]
                last = None
                for ic in range(32):
                    last = p.run("sync", lambda e, g=g, ic=ic, op4=op4: e.dma_start(
                        out=rdn[g][ic][:],
                        in_=tdn_s[op4 * 512:(op4 + 1) * 512, ic * 128:(ic + 1) * 128],
                        transpose=True), inc=sem, inck=16)
                return last

            hqf_ld = []
            p.wait("sync", "HS", v_hq_mid)
            for c in range(32):
                vls = p.run("sync", lambda e, c=c, R0=R0: e.dma_start(
                    out=hqf[c][:, 0:512], in_=hq_s[R0:R0 + 512, c * 128:(c + 1) * 128],
                    transpose=True), inc="XQF", inck=16)
                hqf_ld.append(vls)
            p.wait("sync", "HS", v_hq_t5)
            rdn_rdy = [0, 0]
            rdn_pe_hist = [v_b_pe, v_b_pe]
            rdn_rdy[0] = rdn_load(0)
            p.wait("sync", "HS", v_hq_gd)
            p.wait("sync", "VE", v_b_ve)
            p.wait("sync", "AC", v_b_ac)
            for c in range(32):
                vls = p.run("sync", lambda e, c=c, R0=R0: e.dma_start(
                    out=hqf[c][:, 512:1024],
                    in_=hq_s[R0 + 512:R0 + 1024, c * 128:(c + 1) * 128],
                    transpose=True), inc="XQF", inck=16)
                hqf_ld.append(vls)

            g_pdn = [0, 0, 0, 0]
            g_ob = [0, 0]
            g_x1c = [0, 0]
            g_tdo = 0
            p.wait("tensor", "AC", v_b_ac)
            for op4 in range(4):
                g = op4 % 2
                if op4 + 1 < 4:
                    p.wait("sync", "PE", rdn_pe_hist[(op4 + 1) % 2])
                    rdn_rdy[(op4 + 1) % 2] = rdn_load(op4 + 1)
                for t in range(8):
                    gt = T0 + t
                    b = t % 2
                    pb = t % 4
                    p.wait("sync", "VE", g_x1c[b])
                    v_x1r = slot_load(b, x1c[b][:],
                                      x1_s[gt * 128:(gt + 1) * 128,
                                           op4 * 512:(op4 + 1) * 512])
                    p.wait("tensor", ("RWA", "RWB")[g], rdn_rdy[g])
                    p.wait("tensor", "XQF", hqf_ld[-1])
                    p.wait("tensor", "AC", g_pdn[pb])
                    mms = []
                    for ic in range(32):
                        mms.append((pdn[pb][:], hqf[ic][:, t * 128:(t + 1) * 128],
                                    rdn[g][ic][:], ic == 0, ic == 31))
                    v_dn = mm_seq(mms)
                    p.wait("scalar", "PE", v_dn)
                    p.wait("scalar", "VE", g_tdo)
                    v_ev = p.run("scalar", lambda e, pb=pb, t=t: e.activation(
                        tdo[:], pdn[pb][:], AF.Copy, bias=0.0,
                        scale=evacpan2[:, t:t + 1]), inc="AC")
                    g_pdn[pb] = v_ev
                    p.wait("vector", "AC", v_ev)
                    p.wait("vector", SLOTS[b], v_x1r)
                    p.wait("vector", "ST", p.cnt["ST"] if g_ob[b] else 0)
                    v_o = p.run("vector", lambda e, b=b: e.tensor_tensor(
                        obuf[b][:], tdo[:], x1c[b][:], OP.add), inc="VE")
                    g_x1c[b] = v_o
                    g_tdo = v_o
                    p.wait("gpsimd", "VE", v_o)
                    v_st = p.run("gpsimd", lambda e, b=b, gt=gt, op4=op4: e.dma_start(
                        out=out_t[gt * 128:(gt + 1) * 128,
                                  op4 * 512:(op4 + 1) * 512], in_=obuf[b][:]),
                        inc="ST", inck=16)
                    g_ob[b] = v_st
                rdn_pe_hist[g] = p.cnt["PE"]
            v_dn_ve = p.cnt["VE"]
            v_dn_ac = p.cnt["AC"]
            v_dn_pe = p.cnt["PE"]
            st5c.close()
            return v_dn_pe, v_dn_ve, v_dn_ac

        for hf in range(2):
            v_p4_pe, v_p4_ve, v_p4_ac = ffn_half(
                hf, v_p4_pe, v_p4_ve, v_p4_ac, v_x1_gd, v_nmr2)
            v_x1_gd = 0

        p.wait("sync", "ST", p.cnt["ST"])

        for name in p.cnt:
            p.handles[name] = E(nc.semaphore(name))
        block = E(nc.Block())

        for ename, attr in (("sync", block.sync), ("scalar", block.scalar),
                            ("vector", block.vector), ("tensor", block.tensor),
                            ("gpsimd", block.gpsimd)):
            def section(eng, ename=ename):
                for f in p.ops[ename]:
                    f(eng)
            attr(section)

    return nc


def _host_constants():
    import jax
    import jax.numpy as jnp
    ROPE_BASE = 10000.0
    with jax.default_device(jax.local_devices(backend="cpu")[0]):
        inv_freq = 1.0 / ROPE_BASE ** (jnp.arange(0, RD, 2, dtype=jnp.float32) / RD)
        t = jnp.arange(S, dtype=jnp.float32)
        freqs = jnp.outer(t, inv_freq)
        emb = jnp.concatenate((freqs, freqs), axis=-1)
        cos = np.asarray(jnp.cos(emb).astype(jnp.float32))
        sin = np.asarray(jnp.sin(emb).astype(jnp.float32))
    cosd = np.zeros((128, S), np.float32)
    sind = np.zeros((128, S), np.float32)
    cosd[0:64] = cos.T
    cosd[64:128] = cos.T
    sind[0:64] = sin.T
    sind[64:128] = sin.T
    tm = np.triu(np.ones((128, 128), np.float32)).astype(ml_dtypes.bfloat16)
    return {
        "cosd": cosd,
        "sind": sind,
        "trimask": tm,
        "onescol_bf": np.ones((128, 1), ml_dtypes.bfloat16),
        "onesrow_f": np.ones((1, 128), np.float32),
        "onescol_f": np.ones((128, 1), np.float32),
        "invN_row": np.full((1, 128), 1.0 / (FI * D), np.float32),
        "halves": np.full((128, 512), 1.5, np.float32),
    }


_CACHE = {}


def kernel(L0_attn_x, L0_attn_W_kv_down, L0_attn_W_q_down, L0_attn_W_kc,
           L0_attn_W_v, L0_attn_W_qc, L0_attn_W_kr, L0_attn_W_qr, L0_attn_W_o,
           L0_attn_b_o, L0_ffn_w_up, L0_ffn_w_down):
    if "nc" not in _CACHE:
        _CACHE["nc"] = build_nc()
    nc = _CACHE["nc"]
    x = np.ascontiguousarray(np.asarray(L0_attn_x, np.float32))
    common = dict(
        wkvd=np.ascontiguousarray(np.asarray(L0_attn_W_kv_down, np.float32)),
        wqd=np.ascontiguousarray(np.asarray(L0_attn_W_q_down, np.float32)),
        wkc=np.ascontiguousarray(np.asarray(L0_attn_W_kc, np.float32)),
        wv=np.ascontiguousarray(np.asarray(L0_attn_W_v, np.float32)),
        wqc=np.ascontiguousarray(np.asarray(L0_attn_W_qc, np.float32)),
        wkr=np.ascontiguousarray(np.asarray(L0_attn_W_kr, np.float32)),
        wqr=np.ascontiguousarray(np.asarray(L0_attn_W_qr, np.float32)),
        wo=np.ascontiguousarray(np.asarray(L0_attn_W_o, np.float32)),
        bo=np.ascontiguousarray(np.asarray(L0_attn_b_o, np.float32).reshape(1, D)),
        wup=np.ascontiguousarray(np.asarray(L0_ffn_w_up, np.float32)),
        wdn=np.ascontiguousarray(np.asarray(L0_ffn_w_down, np.float32)),
        **_host_constants(),
    )
    in_maps = []
    for b in range(8):
        m = dict(common)
        m["x_in"] = np.ascontiguousarray(x[b])
        in_maps.append(m)
    res = run_bass_kernel_spmd(nc, in_maps, core_ids=list(range(8)))
    return np.stack([np.asarray(res.results[b]["out"]) for b in range(8)], axis=0)


# version 20 (final)
# speedup vs baseline: 1.0770x; 1.0209x over previous
import numpy as np
import ml_dtypes
from contextlib import ExitStack

import concourse.bass as bass
import concourse.mybir as mybir
from concourse.alu_op_type import AluOpType as OP
from concourse.bass_utils import run_bass_kernel_spmd

F32 = mybir.dt.float32
BF16 = mybir.dt.bfloat16
I32 = mybir.dt.int32
AX = mybir.AxisListType.X
AF = mybir.ActivationFunctionType

S = 2048
D = 2048
H = 16
RD = 64
KVL = 512
QL = 1536
FI = 4096
NT = 16
INV_SQRT_HD = float(1.0 / np.sqrt(128.0))
EPS = 1e-5

ENGS = ("sync", "scalar", "vector", "tensor", "gpsimd")


class Prog:
    def __init__(self):
        self.ops = {e: [] for e in ENGS}
        self.cnt = {k: 0 for k in ("PE", "AC", "VE", "ST", "SA", "SB", "SC",
                                   "LA", "XF", "VT", "CT", "RW", "WG",
                                   "XS", "VS", "CS", "X1S", "QS", "HS", "TS", "CZ",
                                   "RWA", "RWB", "XQF", "XQG", "SE", "SF", "SG", "SH",
                                   "SI", "SJ", "SK", "SL",
                                   "VSA", "VSB", "VTA", "VTB", "VTC", "VTD")}
        self.handles = {}

    def emit(self, eng, fn):
        self.ops[eng].append(fn)

    def bump(self, name, k):
        self.cnt[name] = self.cnt.get(name, 0) + k
        return self.cnt[name]

    def wait(self, eng, name, val):
        if not val or val <= 0:
            return
        h = self.handles

        def f(e, name=name, val=val):
            e.wait_ge(h[name], val)
        self.emit(eng, f)

    AUTO = {"vector": "VE", "scalar": "AC"}

    def run(self, eng, fn, inc=None, inck=1):
        h = self.handles
        own = self.AUTO.get(eng)
        if own is not None:
            prev = self.cnt[own]
            if inc is None:
                inc, inck = own, 1
            assert inc == own, f"{eng} op must inc {own}"
            v = self.bump(inc, inck)

            def f(e, fn=fn, prev=prev, own=own, inck=inck):
                if prev > 0:
                    e.wait_ge(h[own], prev)
                fn(e).then_inc(h[own], inck)
            self.emit(eng, f)
            return v
        if inc is None:
            self.emit(eng, fn)
            return None
        v = self.bump(inc, inck)

        def f(e, fn=fn, inc=inc, inck=inck):
            fn(e).then_inc(h[inc], inck)
        self.emit(eng, f)
        return v


def build_nc():
    nc = bass.Bass("TRN2", target_bir_lowering=False)

    x_in = nc.dram_tensor("x_in", [S, D], F32, kind="ExternalInput").ap()
    wkvd = nc.dram_tensor("wkvd", [D, KVL], F32, kind="ExternalInput").ap()
    wqd = nc.dram_tensor("wqd", [D, QL], F32, kind="ExternalInput").ap()
    wkc = nc.dram_tensor("wkc", [KVL, 1024], F32, kind="ExternalInput").ap()
    wv = nc.dram_tensor("wv", [KVL, D], F32, kind="ExternalInput").ap()
    wqc = nc.dram_tensor("wqc", [QL, 1024], F32, kind="ExternalInput").ap()
    wkr = nc.dram_tensor("wkr", [D, RD], F32, kind="ExternalInput").ap()
    wqr = nc.dram_tensor("wqr", [QL, 1024], F32, kind="ExternalInput").ap()
    wo = nc.dram_tensor("wo", [D, D], F32, kind="ExternalInput").ap()
    bo = nc.dram_tensor("bo", [1, D], F32, kind="ExternalInput").ap()
    wup = nc.dram_tensor("wup", [FI, D], F32, kind="ExternalInput").ap()
    wdn = nc.dram_tensor("wdn", [D, FI], F32, kind="ExternalInput").ap()
    cosd = nc.dram_tensor("cosd", [128, S], F32, kind="ExternalInput").ap()
    sind = nc.dram_tensor("sind", [128, S], F32, kind="ExternalInput").ap()
    trimask = nc.dram_tensor("trimask", [128, 128], BF16, kind="ExternalInput").ap()
    onescol_bf = nc.dram_tensor("onescol_bf", [128, 1], BF16, kind="ExternalInput").ap()
    onesrow_f = nc.dram_tensor("onesrow_f", [1, 128], F32, kind="ExternalInput").ap()
    onescol_f = nc.dram_tensor("onescol_f", [128, 1], F32, kind="ExternalInput").ap()
    invN_row = nc.dram_tensor("invN_row", [1, 128], F32, kind="ExternalInput").ap()
    halves = nc.dram_tensor("halves", [128, 512], F32, kind="ExternalInput").ap()

    xln_s = nc.dram_tensor("xln_s", [S, D], BF16, kind="Internal").ap()
    v_s = nc.dram_tensor("v_s", [S, D], BF16, kind="Internal").ap()
    ctx_s = nc.dram_tensor("ctx_s", [D, S], BF16, kind="Internal").ap()
    x1_s = nc.dram_tensor("x1_s", [S, D], F32, kind="Internal").ap()
    xq_s = nc.dram_tensor("xq_s", [S, D], BF16, kind="Internal").ap()
    hq_s = nc.dram_tensor("hq_s", [S, FI], BF16, kind="Internal").ap()
    tup_s = nc.dram_tensor("tup_s", [FI, D], BF16, kind="Internal").ap()
    tdn_s = nc.dram_tensor("tdn_s", [D, FI], BF16, kind="Internal").ap()
    out_t = nc.dram_tensor("out", [S, D], F32, kind="ExternalOutput").ap()

    p = Prog()

    def mm_seq(mms, inc="PE"):
        v = None
        for i, (o, l, r, st, sp) in enumerate(mms):
            last = (i == len(mms) - 1)
            if last:
                v = p.run("tensor", lambda e, o=o, l=l, r=r, st=st, sp=sp:
                          e.matmul(o, l, r, start=st, stop=sp), inc=inc)
            else:
                p.run("tensor", lambda e, o=o, l=l, r=r, st=st, sp=sp:
                      e.matmul(o, l, r, start=st, stop=sp))
        return v

    SLOTS = ("SA", "SB", "SC")

    def slot_load(slot, out_ap, in_ap, queue="sync"):
        sem = SLOTS[slot]
        p.wait(queue, sem, p.cnt[sem])
        return p.run(queue, lambda e, o=out_ap, i=in_ap: e.dma_start(out=o, in_=i),
                     inc=sem, inck=16)

    def nr_recip(rt, mt, tmp, steps=2):
        for _ in range(steps):
            p.run("vector", lambda e: e.tensor_mul(tmp, mt, rt))
            p.run("vector", lambda e: e.tensor_scalar(
                tmp, tmp, -1.0, 2.0, OP.mult, OP.add))
            p.run("vector", lambda e: e.tensor_mul(rt, rt, tmp))

    def barrier(exclude=(), overrides=None):
        snap = dict(p.cnt)
        if overrides:
            snap.update(overrides)
        for eng in ENGS:
            for sn in snap:
                if sn in exclude:
                    continue
                p.wait(eng, sn, snap[sn])

    with ExitStack() as top:
        E = top.enter_context
        s_toff = E(nc.sbuf_tensor("s_toff", [128, 8], F32))
        s_ones_bf = E(nc.sbuf_tensor("s_ones_bf", [128, 1], BF16))
        s_onesr_f = E(nc.sbuf_tensor("s_onesr_f", [1, 128], F32))
        s_onesc_f = E(nc.sbuf_tensor("s_onesc_f", [128, 1], F32))
        s_invN = E(nc.sbuf_tensor("s_invN", [1, 128], F32))
        s_trim = E(nc.sbuf_tensor("s_trim", [128, 128], BF16))
        s_eps = E(nc.sbuf_tensor("s_eps", [128, 1], F32))
        s_sdc = E(nc.sbuf_tensor("s_sdc", [128, 1], F32))
        s_isq = E(nc.sbuf_tensor("s_isq", [128, 1], F32))
        s_isq2 = E(nc.sbuf_tensor("s_isq2", [128, 1], F32))
        s_r127 = E(nc.sbuf_tensor("s_r127", [128, 1], F32))
        s_r127b = E(nc.sbuf_tensor("s_r127b", [128, 1], F32))
        s_nrw = E(nc.sbuf_tensor("s_nrw", [128, 1], F32))
        pan = {k: E(nc.sbuf_tensor("pan_" + k, [128, NT], F32))
               for k in ("sum", "sum2", "sq", "mean", "var", "rstd", "rstd2", "nmr")}
        pan2 = {k: E(nc.sbuf_tensor("pan2_" + k, [128, NT], F32))
                for k in ("sum", "sum2", "sq", "mean", "var", "rstd", "rstd2", "nmr")}
        s_thr = {k: E(nc.sbuf_tensor(f"s_thr_{k}", [128, 1], F32)) for k in ("up", "dn")}
        s_nthr = {k: E(nc.sbuf_tensor(f"s_nthr_{k}", [128, 1], F32)) for k in ("up", "dn")}
        s_sw = {k: E(nc.sbuf_tensor(f"s_sw_{k}", [128, 1], F32)) for k in ("up", "dn")}
        abspan = {"up": E(nc.sbuf_tensor("abspan_up", [128, 64], F32)),
                  "dn": E(nc.sbuf_tensor("abspan_dn", [128, 64], F32))}
        colsum = E(nc.sbuf_tensor("colsum", [128, 2], F32))
        tot_row = E(nc.sbuf_tensor("tot_row", [1, 2], F32))
        maxpan1 = E(nc.sbuf_tensor("maxpan1", [128, 8], F32))
        scalepan1 = E(nc.sbuf_tensor("scalepan1", [128, 8], F32))
        evacpan1 = E(nc.sbuf_tensor("evacpan1", [128, 8], F32))
        hmaxpan = E(nc.sbuf_tensor("hmaxpan", [128, 64], F32))
        max2pan = E(nc.sbuf_tensor("max2pan", [128, 8], F32))
        scalepan2 = E(nc.sbuf_tensor("scalepan2", [128, 8], F32))
        evacpan2 = E(nc.sbuf_tensor("evacpan2", [128, 8], F32))
        panwk = E(nc.sbuf_tensor("panwk", [128, 8], F32))

        p.run("sync", lambda e: e.dma_start(out=s_ones_bf[:], in_=onescol_bf), inc="LA", inck=16)
        p.run("sync", lambda e: e.dma_start(out=s_onesr_f[:], in_=onesrow_f), inc="LA", inck=16)
        p.run("sync", lambda e: e.dma_start(out=s_onesc_f[:], in_=onescol_f), inc="LA", inck=16)
        p.run("sync", lambda e: e.dma_start(out=s_invN[:], in_=invN_row), inc="LA", inck=16)
        p.run("sync", lambda e: e.dma_start(out=s_trim[:], in_=trimask), inc="LA", inck=16)
        st0 = ExitStack()
        s_offw = st0.enter_context(nc.sbuf_tensor("s_offw", [128, 512], F32))
        s_offs_i = st0.enter_context(nc.sbuf_tensor("s_offs_i", [128, 512], I32))
        p.run("sync", lambda e: e.dma_start(out=s_offw[:], in_=halves), inc="LA", inck=16)
        p.wait("vector", "LA", p.cnt["LA"])
        p.run("vector", lambda e: e.memset(s_eps[:], EPS))
        p.run("vector", lambda e: e.memset(s_isq2[:], 128.0))
        p.run("vector", lambda e: e.memset(s_r127b[:], 127.0))
        v_c0 = p.run("vector", lambda e: e.memset(s_nrw[:], 0.0), inc="VE")
        p.wait("scalar", "VE", v_c0)
        v_c1 = p.run("scalar", lambda e: e.activation(
            s_isq2[:], s_isq2[:], AF.Sqrt, bias=0.0, scale=1.0), inc="AC")
        p.wait("vector", "AC", v_c1)
        p.run("vector", lambda e: e.reciprocal(s_isq[:], s_isq2[:]))
        nr_recip(s_isq[:], s_isq2[:], s_nrw[:])
        p.run("vector", lambda e: e.reciprocal(s_r127[:], s_r127b[:]))
        nr_recip(s_r127[:], s_r127b[:], s_nrw[:])
        p.run("vector", lambda e: e.tensor_copy(s_offs_i[:], s_offw[:]))
        p.run("vector", lambda e: e.tensor_copy(s_offw[:], s_offs_i[:]))
        p.run("vector", lambda e: e.tensor_scalar(
            s_offw[:], s_offw[:], -0.5, 257.0, OP.mult, OP.add))
        p.run("vector", lambda e: e.tensor_copy(s_toff[:], s_offw[:, 0:8]))
        v_toff = p.run("vector", lambda e: e.tensor_scalar(
            s_sdc[:], s_toff[:, 0:1], -1.0, 256.5, OP.mult, OP.add), inc="VE")
        st0.close()
        barrier()

        tern = {}
        NSL = 6
        TSLOT = ("SG", "SH", "SI", "SJ", "SK", "SL")
        st_tern = ExitStack()
        Et = st_tern.enter_context
        tern["buf"] = [Et(nc.sbuf_tensor(f"twb{i}", [128, 1024], F32))
                       for i in range(NSL)]
        tern["ta"] = Et(nc.sbuf_tensor("tta", [128, 1024], BF16))
        tern["tb"] = Et(nc.sbuf_tensor("ttb", [128, 1024], BF16))
        tern["tt"] = [Et(nc.sbuf_tensor(f"ttt{i}", [128, 1024], BF16)) for i in range(2)]
        tern["gv"] = [0] * NSL
        tern["tguard"] = [0, 0]

        def tern_src(kind, c):
            if kind == "up":
                row, hf = c // 2, c % 2
                return (wup[row * 128:(row + 1) * 128, hf * 1024:(hf + 1) * 1024],
                        tup_s[row * 128:(row + 1) * 128, hf * 1024:(hf + 1) * 1024])
            row, qq = c // 4, c % 4
            return (wdn[row * 128:(row + 1) * 128, qq * 1024:(qq + 1) * 1024],
                    tdn_s[row * 128:(row + 1) * 128, qq * 1024:(qq + 1) * 1024])

        tern["ld"] = [0] * NSL

        def tern_issue(units, i):
            kind, c = units[i]
            b = i % NSL
            src, _ = tern_src(kind, c)
            p.wait("sync", "VE", tern["gv"][b])
            p.wait("sync", TSLOT[b], p.cnt[TSLOT[b]])
            tern["ld"][b] = p.run(
                "sync", lambda e, b=b, src=src: e.dma_start(
                    out=tern["buf"][b][:], in_=src), inc=TSLOT[b], inck=16)

        def tern_stat(units, i):
            kind, c = units[i]
            b = i % NSL
            p.wait("vector", TSLOT[b], tern["ld"][b])
            tern["gv"][b] = p.run(
                "vector", lambda e, b=b, kind=kind, c=c: e.reduce_sum(
                    abspan[kind][:, c:c + 1], tern["buf"][b][:], AX,
                    apply_absolute_value=True), inc="VE")

        def tern_scalars(kind, psc, pbc):
            col = 0 if kind == "up" else 1
            v_cs = p.run("vector", lambda e, kind=kind, col=col: e.reduce_sum(
                colsum[:, col:col + 1], abspan[kind][:], AX), inc="VE")
            p.wait("tensor", "VE", v_cs)
            p.wait("tensor", "AC", p.cnt["AC"])
            v_m1 = p.run("tensor", lambda e, col=col, psc=psc: e.matmul(
                psc[0:1, col:col + 1], colsum[:, col:col + 1], s_onesc_f[:],
                start=True, stop=True), inc="PE")
            p.wait("vector", "PE", v_m1)
            v_ev = p.run("vector", lambda e, col=col, psc=psc: e.tensor_copy(
                tot_row[:, col:col + 1], psc[0:1, col:col + 1]), inc="VE")
            p.wait("tensor", "VE", v_ev)
            v_m2 = p.run("tensor", lambda e, col=col, pbc=pbc: e.matmul(
                pbc[:, col:col + 1], s_invN[:], tot_row[:, col:col + 1],
                start=True, stop=True), inc="PE")
            p.wait("vector", "PE", v_m2)
            p.run("vector", lambda e, kind=kind, col=col, pbc=pbc: e.tensor_copy(
                s_sw[kind][:], pbc[:, col:col + 1]))
            p.run("vector", lambda e, kind=kind: e.tensor_scalar(
                s_sw[kind][:], s_sw[kind][:], EPS, None, OP.max))
            p.run("vector", lambda e, kind=kind: e.tensor_scalar(
                s_thr[kind][:], s_sw[kind][:], 0.5, None, OP.mult))
            v_t = p.run("vector", lambda e, kind=kind: e.tensor_scalar(
                s_nthr[kind][:], s_sw[kind][:], -0.5, None, OP.mult), inc="VE")
            tern["rdy_" + kind] = v_t

        def tern_quant(units, i):
            kind, c = units[i]
            b = i % NSL
            tb2 = i % 2
            _, dst = tern_src(kind, c)
            p.wait("vector", TSLOT[b], tern["ld"][b])
            p.wait("vector", "VE", tern["rdy_" + kind])
            p.wait("vector", "TS", p.cnt["TS"] if tern["tguard"][tb2] else 0)
            p.run("vector", lambda e, b=b, kind=kind: e.tensor_scalar(
                tern["ta"][:], tern["buf"][b][:], s_thr[kind][:], None, OP.is_gt))
            p.run("vector", lambda e, b=b, kind=kind: e.tensor_scalar(
                tern["tb"][:], tern["buf"][b][:], s_nthr[kind][:], None, OP.is_lt))
            vt_ = p.run("vector", lambda e, tb2=tb2: e.tensor_tensor(
                tern["tt"][tb2][:], tern["ta"][:], tern["tb"][:], OP.subtract),
                inc="VE")
            tern["gv"][b] = vt_
            p.wait("gpsimd", "VE", vt_)
            vg = p.run("gpsimd", lambda e, tb2=tb2, dst=dst: e.dma_start(
                out=dst, in_=tern["tt"][tb2][:]), inc="TS", inck=16)
            tern["tguard"][tb2] = vg

        stat_units = [("up", c) for c in range(64)] + [("dn", c) for c in range(64)]
        quant_units = list(stat_units)
        cur = {"stat": [0, 0], "quant": [0, 0]}

        def _do(n, key, units, consume):
            for _ in range(n):
                if cur[key][1] >= len(units):
                    break
                while cur[key][0] < min(cur[key][1] + NSL, len(units)):
                    tern_issue(units, cur[key][0])
                    cur[key][0] += 1
                consume(units, cur[key][1])
                cur[key][1] += 1

        def do_stats(n):
            _do(n, "stat", stat_units, tern_stat)

        def do_quants(n):
            _do(n, "quant", quant_units, tern_quant)


        st1 = ExitStack()
        E1 = st1.enter_context
        xt = [E1(nc.sbuf_tensor(f"xt{i}", [128, D], F32)) for i in range(NT)]
        dump = E1(nc.sbuf_tensor("dump", [128, D], F32))
        xlnt = [E1(nc.sbuf_tensor(f"xlnt{i}", [128, D], BF16)) for i in range(2)]

        for t in range(NT):
            vls = slot_load(t % 3, xt[t][:], x_in[t * 128:(t + 1) * 128, :])
            p.wait("scalar", SLOTS[t % 3], vls)
            p.run("scalar", lambda e, t=t: e.activation(
                dump[:], xt[t][:], AF.Square, bias=0.0, scale=1.0,
                accum_out=pan["sq"][:, t:t + 1]), inc="AC")
            p.wait("vector", SLOTS[t % 3], vls)
            p.run("vector", lambda e, t=t: e.reduce_sum(
                pan["sum"][:, t:t + 1], xt[t][:], AX), inc="VE")
        v_stats_ac, v_stats_ve = p.cnt["AC"], p.cnt["VE"]

        def panel_math(P, v_red_ve, v_sq_ac):
            p.wait("vector", "AC", v_sq_ac)
            p.run("vector", lambda e: e.tensor_scalar(
                P["mean"][:], P["sum"][:], 1.0 / D, None, OP.mult))
            p.run("vector", lambda e: e.tensor_scalar(
                P["var"][:], P["sq"][:], 1.0 / D, None, OP.mult))
            p.run("vector", lambda e: e.tensor_tensor(
                P["rstd"][:], P["mean"][:], P["mean"][:], OP.mult))
            v_pv = p.run("vector", lambda e: e.tensor_tensor(
                P["var"][:], P["var"][:], P["rstd"][:], OP.subtract), inc="VE")
            p.wait("scalar", "VE", v_pv)
            v_sq = p.run("scalar", lambda e: e.activation(
                P["rstd"][:], P["var"][:], AF.Sqrt, bias=s_eps[:], scale=1.0), inc="AC")
            p.wait("vector", "AC", v_sq)
            p.run("vector", lambda e: e.reciprocal(P["rstd2"][:], P["rstd"][:]))
            nr_recip(P["rstd2"][:], P["rstd"][:], P["sum2"][:])
            p.run("vector", lambda e: e.tensor_tensor(
                P["nmr"][:], P["mean"][:], P["rstd2"][:], OP.mult))
            return p.run("vector", lambda e: e.tensor_scalar(
                P["nmr"][:], P["nmr"][:], -1.0, None, OP.mult), inc="VE")

        v_nmr = panel_math(pan, v_stats_ve, v_stats_ac)

        p.wait("scalar", "VE", v_nmr)
        for t in range(NT):
            ob = t % 2
            p.wait("scalar", "XS", p.cnt["XS"])
            va = p.run("scalar", lambda e, ob=ob, t=t: e.activation(
                xlnt[ob][:], xt[t][:], AF.Identity,
                bias=pan["nmr"][:, t:t + 1], scale=pan["rstd2"][:, t:t + 1]), inc="AC")
            p.wait("gpsimd", "AC", va)
            p.run("gpsimd", lambda e, ob=ob, t=t: e.dma_start(
                out=xln_s[t * 128:(t + 1) * 128, :], in_=xlnt[ob][:]), inc="XS", inck=16)
        v_xln_gd = p.cnt["XS"]
        st1.close()
        barrier()

        st_ckv = ExitStack()
        ckv = [st_ckv.enter_context(nc.sbuf_tensor(f"ckv{c}", [128, S], BF16))
               for c in range(4)]
        st_krp = ExitStack()
        krp = st_krp.enter_context(nc.sbuf_tensor("krp", [64, S], BF16))
        st_cq = ExitStack()
        cq = [st_cq.enter_context(nc.sbuf_tensor(f"cq{c}", [128, S], BF16))
              for c in range(12)]
        st_rope = ExitStack()
        Er = st_rope.enter_context
        s_cos = Er(nc.sbuf_tensor("s_cos", [128, S], BF16))
        s_sin = Er(nc.sbuf_tensor("s_sin", [128, S], BF16))
        ropeA = Er(nc.sbuf_tensor("ropeA", [128, 512], F32))
        ropeB = Er(nc.sbuf_tensor("ropeB", [128, 512], F32))
        st_p2ps = ExitStack()
        Ep = st_p2ps.enter_context
        pj = [Ep(nc.psum_tensor(f"pj{i}", [128, 512], F32)) for i in range(8)]
        pr = [pj[4], pj[5], pj[6], pj[7]]
        t_psc = pj[2]
        t_pbc = pj[3]

        st_xf = ExitStack()
        xf = [st_xf.enter_context(nc.sbuf_tensor(f"xf{c}", [128, S], BF16, side="right"))
              for c in range(16)]
        st_wqd = ExitStack()
        wqd_t = [st_wqd.enter_context(nc.sbuf_tensor(f"wqdt{k}", [128, QL // 2], BF16))
                 for k in range(16)]

        p.wait("sync", "XS", v_xln_gd)
        xf_ld = []
        for c in range(16):
            vls = p.run("sync", lambda e, c=c: e.dma_start(
                out=xf[c][:], in_=xln_s[:, c * 128:(c + 1) * 128],
                transpose=True), inc="XF", inck=16)
            xf_ld.append(vls)
        v_xf_all = xf_ld[-1]

        p.wait("tensor", "XF", v_xf_all)
        ev_g = [0, 0]
        for half in range(2):
            colbase = half * (QL // 2)
            p.wait("gpsimd", "PE", p.cnt["PE"])
            wld = []
            for k in range(16):
                vg = p.run("gpsimd", lambda e, k=k, colbase=colbase: e.dma_start(
                    out=wqd_t[k][:],
                    in_=wqd[k * 128:(k + 1) * 128, colbase:colbase + QL // 2]),
                    inc="WG", inck=16)
                wld.append(vg)
            p.wait("tensor", "WG", wld[-1])
            for mh in range(6):
                m = half * 6 + mh
                gg = m % 2
                p.wait("tensor", "AC", ev_g[gg])
                mms = []
                for k in range(16):
                    for n4 in range(4):
                        mms.append((pj[4 * gg + n4][:],
                                    wqd_t[k][:, mh * 128:(mh + 1) * 128],
                                    xf[k][:, n4 * 512:(n4 + 1) * 512],
                                    k == 0, k == 15))
                v_mm = mm_seq(mms)
                p.wait("scalar", "PE", v_mm)
                for n4 in range(4):
                    ev_g[gg] = p.run("scalar", lambda e, m=m, n4=n4, gg=gg: e.copy(
                        cq[m][:, n4 * 512:(n4 + 1) * 512], pj[4 * gg + n4][:]),
                        inc="AC")
                do_stats(22)
        do_stats(128)
        st_wqd.close()

        st_wkv = ExitStack()
        wkv_t = [st_wkv.enter_context(nc.sbuf_tensor(f"wkvt{k}", [128, KVL], BF16))
                 for k in range(16)]
        p.wait("gpsimd", "PE", p.cnt["PE"])
        wld2 = []
        for k in range(16):
            vg = p.run("gpsimd", lambda e, k=k: e.dma_start(
                out=wkv_t[k][:], in_=wkvd[k * 128:(k + 1) * 128, :]), inc="WG", inck=16)
            wld2.append(vg)
        p.wait("tensor", "WG", wld2[-1])
        for m in range(4):
            gg = m % 2
            p.wait("tensor", "AC", ev_g[gg])
            mms = []
            for k in range(16):
                for n4 in range(4):
                    mms.append((pj[4 * gg + n4][:],
                                wkv_t[k][:, m * 128:(m + 1) * 128],
                                xf[k][:, n4 * 512:(n4 + 1) * 512],
                                k == 0, k == 15))
            v_mm = mm_seq(mms)
            p.wait("scalar", "PE", v_mm)
            for n4 in range(4):
                ev_g[gg] = p.run("scalar", lambda e, m=m, n4=n4, gg=gg: e.copy(
                    ckv[m][:, n4 * 512:(n4 + 1) * 512], pj[4 * gg + n4][:]),
                    inc="AC")
        st_wkv.close()

        st_kr = ExitStack()
        Ek = st_kr.enter_context
        wkr_t = [Ek(nc.sbuf_tensor(f"wkrt{k}", [128, RD], BF16)) for k in range(16)]
        wkrr_t = [Ek(nc.sbuf_tensor(f"wkrrt{k}", [128, RD], BF16)) for k in range(16)]

        p.wait("gpsimd", "PE", p.cnt["PE"])
        v_cos = p.run("gpsimd", lambda e: e.dma_start(out=s_cos[:], in_=cosd), inc="CZ", inck=16)
        v_sin = p.run("gpsimd", lambda e: e.dma_start(out=s_sin[:], in_=sind), inc="CZ", inck=16)
        wld3 = []
        for k in range(16):
            vg = p.run("gpsimd", lambda e, k=k: e.dma_start(
                out=wkr_t[k][:], in_=wkr[k * 128:(k + 1) * 128, :]), inc="WG", inck=16)
            wld3.append(vg)
        p.wait("vector", "WG", wld3[-1])
        for k in range(16):
            p.run("vector", lambda e, k=k: e.tensor_scalar(
                wkrr_t[k][:, 0:32], wkr_t[k][:, 32:64], -1.0, None, OP.mult))
            if k == 15:
                v_wkrr = p.run("vector", lambda e, k=k: e.tensor_copy(
                    wkrr_t[k][:, 32:64], wkr_t[k][:, 0:32]), inc="VE")
            else:
                p.run("vector", lambda e, k=k: e.tensor_copy(
                    wkrr_t[k][:, 32:64], wkr_t[k][:, 0:32]))
        p.wait("tensor", "VE", v_wkrr)
        p.wait("tensor", "CZ", max(v_cos, v_sin))
        g_prp = [0, 0]
        p.wait("tensor", "AC", p.cnt["AC"])
        for n4 in range(4):
            pp = n4 % 2
            p.wait("tensor", "VE", g_prp[pp])
            mms = []
            for k in range(16):
                mms.append((pr[2 * pp][0:64, :], wkr_t[k][:],
                            xf[k][:, n4 * 512:(n4 + 1) * 512], k == 0, k == 15))
            for k in range(16):
                mms.append((pr[2 * pp + 1][0:64, :], wkrr_t[k][:],
                            xf[k][:, n4 * 512:(n4 + 1) * 512], k == 0, k == 15))
            v_krmm = mm_seq(mms)
            p.wait("vector", "PE", v_krmm)
            p.run("vector", lambda e, n4=n4, pp=pp: e.tensor_tensor(
                ropeA[0:64, :], pr[2 * pp][0:64, :],
                s_cos[0:64, n4 * 512:(n4 + 1) * 512], OP.mult))
            p.run("vector", lambda e, n4=n4, pp=pp: e.tensor_tensor(
                ropeB[0:64, :], pr[2 * pp + 1][0:64, :],
                s_sin[0:64, n4 * 512:(n4 + 1) * 512], OP.mult))
            v_krp = p.run("vector", lambda e, n4=n4: e.tensor_tensor(
                krp[:, n4 * 512:(n4 + 1) * 512], ropeA[0:64, :],
                ropeB[0:64, :], OP.add), inc="VE")
            g_prp[pp] = v_krp
        v_krp_done = v_krp
        st_kr.close()
        v_pe_xf_done = p.cnt["PE"]
        st_xf.close()
        do_stats(999)
        tern_scalars("up", t_psc, t_pbc)
        tern_scalars("dn", t_psc, t_pbc)

        st_qk = ExitStack()
        qh = [st_qk.enter_context(nc.sbuf_tensor(f"qh{h}", [128, S], BF16, side="right"))
              for h in range(H)]
        st_wqc = ExitStack()
        wqc_t = [st_wqc.enter_context(nc.sbuf_tensor(f"wqct{k}", [128, 1024], BF16))
                 for k in range(12)]
        p.wait("gpsimd", "PE", v_pe_xf_done)
        p.wait("gpsimd", "VE", v_krp_done)
        wld4 = []
        for k in range(12):
            vg = p.run("gpsimd", lambda e, k=k: e.dma_start(
                out=wqc_t[k][:], in_=wqc[k * 128:(k + 1) * 128, :]), inc="WG", inck=16)
            wld4.append(vg)
        p.wait("tensor", "WG", wld4[-1])
        p.wait("tensor", "AC", p.cnt["AC"])
        p.wait("tensor", "VE", tern["rdy_dn"])
        for m in range(8):
            gg = m % 2
            p.wait("tensor", "AC", ev_g[gg])
            mms = []
            for k in range(12):
                for n4 in range(4):
                    mms.append((pj[4 * gg + n4][:],
                                wqc_t[k][:, m * 128:(m + 1) * 128],
                                cq[k][:, n4 * 512:(n4 + 1) * 512],
                                k == 0, k == 11))
            v_mm = mm_seq(mms)
            p.wait("scalar", "PE", v_mm)
            for n4 in range(4):
                p.run("scalar", lambda e, m=m, n4=n4, gg=gg: e.copy(
                    qh[2 * m][0:64, n4 * 512:(n4 + 1) * 512],
                    pj[4 * gg + n4][0:64, :]))
                ev_g[gg] = p.run("scalar", lambda e, m=m, n4=n4, gg=gg: e.copy(
                    qh[2 * m + 1][0:64, n4 * 512:(n4 + 1) * 512],
                    pj[4 * gg + n4][64:128, :]), inc="AC")
            do_quants(4)
        v_pe_qc_done = p.cnt["PE"]
        st_wqc.close()

        st_wqr = ExitStack()
        Eq = st_wqr.enter_context
        wqr_t = [Eq(nc.sbuf_tensor(f"wqrt{k}", [128, 512], BF16)) for k in range(12)]
        wqrr_t = [Eq(nc.sbuf_tensor(f"wqrrt{k}", [128, 512], BF16)) for k in range(12)]
        p.wait("tensor", "AC", p.cnt["AC"])
        g_prp = [0, 0]
        step = 0
        v_half_pe = v_pe_qc_done
        for hh in range(2):
            p.wait("gpsimd", "PE", v_half_pe)
            if hh:
                p.wait("gpsimd", "VE", v_wqrr)
            wld5 = []
            for k in range(12):
                vg = p.run("gpsimd", lambda e, k=k, hh=hh: e.dma_start(
                    out=wqr_t[k][:],
                    in_=wqr[k * 128:(k + 1) * 128, hh * 512:(hh + 1) * 512]),
                    inc="WG", inck=16)
                wld5.append(vg)
            p.wait("vector", "WG", wld5[-1])
            for k in range(12):
                wt = wqr_t[k][:].rearrange("p (h t c) -> p h t c", h=8, t=2, c=32)
                rt = wqrr_t[k][:].rearrange("p (h t c) -> p h t c", h=8, t=2, c=32)
                p.run("vector", lambda e, wt=wt, rt=rt: e.tensor_scalar(
                    rt[:, :, 0, :], wt[:, :, 1, :], -1.0, None, OP.mult))
                if k == 11:
                    v_wqrr = p.run("vector", lambda e, wt=wt, rt=rt: e.tensor_copy(
                        rt[:, :, 1, :], wt[:, :, 0, :]), inc="VE")
                else:
                    p.run("vector", lambda e, wt=wt, rt=rt: e.tensor_copy(
                        rt[:, :, 1, :], wt[:, :, 0, :]))
            p.wait("tensor", "VE", v_wqrr)
            for m in range(4 * hh, 4 * hh + 4):
                mh = m - 4 * hh
                for n4 in range(4):
                    pp = step % 2
                    step += 1
                    p.wait("tensor", "VE", g_prp[pp])
                    mms = []
                    for k in range(12):
                        mms.append((pr[2 * pp][:], wqr_t[k][:, mh * 128:(mh + 1) * 128],
                                    cq[k][:, n4 * 512:(n4 + 1) * 512], k == 0, k == 11))
                    for k in range(12):
                        mms.append((pr[2 * pp + 1][:],
                                    wqrr_t[k][:, mh * 128:(mh + 1) * 128],
                                    cq[k][:, n4 * 512:(n4 + 1) * 512], k == 0, k == 11))
                    v_qr = mm_seq(mms)
                    p.wait("vector", "PE", v_qr)
                    p.run("vector", lambda e, n4=n4, pp=pp: e.tensor_tensor(
                        ropeA[:], pr[2 * pp][:],
                        s_cos[:, n4 * 512:(n4 + 1) * 512], OP.mult))
                    p.run("vector", lambda e, n4=n4, pp=pp: e.tensor_tensor(
                        ropeB[:], pr[2 * pp + 1][:],
                        s_sin[:, n4 * 512:(n4 + 1) * 512], OP.mult))
                    p.run("vector", lambda e, m=m, n4=n4: e.tensor_tensor(
                        qh[2 * m][64:128, n4 * 512:(n4 + 1) * 512],
                        ropeA[0:64, :], ropeB[0:64, :], OP.add))
                    v_qrope = p.run("vector", lambda e, m=m, n4=n4: e.tensor_tensor(
                        qh[2 * m + 1][64:128, n4 * 512:(n4 + 1) * 512],
                        ropeA[64:128, :], ropeB[64:128, :], OP.add), inc="VE")
                    g_prp[pp] = v_qrope
                    do_quants(1)
            v_half_pe = v_qr
        st_wqr.close()
        st_rope.close()
        v_pe_cq_done = p.cnt["PE"]
        st_cq.close()

        st_kh = ExitStack()
        kh = [st_kh.enter_context(nc.sbuf_tensor(f"kh{h}", [128, S], BF16, side="right"))
              for h in range(H)]
        st_wkc = ExitStack()
        wkc_t = [st_wkc.enter_context(nc.sbuf_tensor(f"wkct{k}", [128, 1024], BF16))
                 for k in range(4)]
        p.wait("gpsimd", "PE", v_pe_cq_done)
        p.wait("gpsimd", "VE", v_qrope)
        wld6 = []
        for k in range(4):
            vg = p.run("gpsimd", lambda e, k=k: e.dma_start(
                out=wkc_t[k][:], in_=wkc[k * 128:(k + 1) * 128, :]), inc="WG", inck=16)
            wld6.append(vg)
        p.wait("tensor", "WG", wld6[-1])
        p.wait("tensor", "VE", v_qrope)
        for m in range(8):
            gg = m % 2
            p.wait("tensor", "AC", ev_g[gg])
            mms = []
            for k in range(4):
                for n4 in range(4):
                    mms.append((pj[4 * gg + n4][:],
                                wkc_t[k][:, m * 128:(m + 1) * 128],
                                ckv[k][:, n4 * 512:(n4 + 1) * 512], k == 0, k == 3))
            v_mm = mm_seq(mms)
            p.wait("scalar", "PE", v_mm)
            for n4 in range(4):
                p.run("scalar", lambda e, m=m, n4=n4, gg=gg: e.copy(
                    kh[2 * m][0:64, n4 * 512:(n4 + 1) * 512],
                    pj[4 * gg + n4][0:64, :]))
                ev_g[gg] = p.run("scalar", lambda e, m=m, n4=n4, gg=gg: e.copy(
                    kh[2 * m + 1][0:64, n4 * 512:(n4 + 1) * 512],
                    pj[4 * gg + n4][64:128, :]), inc="AC")
        p.wait("vector", "VE", v_krp_done)
        for h in range(H):
            if h == H - 1:
                v_khcp = p.run("vector", lambda e, h=h: e.tensor_copy(
                    kh[h][64:128, :], krp[:]), inc="VE")
            else:
                p.run("vector", lambda e, h=h: e.tensor_copy(
                    kh[h][64:128, :], krp[:]))
        st_wkc.close()

        st_wv = ExitStack()
        Ev = st_wv.enter_context
        wv_t = [Ev(nc.sbuf_tensor(f"wvt{k}", [128, D], BF16)) for k in range(4)]
        vout = [Ev(nc.sbuf_tensor(f"vout{i}", [128, D], BF16)) for i in range(2)]
        p.wait("gpsimd", "PE", p.cnt["PE"])
        wld7 = []
        for k in range(4):
            vg = p.run("gpsimd", lambda e, k=k: e.dma_start(
                out=wv_t[k][:], in_=wv[k * 128:(k + 1) * 128, :]), inc="WG", inck=16)
            wld7.append(vg)
        p.wait("tensor", "WG", wld7[-1])
        VSS = ("VSA", "VSB")
        VTS = ("VTA", "VTB")
        for t in range(NT):
            ob = t % 2
            gg = t % 2
            p.wait("tensor", "AC", ev_g[gg])
            mms = []
            for k in range(4):
                for n4 in range(4):
                    mms.append((pj[4 * gg + n4][:],
                                ckv[k][:, t * 128:(t + 1) * 128],
                                wv_t[k][:, n4 * 512:(n4 + 1) * 512], k == 0, k == 3))
            v_mm = mm_seq(mms)
            p.wait("scalar", "PE", v_mm)
            p.wait("scalar", VSS[ob], 16 * (t // 2))
            for n4 in range(4):
                ev_g[gg] = p.run("scalar", lambda e, ob=ob, n4=n4, gg=gg: e.copy(
                    vout[ob][:, n4 * 512:(n4 + 1) * 512], pj[4 * gg + n4][:]),
                    inc="AC")
            p.wait("gpsimd", "AC", ev_g[gg])
            p.run("gpsimd", lambda e, ob=ob, t=t: e.dma_start(
                out=v_s[t * 128:(t + 1) * 128, :], in_=vout[ob][:]),
                inc=VSS[ob], inck=16)
        st_wv.close()
        st_krp.close()
        st_ckv.close()
        st_p2ps.close()
        barrier(exclude=("VSA", "VSB", "SG", "SH", "SI", "SJ", "SK", "SL", "TS"),
                overrides={"VE": v_khcp})
        v_p2_pe = p.cnt["PE"]
        v_p2_ve = v_khcp
        v_p2_ac = p.cnt["AC"]

        st3 = ExitStack()
        E3 = st3.enter_context
        vth = [E3(nc.sbuf_tensor(f"vth{i}", [128, S], BF16)) for i in range(4)]
        ae = [E3(nc.sbuf_tensor(f"ae{i}", [128, 512], BF16)) for i in range(4)]
        srow = E3(nc.sbuf_tensor("srow", [1, 512], F32))
        srr = [E3(nc.sbuf_tensor("srr0", [1, 512], F32))]
        rbc = [E3(nc.sbuf_tensor("rbc0", [128, 512], F32))]
        cxo = [E3(nc.sbuf_tensor(f"cxo{i}", [128, 512], BF16)) for i in range(2)]
        psc = [E3(nc.psum_tensor(f"psc{i}", [128, 512], F32)) for i in range(4)]
        pctx = [E3(nc.psum_tensor(f"pctx{i}", [128, 512], F32)) for i in range(2)]
        psr = [E3(nc.psum_tensor(f"psr{i}", [128, 512], F32)) for i in range(2)]

        VTS4 = ("VTA", "VTB", "VTC", "VTD")
        p.wait("sync", "VE", v_p2_ve)
        p.wait("sync", "AC", v_p2_ac)
        p.wait("sync", "VSA", 16 * 8)
        p.wait("sync", "VSB", 16 * 8)

        def vth_issue(h, pe_guard):
            sl = VTS4[h % 4]
            p.wait("sync", "PE", pe_guard)
            for t in range(NT):
                p.wait("sync", sl, p.cnt[sl])
                p.run("sync", lambda e, h=h, t=t: e.dma_start(
                    out=vth[h % 4][:, t * 128:(t + 1) * 128],
                    in_=v_s[t * 128:(t + 1) * 128, h * 128:(h + 1) * 128]),
                    inc=sl, inck=16)

        for hh in range(4):
            vth_issue(hh, 0)

        g_psc = [0, 0, 0, 0]
        g_ae = [0, 0, 0, 0]
        g_pctx = [0, 0]
        g_psr = [0, 0]
        g_psrv = [0, 0]
        g_pbc = 0
        g_cxo = [0, 0]
        g_srr = 0
        p.wait("tensor", "VE", v_p2_ve)
        p.wait("tensor", "AC", v_p2_ac)
        for hp in range(8):
            heads = (2 * hp, 2 * hp + 1)
            for h in heads:
                p.wait("tensor", VTS4[h % 4], 256 * (h // 4 + 1))
            for s4 in range(4):
                Q0 = 512 * s4
                nblocks = 4 * (s4 + 1)
                U = [dict(h=heads[ui], hb=ui, sd={}) for ui in range(2)]

                def emit_score(un, j, Q0=Q0, s4=s4):
                    r = 128 * (j - 4 * s4) if j >= 4 * s4 else 0
                    sb = 2 * un["hb"] + j % 2
                    p.wait("tensor", "AC", g_psc[sb])
                    if j == 0:
                        p.wait("tensor", "VE", g_pctx[un["hb"]])
                        p.wait("tensor", "VE", g_psrv[un["hb"]])
                        p.wait("tensor", "AC", g_psr[un["hb"]])
                    un["sd"][j] = p.run(
                        "tensor", lambda e, h=un["h"], j=j, sb=sb, r=r, Q0=Q0:
                        e.matmul(psc[sb][:, r:512],
                                 kh[h][:, j * 128:(j + 1) * 128],
                                 qh[h][:, Q0 + r:Q0 + 512],
                                 start=True, stop=True), inc="PE")

                for un in U:
                    emit_score(un, 0)
                if nblocks > 1:
                    for un in U:
                        emit_score(un, 1)
                for j in range(nblocks):
                    r = 128 * (j - 4 * s4) if j >= 4 * s4 else 0
                    for un in U:
                        hb = un["hb"]
                        sb = 2 * hb + j % 2
                        ab = 2 * hb + j % 2
                        p.wait("scalar", "PE", max(un["sd"][j], g_ae[ab]))
                        v_exp = p.run("scalar", lambda e, sb=sb, r=r, ab=ab:
                                      e.activation(ae[ab][:, r:512],
                                                   psc[sb][:, r:512],
                                                   AF.Exp, bias=0.0,
                                                   scale=s_isq[:]), inc="AC")
                        g_psc[sb] = v_exp
                        if j >= 4 * s4:
                            p.wait("vector", "AC", v_exp)
                            v_msk = p.run("vector", lambda e, ab=ab, r=r:
                                          e.tensor_tensor(ae[ab][:, r:r + 128],
                                                          ae[ab][:, r:r + 128],
                                                          s_trim[:], OP.mult),
                                          inc="VE")
                        else:
                            v_msk = None
                        if j + 2 < nblocks:
                            emit_score(un, j + 2)
                        if v_msk is not None:
                            p.wait("tensor", "VE", v_msk)
                        else:
                            p.wait("tensor", "AC", v_exp)
                        st_, sp_ = (j == 0), (j == nblocks - 1)
                        p.run("tensor", lambda e, hb4=un["h"] % 4, j=j, hb=hb, r=r,
                              ab=ab, st_=st_, sp_=sp_:
                              e.matmul(pctx[hb][:, r:512],
                                       vth[hb4][:, j * 128:(j + 1) * 128],
                                       ae[ab][:, r:512], start=st_, stop=sp_))
                        v_bmm = p.run("tensor", lambda e, hb=hb, r=r, ab=ab,
                                      st_=st_, sp_=sp_:
                                      e.matmul(psr[hb][0:1, r:512], s_ones_bf[:],
                                               ae[ab][:, r:512],
                                               start=st_, stop=sp_), inc="PE")
                        g_ae[ab] = v_bmm
                        un["bmm"] = v_bmm
                for un in U:
                    hb = un["hb"]
                    h = un["h"]
                    p.wait("scalar", "PE", max(un["bmm"], g_srr))
                    p.run("scalar", lambda e, hb=hb: e.activation(
                        srow[:], psr[hb][0:1, :], AF.Ln, bias=0.0, scale=1.0))
                    v_rcp = p.run("scalar", lambda e: e.activation(
                        srr[0][:], srow[:], AF.Exp, bias=0.0, scale=-1.0), inc="AC")
                    g_psr[hb] = v_rcp
                    p.wait("tensor", "AC", v_rcp)
                    p.wait("tensor", "VE", g_pbc if g_pbc else 0)
                    v_bc = p.run("tensor", lambda e, hb=hb: e.matmul(
                        psr[hb][:, :], s_onesr_f[:], srr[0][:],
                        start=True, stop=True), inc="PE")
                    g_srr = v_bc
                    p.wait("vector", "PE", v_bc)
                    v_rbc = p.run("vector", lambda e, hb=hb: e.tensor_copy(
                        rbc[0][:], psr[hb][:, :]), inc="VE")
                    g_pbc = v_rbc
                    g_psrv[hb] = v_rbc
                    p.wait("vector", "CS", p.cnt["CS"] if g_cxo[hb] else 0)
                    v_cx = p.run("vector", lambda e, hb=hb: e.tensor_tensor(
                        cxo[hb][:], pctx[hb][:], rbc[0][:], OP.mult), inc="VE")
                    g_pctx[hb] = v_cx
                    p.wait("gpsimd", "VE", v_cx)
                    v_gd = p.run("gpsimd", lambda e, h=h, s4=s4, hb=hb: e.dma_start(
                        out=ctx_s[h * 128:(h + 1) * 128, s4 * 512:(s4 + 1) * 512],
                        in_=cxo[hb][:]), inc="CS", inck=16)
                    g_cxo[hb] = v_gd
                    do_quants(1)
            for h in heads:
                if h + 4 < H:
                    vth_issue(h + 4, p.cnt["PE"])
        v_ctx_gd = p.cnt["CS"]
        v_p3_pe = p.cnt["PE"]
        v_p3_ve = p.cnt["VE"]
        v_p3_ac = p.cnt["AC"]
        st3.close()
        st_kh.close()
        st_qk.close()
        barrier(exclude=("SG", "SH", "SI", "SJ", "SK", "SL", "TS"))

        st4 = ExitStack()
        E4 = st4.enter_context
        ctxc = [E4(nc.sbuf_tensor(f"ctxc{c}", [128, S], BF16)) for c in range(16)]
        woc = [E4(nc.sbuf_tensor(f"woc{c}", [128, D], BF16)) for c in range(16)]
        s_bo = E4(nc.sbuf_tensor("s_bo", [1, D], F32))
        xt4 = [E4(nc.sbuf_tensor(f"xt4_{i}", [128, D], F32)) for i in range(2)]
        x1t = [E4(nc.sbuf_tensor(f"x1t{i}", [128, D], F32)) for i in range(2)]
        pout = [E4(nc.psum_tensor(f"pout{i}", [128, 512], F32)) for i in range(8)]

        p.wait("sync", "CS", v_ctx_gd)
        p.wait("sync", "VE", v_p3_ve)
        p.wait("sync", "AC", v_p3_ac)
        ctx_ld = []
        for c in range(16):
            vls = p.run("sync", lambda e, c=c: e.dma_start(
                out=ctxc[c][:], in_=ctx_s[c * 128:(c + 1) * 128, :]), inc="CT", inck=16)
            ctx_ld.append(vls)
        v_bo_ld = p.run("sync", lambda e: e.dma_start(out=s_bo[:], in_=bo),
                        inc="LA", inck=16)
        p.wait("gpsimd", "PE", v_p3_pe)
        wo_ld = []
        for c in range(16):
            vg = p.run("gpsimd", lambda e, c=c: e.dma_start(
                out=woc[c][:], in_=wo[c * 128:(c + 1) * 128, :]), inc="WG", inck=16)
            wo_ld.append(vg)
        p.wait("tensor", "LA", v_bo_ld)
        p.wait("tensor", "PE", v_p3_pe)
        p.wait("tensor", "VE", v_p3_ve)
        p.wait("tensor", "AC", v_p3_ac)

        g_pout = [0, 0]
        g_x1 = [0, 0]
        g_xt4 = [0, 0]
        for t in range(NT):
            pg = t % 2
            p.wait("sync", "AC", g_xt4[pg])
            v_xl = slot_load(pg, xt4[pg][:], x_in[t * 128:(t + 1) * 128, :])
            p.wait("tensor", "WG", wo_ld[15])
            p.wait("tensor", "CT", ctx_ld[15])
            p.wait("tensor", "VE", g_pout[pg])
            mms = []
            for dv in range(16):
                for o4 in range(4):
                    mms.append((pout[4 * pg + o4][:],
                                ctxc[dv][:, t * 128:(t + 1) * 128],
                                woc[dv][:, o4 * 512:(o4 + 1) * 512],
                                dv == 0, False))
            for o4 in range(4):
                mms.append((pout[4 * pg + o4][:], s_onesr_f[:],
                            s_bo[:, o4 * 512:(o4 + 1) * 512], False, True))
            v_wmm = mm_seq(mms)
            p.wait("vector", "PE", v_wmm)
            p.wait("vector", SLOTS[pg], v_xl)
            p.wait("vector", "X1S", p.cnt["X1S"] if g_x1[pg] else 0)
            for o4 in range(4):
                p.run("vector", lambda e, pg=pg, o4=o4: e.tensor_tensor(
                    x1t[pg][:, o4 * 512:(o4 + 1) * 512], pout[4 * pg + o4][:],
                    xt4[pg][:, o4 * 512:(o4 + 1) * 512], OP.add))
            v_x1 = p.run("vector", lambda e, pg=pg, t=t: e.reduce_sum(
                pan2["sum"][:, t:t + 1], x1t[pg][:], AX), inc="VE")
            g_pout[pg] = v_x1
            p.wait("scalar", "VE", v_x1)
            v_sq2 = p.run("scalar", lambda e, pg=pg, t=t: e.activation(
                xt4[pg][:], x1t[pg][:], AF.Square, bias=0.0, scale=1.0,
                accum_out=pan2["sq"][:, t:t + 1]), inc="AC")
            g_xt4[pg] = v_sq2
            p.wait("gpsimd", "AC", v_sq2)
            v_gd = p.run("gpsimd", lambda e, pg=pg, t=t: e.dma_start(
                out=x1_s[t * 128:(t + 1) * 128, :], in_=x1t[pg][:]), inc="X1S", inck=16)
            g_x1[pg] = v_gd
            do_quants(3)
        do_quants(999)
        v_x1_gd = p.cnt["X1S"]
        v_nmr2 = panel_math(pan2, p.cnt["VE"], p.cnt["AC"])
        v_p4_pe = p.cnt["PE"]
        v_p4_ve = p.cnt["VE"]
        v_p4_ac = p.cnt["AC"]
        st4.close()
        st_tern.close()
        barrier()

        def ffn_half(hf, v_p4_pe, v_p4_ve, v_p4_ac, v_x1_gd, v_nmr2):
            T0 = hf * 8
            R0 = hf * 1024
            st5a = ExitStack()
            E5 = st5a.enter_context
            x1l = [E5(nc.sbuf_tensor(f"x1l{i}_{hf}", [128, D], F32)) for i in range(2)]
            xln2 = [E5(nc.sbuf_tensor(f"xln2_{i}_{hf}", [128, D], F32)) for i in range(8)]
            tq = [E5(nc.sbuf_tensor(f"tq0_{hf}", [128, D], F32))] * 2
            tqk = [E5(nc.sbuf_tensor(f"tqk0_{hf}", [128, D], F32))] * 2
            tqd = E5(nc.sbuf_tensor(f"tqd_{hf}", [128, D], BF16))
            tqc = E5(nc.sbuf_tensor(f"tqc_{hf}", [128, D], BF16))
            tqi = [E5(nc.sbuf_tensor(f"tqi0_{hf}", [128, D], I32))] * 2
            xqb = [E5(nc.sbuf_tensor(f"xqb{i}_{hf}", [128, D], BF16)) for i in range(2)]

            p.wait("scalar", "VE", v_nmr2)
            p.wait("sync", "X1S", v_x1_gd)
            p.wait("sync", "PE", v_p4_pe)
            p.wait("sync", "VE", v_p4_ve)
            p.wait("sync", "AC", v_p4_ac)
            p.wait("sync", "ST", p.cnt["ST"])
            g_x1l = [0, 0]
            for t in range(8):
                gt = T0 + t
                b = t % 2
                p.wait("sync", "VE", g_x1l[b])
                v_xls = slot_load(b, x1l[b][:], x1_s[gt * 128:(gt + 1) * 128, :])
                p.wait("scalar", SLOTS[b], v_xls)
                v_n = p.run("scalar", lambda e, b=b, t=t, gt=gt: e.activation(
                    xln2[t][:], x1l[b][:], AF.Identity,
                    bias=pan2["nmr"][:, gt:gt + 1],
                    scale=pan2["rstd2"][:, gt:gt + 1]), inc="AC")
                p.wait("vector", "AC", v_n)
                v_mx = p.run("vector", lambda e, t=t: e.reduce_max(
                    maxpan1[:, t:t + 1], xln2[t][:], AX,
                    apply_absolute_value=True), inc="VE")
                g_x1l[b] = v_mx
            p.run("vector", lambda e: e.tensor_scalar(
                maxpan1[:], maxpan1[:], EPS, None, OP.max))
            p.run("vector", lambda e: e.reciprocal(panwk[:], maxpan1[:]))
            nr_recip(panwk[:], maxpan1[:], evacpan1[:])
            p.run("vector", lambda e: e.tensor_scalar(
                scalepan1[:], panwk[:], 127.0, None, OP.mult))
            p.run("vector", lambda e: e.tensor_scalar(
                evacpan1[:], maxpan1[:], s_r127[:], None, OP.mult))
            v_pan1 = p.run("vector", lambda e: e.tensor_scalar(
                evacpan1[:], evacpan1[:], s_sw["up"][:], None, OP.mult), inc="VE")

            g_xqb = [0, 0]
            g_tq = [0, 0]
            p.wait("scalar", "VE", max(v_toff, v_pan1))
            v_q = 0
            for t in range(8):
                gt = T0 + t
                b = t % 2
                p.wait("scalar", "VE", v_q)
                p.run("scalar", lambda e, t=t, b=b: e.activation(
                    tq[b][:], xln2[t][:], AF.Identity,
                    bias=s_toff[:, 0:1], scale=scalepan1[:, t:t + 1]))
                p.run("scalar", lambda e, b=b: e.copy(tqi[b][:], tq[b][:]))
                v_t1 = p.run("scalar", lambda e, b=b: e.copy(tqk[b][:], tqi[b][:]),
                             inc="AC")
                p.wait("vector", "AC", v_t1)
                p.wait("vector", "QS", p.cnt["QS"] if g_xqb[b] else 0)
                p.run("vector", lambda e, b=b: e.tensor_sub(tqd[:], tq[b][:], tqk[b][:]))
                p.run("vector", lambda e: e.tensor_scalar(
                    tqd[:], tqd[:], s_sdc[:], None, OP.add))
                p.run("vector", lambda e: e.tensor_scalar(
                    tqc[:], tqd[:], 1.0, None, OP.is_gt))
                p.run("vector", lambda e: e.scalar_tensor_tensor(
                    tqc[:], tqd[:], 2.0, tqc[:], OP.is_gt, OP.add))
                p.run("vector", lambda e: e.scalar_tensor_tensor(
                    tqc[:], tqd[:], 3.0, tqc[:], OP.is_gt, OP.add))
                p.run("vector", lambda e: e.scalar_tensor_tensor(
                    tqc[:], tqd[:], 0.0, tqc[:], OP.is_ge, OP.add))
                p.run("vector", lambda e: e.scalar_tensor_tensor(
                    tqc[:], tqd[:], -1.0, tqc[:], OP.is_ge, OP.add))
                p.run("vector", lambda e: e.scalar_tensor_tensor(
                    tqc[:], tqd[:], -2.0, tqc[:], OP.is_ge, OP.add))
                v_q = p.run("vector", lambda e, b=b: e.scalar_tensor_tensor(
                    xqb[b][:], tqc[:], -259.0, tqk[b][:], OP.add, OP.add), inc="VE")
                g_tq[b] = v_q
                p.wait("gpsimd", "VE", v_q)
                g_xqb[b] = p.run("gpsimd", lambda e, b=b, gt=gt: e.dma_start(
                    out=xq_s[gt * 128:(gt + 1) * 128, :], in_=xqb[b][:]),
                    inc="QS", inck=16)
                if t == 3:
                    v_xq_mid = p.cnt["QS"]
            v_xq_gd = p.cnt["QS"]
            v_a_ve = p.cnt["VE"]
            v_a_ac = p.cnt["AC"]
            st5a.close()

            st5b = ExitStack()
            E5 = st5b.enter_context
            ht = [E5(nc.sbuf_tensor(f"ht{t}_{hf}", [128, FI], F32)) for t in range(8)]
            trelu = E5(nc.sbuf_tensor(f"trelu_{hf}", [128, 512], F32))
            pup = [E5(nc.psum_tensor(f"pup{i}_{hf}", [128, 512], F32)) for i in range(4)]
            st5bio = ExitStack()
            E5i = st5bio.enter_context
            xqf = [E5i(nc.sbuf_tensor(f"xqf{c}_{hf}", [128, 1024], BF16)) for c in range(16)]
            rup = [[E5i(nc.sbuf_tensor(f"rup{g}_{k}_{hf}", [128, 512], BF16))
                    for k in range(16)] for g in range(2)]

            def rup_load(ip):
                g = ip % 2
                sem = ("RWA", "RWB")[g]
                last = None
                for hc in range(16):
                    last = p.run("sync", lambda e, g=g, hc=hc, ip=ip: e.dma_start(
                        out=rup[g][hc][:],
                        in_=tup_s[ip * 512:(ip + 1) * 512, hc * 128:(hc + 1) * 128],
                        transpose=True), inc=sem, inck=16)
                return last

            rup_rdy = [0, 0]
            rup_pe_hist = [0, 0]
            rup_rdy[0] = rup_load(0)
            rup_rdy[1] = rup_load(1)
            p.wait("sync", "QS", v_xq_mid)
            for c in range(16):
                p.run("sync", lambda e, c=c, R0=R0: e.dma_start(
                    out=xqf[c][:, 0:512], in_=xq_s[R0:R0 + 512, c * 128:(c + 1) * 128],
                    transpose=True), inc="XQF", inck=16)
            v_xqfA = p.cnt["XQF"]
            p.wait("sync", "QS", v_xq_gd)
            p.wait("sync", "VE", v_a_ve)
            p.wait("sync", "AC", v_a_ac)
            for c in range(16):
                p.run("sync", lambda e, c=c, R0=R0: e.dma_start(
                    out=xqf[c][:, 512:1024],
                    in_=xq_s[R0 + 512:R0 + 1024, c * 128:(c + 1) * 128],
                    transpose=True), inc="XQG", inck=16)
            v_xqfB = p.cnt["XQG"]

            g_pup = [0, 0, 0, 0]
            p.wait("tensor", "PE", v_p4_pe)
            p.wait("tensor", "VE", v_p4_ve)
            p.wait("tensor", "AC", v_p4_ac)
            for ip in range(8):
                g = ip % 2
                if 2 <= ip + 1 < 8:
                    p.wait("sync", "PE", rup_pe_hist[(ip + 1) % 2])
                    rup_rdy[(ip + 1) % 2] = rup_load(ip + 1)
                for t in range(8):
                    pb = t % 4
                    p.wait("tensor", ("RWA", "RWB")[g], rup_rdy[g])
                    if t <= 3:
                        p.wait("tensor", "XQF", v_xqfA)
                    else:
                        p.wait("tensor", "XQG", v_xqfB)
                    p.wait("tensor", "AC", g_pup[pb])
                    mms = []
                    for hc in range(16):
                        mms.append((pup[pb][:], xqf[hc][:, t * 128:(t + 1) * 128],
                                    rup[g][hc][:], hc == 0, hc == 15))
                    v_up = mm_seq(mms)
                    p.wait("scalar", "PE", v_up)
                    p.run("scalar", lambda e, pb=pb, t=t: e.activation(
                        trelu[:], pup[pb][:], AF.Relu, bias=0.0,
                        scale=evacpan1[:, t:t + 1]))
                    v_hsq = p.run("scalar", lambda e, t=t, ip=ip: e.activation(
                        ht[t][:, ip * 512:(ip + 1) * 512], trelu[:], AF.Square,
                        bias=0.0, scale=1.0), inc="AC")
                    g_pup[pb] = v_hsq
                    p.wait("vector", "AC", v_hsq)
                    v_hm = p.run("vector", lambda e, t=t, ip=ip: e.reduce_max(
                        hmaxpan[:, t * 8 + ip:t * 8 + ip + 1],
                        ht[t][:, ip * 512:(ip + 1) * 512], AX), inc="VE")
                rup_pe_hist[g] = p.cnt["PE"]

            v_up_ac = p.cnt["AC"]
            for t in range(8):
                p.run("vector", lambda e, t=t: e.reduce_max(
                    max2pan[:, t:t + 1], hmaxpan[:, t * 8:(t + 1) * 8], AX))
            p.run("vector", lambda e: e.tensor_scalar(
                max2pan[:], max2pan[:], EPS, None, OP.max))
            p.run("vector", lambda e: e.reciprocal(panwk[:], max2pan[:]))
            nr_recip(panwk[:], max2pan[:], evacpan2[:])
            p.run("vector", lambda e: e.tensor_scalar(
                scalepan2[:], panwk[:], 127.0, None, OP.mult))
            p.run("vector", lambda e: e.tensor_scalar(
                evacpan2[:], max2pan[:], s_r127[:], None, OP.mult))
            v_pan2 = p.run("vector", lambda e: e.tensor_scalar(
                evacpan2[:], evacpan2[:], s_sw["dn"][:], None, OP.mult), inc="VE")

            st5bio.close()
            FH = FI // 2
            tq2 = [E5(nc.sbuf_tensor(f"tq2{i}_{hf}", [128, FH], F32)) for i in range(2)]
            tq2k = [E5(nc.sbuf_tensor(f"tq2k{i}_{hf}", [128, FH], F32)) for i in range(2)]
            tq2d = E5(nc.sbuf_tensor(f"tq2d_{hf}", [128, FH], BF16))
            tq2c = E5(nc.sbuf_tensor(f"tq2c_{hf}", [128, FH], BF16))
            tq2i = [E5(nc.sbuf_tensor(f"tq2i{i}_{hf}", [128, FH], I32)) for i in range(2)]
            hqb = [E5(nc.sbuf_tensor(f"hqb{i}_{hf}", [128, FH], BF16)) for i in range(2)]
            g_hqb = [0, 0]
            g_tq2 = [0, 0]
            p.wait("scalar", "VE", v_pan2)
            v_q2 = 0
            for idx in range(16):
                t, q2 = idx // 2, idx % 2
                gt = T0 + t
                b = idx % 2
                p.wait("scalar", "VE", g_tq2[b])
                p.run("scalar", lambda e, t=t, q2=q2, b=b: e.activation(
                    tq2[b][:], ht[t][:, q2 * FH:(q2 + 1) * FH], AF.Identity,
                    bias=s_toff[:, 0:1], scale=scalepan2[:, t:t + 1]))
                p.run("scalar", lambda e, b=b: e.copy(tq2i[b][:], tq2[b][:]))
                v_t1 = p.run("scalar", lambda e, b=b: e.copy(tq2k[b][:], tq2i[b][:]),
                             inc="AC")
                p.wait("vector", "AC", v_t1)
                p.wait("vector", "HS", p.cnt["HS"] if g_hqb[b] else 0)
                p.run("vector", lambda e, b=b: e.tensor_sub(
                    tq2d[:], tq2[b][:], tq2k[b][:]))
                p.run("vector", lambda e: e.tensor_scalar(
                    tq2d[:], tq2d[:], s_sdc[:], None, OP.add))
                p.run("vector", lambda e: e.tensor_scalar(
                    tq2c[:], tq2d[:], 1.0, None, OP.is_gt))
                p.run("vector", lambda e: e.scalar_tensor_tensor(
                    tq2c[:], tq2d[:], 2.0, tq2c[:], OP.is_gt, OP.add))
                p.run("vector", lambda e: e.scalar_tensor_tensor(
                    tq2c[:], tq2d[:], 3.0, tq2c[:], OP.is_gt, OP.add))
                p.run("vector", lambda e: e.scalar_tensor_tensor(
                    tq2c[:], tq2d[:], 0.0, tq2c[:], OP.is_ge, OP.add))
                p.run("vector", lambda e: e.scalar_tensor_tensor(
                    tq2c[:], tq2d[:], -1.0, tq2c[:], OP.is_ge, OP.add))
                p.run("vector", lambda e: e.scalar_tensor_tensor(
                    tq2c[:], tq2d[:], -2.0, tq2c[:], OP.is_ge, OP.add))
                v_q2 = p.run("vector", lambda e, b=b: e.scalar_tensor_tensor(
                    hqb[b][:], tq2c[:], -259.0, tq2k[b][:], OP.add, OP.add), inc="VE")
                g_tq2[b] = v_q2
                p.wait("gpsimd", "VE", v_q2)
                g_hqb[b] = p.run("gpsimd", lambda e, b=b, gt=gt, q2=q2: e.dma_start(
                    out=hq_s[gt * 128:(gt + 1) * 128, q2 * FH:(q2 + 1) * FH],
                    in_=hqb[b][:]), inc="HS", inck=16)
                if idx == 7:
                    v_hq_mid = p.cnt["HS"]
                if idx == 11:
                    v_hq_t5 = p.cnt["HS"]
            v_hq_gd = p.cnt["HS"]
            v_b_ve = p.cnt["VE"]
            v_b_ac = p.cnt["AC"]
            v_b_pe = p.cnt["PE"]
            st5b.close()

            st5c = ExitStack()
            E5 = st5c.enter_context
            hqf = [E5(nc.sbuf_tensor(f"hqf{c}_{hf}", [128, 1024], BF16)) for c in range(32)]
            rdn = [[E5(nc.sbuf_tensor(f"rdn{g}_{k}_{hf}", [128, 512], BF16))
                    for k in range(32)] for g in range(2)]
            x1c = [E5(nc.sbuf_tensor(f"x1c{i}_{hf}", [128, 512], F32)) for i in range(2)]
            obuf = [E5(nc.sbuf_tensor(f"obuf{i}_{hf}", [128, 512], F32)) for i in range(2)]
            tdo = E5(nc.sbuf_tensor(f"tdo_{hf}", [128, 512], F32))
            pdn = [E5(nc.psum_tensor(f"pdn{i}_{hf}", [128, 512], F32)) for i in range(4)]

            def rdn_load(op4):
                g = op4 % 2
                sem = ("RWA", "RWB")[g]
                last = None
                for ic in range(32):
                    last = p.run("sync", lambda e, g=g, ic=ic, op4=op4: e.dma_start(
                        out=rdn[g][ic][:],
                        in_=tdn_s[op4 * 512:(op4 + 1) * 512, ic * 128:(ic + 1) * 128],
                        transpose=True), inc=sem, inck=16)
                return last

            p.wait("sync", "HS", v_hq_mid)
            for c in range(32):
                p.run("sync", lambda e, c=c, R0=R0: e.dma_start(
                    out=hqf[c][:, 0:512], in_=hq_s[R0:R0 + 512, c * 128:(c + 1) * 128],
                    transpose=True), inc="XQF", inck=16)
            v_hqfA = p.cnt["XQF"]
            p.wait("sync", "HS", v_hq_t5)
            rdn_rdy = [0, 0]
            rdn_pe_hist = [v_b_pe, v_b_pe]
            rdn_rdy[0] = rdn_load(0)
            p.wait("sync", "HS", v_hq_gd)
            p.wait("sync", "VE", v_b_ve)
            p.wait("sync", "AC", v_b_ac)
            for c in range(32):
                p.run("sync", lambda e, c=c, R0=R0: e.dma_start(
                    out=hqf[c][:, 512:1024],
                    in_=hq_s[R0 + 512:R0 + 1024, c * 128:(c + 1) * 128],
                    transpose=True), inc="XQG", inck=16)
            v_hqfB = p.cnt["XQG"]

            g_pdn = [0, 0, 0, 0]
            g_ob = [0, 0]
            g_x1c = [0, 0]
            g_tdo = v_b_ve
            p.wait("tensor", "AC", v_up_ac)
            for op4 in range(4):
                g = op4 % 2
                if op4 + 1 < 4:
                    p.wait("sync", "PE", rdn_pe_hist[(op4 + 1) % 2])
                    rdn_rdy[(op4 + 1) % 2] = rdn_load(op4 + 1)
                for t in range(8):
                    gt = T0 + t
                    b = t % 2
                    pb = t % 4
                    p.wait("sync", "VE", g_x1c[b])
                    v_x1r = slot_load(b, x1c[b][:],
                                      x1_s[gt * 128:(gt + 1) * 128,
                                           op4 * 512:(op4 + 1) * 512])
                    p.wait("tensor", ("RWA", "RWB")[g], rdn_rdy[g])
                    if t <= 3:
                        p.wait("tensor", "XQF", v_hqfA)
                    else:
                        p.wait("tensor", "XQG", v_hqfB)
                    p.wait("tensor", "AC", g_pdn[pb])
                    mms = []
                    for ic in range(32):
                        mms.append((pdn[pb][:], hqf[ic][:, t * 128:(t + 1) * 128],
                                    rdn[g][ic][:], ic == 0, ic == 31))
                    v_dn = mm_seq(mms)
                    p.wait("scalar", "PE", v_dn)
                    p.wait("scalar", "VE", g_tdo)
                    v_ev = p.run("scalar", lambda e, pb=pb, t=t: e.activation(
                        tdo[:], pdn[pb][:], AF.Copy, bias=0.0,
                        scale=evacpan2[:, t:t + 1]), inc="AC")
                    g_pdn[pb] = v_ev
                    p.wait("vector", "AC", v_ev)
                    p.wait("vector", SLOTS[b], v_x1r)
                    p.wait("vector", "ST", p.cnt["ST"] if g_ob[b] else 0)
                    v_o = p.run("vector", lambda e, b=b: e.tensor_tensor(
                        obuf[b][:], tdo[:], x1c[b][:], OP.add), inc="VE")
                    g_x1c[b] = v_o
                    g_tdo = v_o
                    p.wait("gpsimd", "VE", v_o)
                    v_st = p.run("gpsimd", lambda e, b=b, gt=gt, op4=op4: e.dma_start(
                        out=out_t[gt * 128:(gt + 1) * 128,
                                  op4 * 512:(op4 + 1) * 512], in_=obuf[b][:]),
                        inc="ST", inck=16)
                    g_ob[b] = v_st
                rdn_pe_hist[g] = p.cnt["PE"]
            v_dn_ve = p.cnt["VE"]
            v_dn_ac = p.cnt["AC"]
            v_dn_pe = p.cnt["PE"]
            st5c.close()
            return v_dn_pe, v_dn_ve, v_dn_ac

        for hf in range(2):
            v_p4_pe, v_p4_ve, v_p4_ac = ffn_half(
                hf, v_p4_pe, v_p4_ve, v_p4_ac, v_x1_gd, v_nmr2)
            v_x1_gd = 0

        p.wait("sync", "ST", p.cnt["ST"])

        for name in p.cnt:
            p.handles[name] = E(nc.semaphore(name))
        block = E(nc.Block())

        for ename, attr in (("sync", block.sync), ("scalar", block.scalar),
                            ("vector", block.vector), ("tensor", block.tensor),
                            ("gpsimd", block.gpsimd)):
            def section(eng, ename=ename):
                for f in p.ops[ename]:
                    f(eng)
            attr(section)

    return nc


def _host_constants():
    import jax
    import jax.numpy as jnp
    ROPE_BASE = 10000.0
    with jax.default_device(jax.local_devices(backend="cpu")[0]):
        inv_freq = 1.0 / ROPE_BASE ** (jnp.arange(0, RD, 2, dtype=jnp.float32) / RD)
        t = jnp.arange(S, dtype=jnp.float32)
        freqs = jnp.outer(t, inv_freq)
        emb = jnp.concatenate((freqs, freqs), axis=-1)
        cos = np.asarray(jnp.cos(emb).astype(jnp.float32))
        sin = np.asarray(jnp.sin(emb).astype(jnp.float32))
    cosd = np.zeros((128, S), np.float32)
    sind = np.zeros((128, S), np.float32)
    cosd[0:64] = cos.T
    cosd[64:128] = cos.T
    sind[0:64] = sin.T
    sind[64:128] = sin.T
    tm = np.triu(np.ones((128, 128), np.float32)).astype(ml_dtypes.bfloat16)
    return {
        "cosd": cosd,
        "sind": sind,
        "trimask": tm,
        "onescol_bf": np.ones((128, 1), ml_dtypes.bfloat16),
        "onesrow_f": np.ones((1, 128), np.float32),
        "onescol_f": np.ones((128, 1), np.float32),
        "invN_row": np.full((1, 128), 1.0 / (FI * D), np.float32),
        "halves": np.full((128, 512), 1.5, np.float32),
    }


_CACHE = {}


def kernel(L0_attn_x, L0_attn_W_kv_down, L0_attn_W_q_down, L0_attn_W_kc,
           L0_attn_W_v, L0_attn_W_qc, L0_attn_W_kr, L0_attn_W_qr, L0_attn_W_o,
           L0_attn_b_o, L0_ffn_w_up, L0_ffn_w_down):
    if "nc" not in _CACHE:
        _CACHE["nc"] = build_nc()
    nc = _CACHE["nc"]
    x = np.ascontiguousarray(np.asarray(L0_attn_x, np.float32))
    common = dict(
        wkvd=np.ascontiguousarray(np.asarray(L0_attn_W_kv_down, np.float32)),
        wqd=np.ascontiguousarray(np.asarray(L0_attn_W_q_down, np.float32)),
        wkc=np.ascontiguousarray(np.asarray(L0_attn_W_kc, np.float32)),
        wv=np.ascontiguousarray(np.asarray(L0_attn_W_v, np.float32)),
        wqc=np.ascontiguousarray(np.asarray(L0_attn_W_qc, np.float32)),
        wkr=np.ascontiguousarray(np.asarray(L0_attn_W_kr, np.float32)),
        wqr=np.ascontiguousarray(np.asarray(L0_attn_W_qr, np.float32)),
        wo=np.ascontiguousarray(np.asarray(L0_attn_W_o, np.float32)),
        bo=np.ascontiguousarray(np.asarray(L0_attn_b_o, np.float32).reshape(1, D)),
        wup=np.ascontiguousarray(np.asarray(L0_ffn_w_up, np.float32)),
        wdn=np.ascontiguousarray(np.asarray(L0_ffn_w_down, np.float32)),
        **_host_constants(),
    )
    in_maps = []
    for b in range(8):
        m = dict(common)
        m["x_in"] = np.ascontiguousarray(x[b])
        in_maps.append(m)
    res = run_bass_kernel_spmd(nc, in_maps, core_ids=list(range(8)))
    return np.stack([np.asarray(res.results[b]["out"]) for b in range(8)], axis=0)
